# version 10
# baseline (speedup 1.0000x reference)
import numpy as np
import ml_dtypes
import concourse.bass as bass
import concourse.mybir as mybir
from concourse.bass_utils import run_bass_kernel_spmd
from contextlib import ExitStack

F32 = mybir.dt.float32
BF16 = mybir.dt.bfloat16
I8 = mybir.dt.int8
AF = mybir.ActivationFunctionType
ALU = mybir.AluOpType
NPBF = ml_dtypes.bfloat16

B = 2
L_TOT = 8192
D = 1024
NH, DK, DVH = 12, 64, 128
HPC = 3
SCALE = float(DK) ** -0.5
CS = 128
TPG = 512
CPG = TPG // CS
NQ = 4
QT = L_TOT // NQ
GPQ = QT // TPG
ST = QT // 4
NCOL = 1158
NEG = -1e30
LNS = float(np.log(SCALE))
POWJ = [2, 4, 8, 16, 32, 64]
DBG = False
STAGE = 99
HSTAGE = 99



def _bf(x):
    return np.ascontiguousarray(np.asarray(x).astype(NPBF))


def _host_consts():
    c = {}
    i = np.arange(CS)
    c["stepi"] = _bf((i[:, None] <= i[None, :]).astype(np.float32))
    c["steps"] = _bf((i[:, None] < i[None, :]).astype(np.float32))
    c["negi"] = _bf((np.eye(CS) * NEG).astype(np.float32))
    c["ident"] = _bf(np.eye(CS, dtype=np.float32))
    c["identf"] = np.eye(CS, dtype=np.float32)
    c["ones_1x128"] = np.ones((1, 128), np.float32)
    c["ones_1x128b"] = _bf(np.ones((1, 128), np.float32))
    c["ones_row3"] = np.ones((HPC, CS), np.float32)
    c["onesf"] = np.ones((128, TPG), np.float32)
    c["cbias"] = np.tile(np.array([[1e-5, 1e-6, 1.0]], np.float32), (128, 1))
    c["ones128_bf"] = _bf(np.ones((128, 1), np.float32))
    m = np.zeros((4, 128, 3), np.float32)
    m[0, 0:64, 0] = 1; m[0, 64:128, 1] = 1
    m[1, 0:64, 2] = 1
    m[2, 64:128, 0] = 1
    m[3, 0:64, 1] = 1; m[3, 64:128, 2] = 1
    for j in range(4):
        c[f"ssqind{j}"] = _bf(m[j])
    return c


def _prep_core_inputs(core, inp, consts):
    b = core // 4
    j = core % 4
    heads = [HPC * j + hh for hh in range(HPC)]
    x = np.asarray(inp["L0_attn_hidden_states"][b, :L_TOT], np.float32)
    x_fm = np.ascontiguousarray(x.T)
    xq = np.stack([x_fm[:, q * QT + j * ST: q * QT + (j + 1) * ST] for q in range(NQ)], axis=1)
    x_q = np.ascontiguousarray(xq)

    hk = np.concatenate([np.arange(h * DK, (h + 1) * DK) for h in heads])
    hv = np.concatenate([np.arange(h * DVH, (h + 1) * DVH) for h in heads])
    wproj = np.zeros((D, NCOL), np.float32)
    wproj[:, 0:192] = inp["L0_attn_Wq"][:, hk]
    wproj[:, 192:384] = inp["L0_attn_Wk"][:, hk]
    wproj[:, 384:768] = inp["L0_attn_Wv"][:, hv]
    wproj[:, 768:1152] = inp["L0_attn_Wg"][:, hv]
    wproj[:, 1152:1155] = inp["L0_attn_Wgk"][:, heads]
    wproj[:, 1155:1158] = inp["L0_attn_Wb"][:, heads]

    conv_all = np.concatenate([
        np.asarray(inp["L0_attn_conv_q"], np.float32)[hk],
        np.asarray(inp["L0_attn_conv_k"], np.float32)[hk],
        np.asarray(inp["L0_attn_conv_v"], np.float32)[hv]], axis=0)

    gvec = np.zeros((HPC, 4), np.float32)
    gvec[:, 0] = inp["L0_attn_A_log"][heads]
    gvec[:, 1] = inp["L0_attn_dt_bias"][heads]
    gvec[:, 2] = inp["L0_attn_b_b"][heads]

    wo_mine = np.ascontiguousarray(np.asarray(inp["L0_attn_Wo"], np.float32)[hv])

    qw = np.asarray(inp["L0_ffn_qweights"], np.int32)
    q8 = np.ascontiguousarray(
        np.ascontiguousarray(qw.transpose(0, 2, 1)).view(np.int8)[..., ::4])
    scales = np.asarray(inp["L0_ffn_scales"], np.float32)
    srep = np.ascontiguousarray(np.repeat(scales.transpose(0, 2, 1), 16, axis=1))
    lbT = np.ascontiguousarray(np.asarray(inp["L0_ffn_lora_b"], np.float32).transpose(0, 2, 1))

    d = dict(
        x_fm=x_fm, x_q=x_q, wproj=wproj, conv_all=conv_all, gvec=gvec,
        wo=wo_mine,
        gnw=np.asarray(inp["L0_attn_g_norm_weight"], np.float32).reshape(128, 1),
        q8=q8, srep=srep,
        la=np.asarray(inp["L0_ffn_lora_a"], np.float32),
        lbT=lbT,
        biases=np.ascontiguousarray(np.asarray(inp["L0_ffn_biases"], np.float32).T),
        ln_g=np.ascontiguousarray(np.asarray(inp["L0_ffn_ln_gamma"], np.float32).T),
        ln_b=np.ascontiguousarray(np.asarray(inp["L0_ffn_ln_beta"], np.float32).T),
    )
    d.update(consts)
    return d



class RowView:

    def __init__(self, tile, base):
        self.tile = tile
        self.base = base

    def __getitem__(self, idx):
        if not isinstance(idx, tuple):
            idx = (idx,)
        p = idx[0]
        if isinstance(p, slice):
            st = 0 if p.start is None else p.start
            en = HPC if p.stop is None else p.stop
            np_ = slice(self.base + st, self.base + en)
        else:
            np_ = self.base + p
        return self.tile.__getitem__((np_,) + tuple(idx[1:]))


class Ctx:
    def __init__(self, nc, cur, sems, dr, sb, ps):
        self.nc = nc
        self.cur = cur
        self.eng = None
        self.sems = sems
        self.dr = dr
        self.sb = sb
        self.ps = ps
        self.cnt = {}
        self.waited = {}
        self.xstg_last = {0: 0, 1: 0}

    def inc(self, name, n=1):
        self.cnt[name] = self.cnt.get(name, 0) + n
        return self.cnt[name]

    def val(self, name):
        return self.cnt.get(name, 0)

    def wait(self, engname, name, v=None):
        if v is None:
            v = self.cnt.get(name, 0)
        if self.cur != engname or v <= 0:
            return
        if self.waited.get(name, -1) >= v:
            return
        self.eng.wait_ge(self.sems[name], v)
        self.waited[name] = v

    def tap(self, idx, tile_ap):
        if not DBG:
            return
        self.wait("gpsimd", "V")
        self.wait("gpsimd", "S")
        self.wait("gpsimd", "G")
        self.emit("gpsimd", lambda e: e.dma_start(
            out=self.dr["dbg"][idx], in_=tile_ap), "ST", 16)
        self.wait("sync", "ST")
        self.wait("tensor", "ST")
        self.wait("vector", "ST")
        self.wait("scalar", "ST")
        self.wait("gpsimd", "ST")

    def emit(self, engname, fn, sem=None, n=1):
        v = self.inc(sem, n) if sem is not None else None
        if self.cur == engname:
            ins = fn(self.eng)
            if sem is not None:
                ins.then_inc(self.sems[sem], n)
        return v


SEMNAMES = ["T", "V", "S", "G", "LX0", "LX1", "LW", "SPR", "OG", "CC", "LF",
            "LR0", "LR1", "LG", "ST", "DMV", "LQ", "PH"]
ENGS = ("sync", "tensor", "vector", "scalar", "gpsimd")


def build_nc():
    nc = bass.Bass("TRN2", target_bir_lowering=False, debug=False)
    dr = {}

    def din(name, shape, dt):
        dr[name] = nc.dram_tensor(name, list(shape), dt, kind="ExternalInput").ap()

    din("x_fm", (D, L_TOT), F32)
    din("x_q", (D, NQ, ST), F32)
    din("wproj", (D, NCOL), F32)
    din("conv_all", (768, 4), F32)
    din("gvec", (HPC, 4), F32)
    din("wo", (HPC * DVH, D), F32)
    din("gnw", (128, 1), F32)
    din("q8", (6, D, D), I8)
    din("srep", (6, D, D), F32)
    din("la", (6, 32, D), F32)
    din("lbT", (6, 32, D), F32)
    din("biases", (D, 6), F32)
    din("ln_g", (D, 3), F32)
    din("ln_b", (D, 3), F32)
    for k, shape, dt in [
        ("stepi", (CS, CS), BF16), ("steps", (CS, CS), BF16), ("negi", (CS, CS), BF16),
        ("ident", (CS, CS), BF16), ("identf", (CS, CS), F32),
        ("ones_1x128", (1, 128), F32), ("ones_1x128b", (1, 128), BF16),
        ("ones_row3", (HPC, CS), F32), ("ones128_bf", (128, 1), BF16),
        ("onesf", (128, TPG), F32), ("cbias", (128, 3), F32),
        ("ssqind0", (128, 3), BF16), ("ssqind1", (128, 3), BF16),
        ("ssqind2", (128, 3), BF16), ("ssqind3", (128, 3), BF16),
    ]:
        din(k, shape, dt)
    dr["out_fm"] = nc.dram_tensor("out_fm", [D, NQ, ST], F32, kind="ExternalOutput").ap()
    dr["sbounce"] = [nc.dram_tensor(f"sbn{q}", [4 * D, ST], BF16).ap() for q in range(NQ)]
    dr["sshard"] = [nc.dram_tensor(f"ssh{q}", [D, ST], BF16).ap() for q in range(NQ)]
    if DBG:
        dr["dbg"] = nc.dram_tensor("dbg", [6, 128, 8, ST], F32, kind="ExternalOutput").ap()

    es = ExitStack()
    with es:
        sb = {}

        def T(name, shape, dt, stack=None):
            sb[name] = (stack or es).enter_context(nc.sbuf_tensor(name, list(shape), dt))

        T("wproj_bf", (128, 8, NCOL), BF16)
        T("convw", (128, 6, 4), F32)
        T("gvec_s", (HPC, 4), F32)
        T("gknegA", (HPC, 1), F32)
        T("wo_bf", (128, HPC, D), BF16)
        T("gnw_s", (128, 1), F32)
        T("c_stepi", (CS, CS), BF16)
        T("c_steps", (CS, CS), BF16)
        T("c_negi", (CS, CS), BF16)
        T("c_ident", (CS, CS), BF16)
        T("c_identf", (CS, CS), F32)
        T("c_ones1x128", (1, 128), F32)
        T("c_ones1x128b", (1, 128), BF16)
        T("c_onesrow3", (HPC, CS), F32)
        T("c_ones128bf", (128, 1), BF16)
        T("c_onesf", (128, TPG), F32)
        T("c_cbias", (128, 3), F32)
        T("c_ssqind", (128, 4, 3), BF16)

        ps = [es.enter_context(nc.psum_tensor(f"pb{i}", [128, 512], F32)) for i in range(8)]
        sems = {n: es.enter_context(nc.semaphore(n)) for n in SEMNAMES}
        ctxs = {en: Ctx(nc, en, sems, dr, sb, ps) for en in ENGS}

        with ExitStack() as es1:
            for name, shape, dt in [
                ("stg", (128, 640), F32), ("xstg", (128, 2, TPG), F32),
                ("x_bf", (128, 8, TPG), BF16), ("xn", (128, 8, TPG), BF16),
                ("bcA_bf", (128, TPG), BF16), ("bcB_bf", (128, TPG), BF16),
                ("rstdq", (1, QT), BF16), ("nmurq", (1, QT), BF16),
                ("r1a", (1, TPG), F32), ("r1b", (1, TPG), F32),
                ("praw", (128, 6, TPG + 3), BF16), ("convt", (128, TPG), BF16),
                ("qkv", (128, 6, QT), BF16), ("gg", (128, HPC, QT), BF16),
                ("zgk", (HPC, QT), F32), ("zb", (HPC, QT), BF16),
                ("sqs", (128, 3, TPG), BF16),
                ("gcum", (HPC, TPG), F32), ("lnsq_q", (HPC, TPG), F32),
                ("lnsq_k", (HPC, TPG), F32), ("lnbeta", (HPC, TPG), F32),
                ("betar", (HPC, TPG), F32), ("r3a", (HPC, TPG), F32),
                ("r3b", (HPC, TPG), F32), ("r3c", (HPC, TPG), F32),
                ("colM", (HPC, TPG), F32), ("rowM", (HPC, TPG), F32),
                ("rowA", (HPC, TPG), F32), ("lqrow", (HPC, TPG), F32),
                ("kwrow", (HPC, TPG), F32), ("kprow", (HPC, TPG), F32),
                ("lcrow", (1, HPC * CPG), F32), ("lcbc", (64, HPC * CPG), F32),
                ("sp_colM", (128, TPG), F32), ("sp_rowM", (128, TPG), F32),
                ("sp_rowA", (128, TPG), F32), ("sp_lq", (128, TPG), F32),
                ("colsh", (128, CPG, 3, HPC), F32), ("rows_bf", (3, 3, TPG), BF16),
                ("eMT", (128, TPG), BF16), ("eM", (128, TPG), BF16), ("eA", (128, TPG), BF16),
                ("mT", (128, TPG), F32), ("mN", (128, TPG), F32),
                ("attT", (128, TPG), BF16),
                ("powT", (128, 6, TPG), F32), ("powN", (128, 5, TPG), F32),
                ("qst", (64, TPG), BF16),
                ("khh", (64, TPG), BF16), ("qhh", (64, TPG), BF16),
                ("vhh", (128, TPG), BF16),
                ("wcur", (128, CPG, 192), F32), ("w_bf", (128, CPG, 192), BF16),
                ("pt_bf", (64, CPG, 64), BF16), ("lcid", (64, CPG, 64), BF16),
                ("kptm", (128, CPG, DK), BF16), ("s_st", (64, HPC, DVH), BF16),
                ("o_bf", (128, TPG), BF16),
                ("ogh", (128, HPC, TPG), BF16),
            ]:
                T(name, shape, dt, es1)

            sb["sgrp"] = sb["xn"]
            sb["o2"] = sb["eA"]

            with nc.Block() as b1:
                for en in ENGS:
                    def body(eng, en=en):
                        ctx = ctxs[en]
                        ctx.eng = eng
                        emit_phase1(ctx)
                        ctx.emit(en, lambda e: e.nop(), "PH", 1)
                    getattr(b1, en)(body)

        with ExitStack() as es2:
            for name, shape, dt in [
                ("hbuf", (128, 8, QT), BF16), ("ybuf", (128, 8, ST), BF16),
                ("ygel", (128, 8, ST), BF16), ("f2s", (128, 8, ST), BF16),
                ("q8s", (128, 8, D), I8), ("sreps", (128, 2, D), F32),
                ("w1t", (128, 2, D), BF16), ("weff", (128, 2, 8, D), BF16),
                ("la_f", (32, D), F32), ("lbT_f", (32, D), F32),
                ("la_bf", (32, D), BF16), ("lbT_bf", (32, D), BF16),
                ("bias_f", (128, 8, 6), F32), ("lng_f", (128, 8, 3), F32),
                ("lnb_f", (128, 8, 3), F32),
                ("p2ra", (1, ST), F32), ("p2rb", (1, ST), F32),
                ("bcr_bf", (128, ST), BF16), ("bcm_bf", (128, ST), BF16),
                ("s_bf", (128, 8, ST), BF16), ("outt", (128, 8, ST), F32),
                ("xq_s", (128, 8, ST), F32),
            ]:
                T(name, shape, dt, es2)

            with nc.Block() as b2:
                for en in ENGS:
                    def body(eng, en=en):
                        ctx = ctxs[en]
                        ctx.eng = eng
                        ctx.wait(en, "PH", 5)
                        emit_phase2(ctx)
                    getattr(b2, en)(body)

    return nc



def gq_of(g):
    return slice(g * TPG, (g + 1) * TPG)


KQ_SLICE = {
    "q": [(0, 0), (0, 64), (1, 0)],
    "k": [(1, 64), (2, 0), (2, 64)],
}



def emit_phase1(c: Ctx):
    emit_phase0(c)
    for q in range(NQ):
        if STAGE < 1:
            break
        _t1_stats(c, q)
        if STAGE < 2:
            continue
        _t2_proj(c, q)
        if STAGE < 3:
            continue
        _t3_scan(c, q)
        if STAGE < 5:
            continue
        nog = c.val("OG")
        c.wait("gpsimd", "OG", nog)
        c.emit("gpsimd", lambda e, q=q: e.collective_compute(
            "ReduceScatter", ALU.add, replica_groups=[[0, 1, 2, 3], [4, 5, 6, 7]],
            ins=[c.dr["sbounce"][q].opt()], outs=[c.dr["sshard"][q].opt()]), "CC", 1)


def emit_phase0(c: Ctx):
    sb, dr, ps = c.sb, c.dr, c.ps

    def dma(dst, src):
        return c.emit("sync", lambda e: e.dma_start(out=dst, in_=src), "LW", 16)

    dma(sb["c_stepi"][:], dr["stepi"])
    dma(sb["c_steps"][:], dr["steps"])
    dma(sb["c_negi"][:], dr["negi"])
    dma(sb["c_ident"][:], dr["ident"])
    dma(sb["c_identf"][:], dr["identf"])
    dma(sb["c_ones1x128"][:], dr["ones_1x128"])
    dma(sb["c_ones1x128b"][:], dr["ones_1x128b"])
    dma(sb["c_onesrow3"][:], dr["ones_row3"])
    dma(sb["c_ones128bf"][:], dr["ones128_bf"])
    dma(sb["c_onesf"][:], dr["onesf"])
    dma(sb["c_cbias"][:], dr["cbias"])
    for j in range(4):
        dma(sb["c_ssqind"][:, j], dr[f"ssqind{j}"])
    dma(sb["gvec_s"][:], dr["gvec"])
    dma(sb["gnw_s"][:], dr["gnw"])
    dma(sb["convw"][:, 0:6], dr["conv_all"].rearrange("(s p) k -> p s k", p=128))

    for kc in range(8):
        for half in range(2):
            o0 = half * 640
            w = min(640, NCOL - o0)
            c.wait("sync", "V")
            lw = dma(sb["stg"][:, 0:w], dr["wproj"][128 * kc:128 * (kc + 1), o0:o0 + w])
            c.wait("vector", "LW", lw)
            c.emit("vector", lambda e, kc=kc, o0=o0, w=w: e.tensor_copy(
                sb["wproj_bf"][:, kc, o0:o0 + w], sb["stg"][:, 0:w]), "V")
    for h in range(HPC):
        for half in range(2):
            o0 = half * 512
            c.wait("sync", "V")
            lw = dma(sb["stg"][:, 0:512], dr["wo"][128 * h:128 * (h + 1), o0:o0 + 512])
            c.wait("vector", "LW", lw)
            c.emit("vector", lambda e, h=h, o0=o0: e.tensor_scalar(
                sb["wo_bf"][:, h, o0:o0 + 512], sb["stg"][:, 0:512], sb["gnw_s"][:], None, ALU.mult), "V")
    c.wait("scalar", "LW")
    t1 = c.emit("scalar", lambda e: e.activation(sb["gknegA"][:], sb["gvec_s"][:, 0:1], AF.Exp), "S")
    c.wait("vector", "S", t1)
    c.emit("vector", lambda e: e.tensor_scalar(sb["gknegA"][:], sb["gknegA"][:], -1.0, None, ALU.mult), "V")
    c.emit("vector", lambda e: e.memset(sb["praw"][:, :, 0:3], 0.0), "V")
    c.emit("vector", lambda e: e.memset(sb["s_st"][:], 0.0), "V")


def _load_x_group(c: Ctx, t0):
    sb, dr = c.sb, c.dr
    vtoks = []
    for kc in range(8):
        pp = kc % 2
        c.wait("sync", "V", vtoks[kc - 2] if kc >= 2 else c.xstg_last[pp])
        lx = c.emit("sync", lambda e, kc=kc, t0=t0, pp=pp: e.dma_start(
            out=sb["xstg"][:, pp, :],
            in_=dr["x_fm"][128 * kc:128 * (kc + 1), t0:t0 + TPG]), f"LX{pp}", 16)
        c.wait("vector", f"LX{pp}", lx)
        vtoks.append(c.emit("vector", lambda e, kc=kc, pp=pp: e.tensor_copy(
            sb["x_bf"][:, kc, :], sb["xstg"][:, pp, :]), "V"))
    c.xstg_last[0] = vtoks[6]
    c.xstg_last[1] = vtoks[7]
    return vtoks[-1]


def _t1_stats(c: Ctx, q):
    sb, dr, ps = c.sb, c.dr, c.ps
    for g in range(GPQ):
        t0 = q * QT + g * TPG
        gq = gq_of(g)
        c.wait("vector", "V")
        c.wait("vector", "T")
        c.wait("vector", "S")
        vx = _load_x_group(c, t0)
        c.wait("scalar", "V", vx)
        c.wait("scalar", "T")
        c.wait("scalar", "OG")
        sx2 = c.emit("scalar", lambda e: e.activation(
            sb["xn"].rearrange("p k t -> p (k t)"),
            sb["x_bf"].rearrange("p k t -> p (k t)"), AF.Square), "S")
        c.wait("tensor", "V", vx)
        c.wait("tensor", "S", sx2)
        for kc in range(8):
            c.emit("tensor", lambda e, kc=kc: e.matmul(
                ps[7][0:1, 0:TPG], sb["c_ones128bf"][:], sb["x_bf"][:, kc, :],
                start=(kc == 0), stop=(kc == 7)), None)
        tsx = None
        for kc in range(8):
            tsx = c.emit("tensor", lambda e, kc=kc: e.matmul(
                ps[6][0:1, 0:TPG], sb["c_ones128bf"][:], sb["xn"][:, kc, :],
                start=(kc == 0), stop=(kc == 7)), "T" if kc == 7 else None)
        c.wait("vector", "T", tsx)
        v0 = c.emit("vector", lambda e: e.tensor_scalar(
            sb["r1a"][:], ps[7][0:1, 0:TPG], 1.0 / D, None, ALU.mult), "V")
        v1 = c.emit("vector", lambda e: e.tensor_scalar(
            sb["r1b"][:], ps[6][0:1, 0:TPG], 1.0 / D, None, ALU.mult), "V")
        c.wait("vector", "V", v1)
        v2 = c.emit("vector", lambda e: e.tensor_tensor(sb["r1a"][:], sb["r1a"][:], sb["r1a"][:], ALU.mult), "V")
        c.wait("vector", "V", v2)
        v3 = c.emit("vector", lambda e: e.tensor_tensor(sb["r1b"][:], sb["r1b"][:], sb["r1a"][:], ALU.subtract), "V")
        c.wait("scalar", "V", v3)
        s1 = c.emit("scalar", lambda e: e.activation(sb["r1a"][:], sb["r1b"][:], AF.Ln, bias=sb["c_cbias"][0:1, 0:1]), "S")
        c.wait("scalar", "S", s1)
        s2 = c.emit("scalar", lambda e, gq=gq: e.activation(
            sb["rstdq"][:, gq], sb["r1a"][:], AF.Exp, scale=-0.5), "S")
        c.wait("vector", "S", s2)
        v4 = c.emit("vector", lambda e: e.tensor_scalar(
            sb["r1a"][:], ps[7][0:1, 0:TPG], -1.0 / D, None, ALU.mult), "V")
        c.wait("vector", "V", v4)
        c.emit("vector", lambda e, gq=gq: e.tensor_tensor(
            sb["nmurq"][:, gq], sb["r1a"][:], sb["rstdq"][:, gq], ALU.mult), "V")


def _t2_proj(c: Ctx, q):
    sb, dr, ps = c.sb, c.dr, c.ps
    NSLICE = [(s * 128, 128) for s in range(9)] + [(1152, 3), (1155, 3)]
    prev_ssl = 0
    bank_free = {0: None, 1: None, 2: None}
    for g in range(GPQ):
        t0 = q * QT + g * TPG
        gq = gq_of(g)
        c.wait("vector", "V")
        c.wait("vector", "S")
        c.wait("vector", "T")
        c.wait("scalar", "T")
        c.wait("scalar", "G")
        vx = _load_x_group(c, t0)
        vrows = c.val("V")
        c.wait("tensor", "V", vrows)
        c.emit("tensor", lambda e, gq=gq: e.matmul(
            ps[3][:, 0:TPG], sb["c_ones1x128b"][:], sb["rstdq"][:, gq], start=True, stop=True), None)
        tbc = c.emit("tensor", lambda e, gq=gq: e.matmul(
            ps[4][:, 0:TPG], sb["c_ones1x128b"][:], sb["nmurq"][:, gq], start=True, stop=True), "T")
        c.wait("vector", "T", tbc)
        va = c.emit("vector", lambda e: e.tensor_copy(sb["bcA_bf"][:], ps[3][:, 0:TPG]), "V")
        vb = c.emit("vector", lambda e: e.tensor_copy(sb["bcB_bf"][:], ps[4][:, 0:TPG]), "V")
        c.wait("vector", "V", vb)
        c.wait("gpsimd", "T")
        vxn1 = None
        for kc in range(8):
            vxn1 = c.emit("vector", lambda e, kc=kc: e.tensor_tensor(
                sb["xn"][:, kc, :], sb["x_bf"][:, kc, :], sb["bcA_bf"][:], ALU.mult), "V")
        c.wait("gpsimd", "V", vxn1)
        gxn = None
        for kc in range(8):
            gxn = c.emit("gpsimd", lambda e, kc=kc: e.tensor_tensor(
                sb["xn"][:, kc, :], sb["xn"][:, kc, :], sb["bcB_bf"][:], ALU.add), "G")
        c.wait("tensor", "G", gxn)
        for s, (off, w) in enumerate(NSLICE):
            bank = ps[s % 3]
            bf = bank_free[s % 3]
            if bf is not None:
                c.wait("tensor", bf[0], bf[1])
            tsl = None
            for kc in range(8):
                tsl = c.emit("tensor", lambda e, kc=kc, off=off, w=w, bank=bank: e.matmul(
                    bank[0:w, 0:TPG], sb["wproj_bf"][:, kc, off:off + w], sb["xn"][:, kc, :],
                    start=(kc == 0), stop=(kc == 7)), "T" if kc == 7 else None)
            if s <= 5:
                c.wait("vector", "T", tsl)
                vev = c.emit("vector", lambda e, s=s, bank=bank: e.tensor_copy(
                    sb["praw"][:, s, 3:3 + TPG], bank[0:128, 0:TPG]), "V")
                bank_free[s % 3] = ("V", vev)
                c.wait("vector", "V", vev)
                c.wait("vector", "S", prev_ssl)
                vcv = c.emit("vector", lambda e, s=s: e.tensor_scalar(
                    sb["convt"][:], sb["praw"][:, s, 0:TPG], sb["convw"][:, s, 0:1], None, ALU.mult), "V")
                for jj in (1, 2, 3):
                    c.wait("vector", "V", vcv)
                    vcv = c.emit("vector", lambda e, s=s, jj=jj: e.scalar_tensor_tensor(
                        sb["convt"][:], sb["praw"][:, s, jj:jj + TPG], sb["convw"][:, s, jj:jj + 1],
                        sb["convt"][:], ALU.mult, ALU.add), "V")
                c.wait("scalar", "V", vcv)
                prev_ssl = c.emit("scalar", lambda e, s=s, gq=gq: e.activation(
                    sb["qkv"][:, s, gq], sb["convt"][:], AF.Silu), "S")
                c.wait("vector", "V", vcv)
                c.emit("vector", lambda e, s=s: e.tensor_copy(
                    sb["praw"][:, s, 0:3], sb["praw"][:, s, TPG:TPG + 3]), "V")
            elif s <= 8:
                h = s - 6
                c.wait("scalar", "T", tsl)
                ssil = c.emit("scalar", lambda e, h=h, gq=gq, bank=bank: e.activation(
                    sb["gg"][:, h, gq], bank[0:128, 0:TPG], AF.Silu), "S")
                bank_free[s % 3] = ("S", ssil)
            elif s == 9:
                c.wait("vector", "T", tsl)
                vz = c.emit("vector", lambda e, gq=gq, bank=bank: e.tensor_scalar(
                    sb["zgk"][:, gq], bank[0:HPC, 0:TPG], sb["gvec_s"][:, 1:2], None, ALU.add), "V")
                bank_free[s % 3] = ("V", vz)
            else:
                c.wait("vector", "T", tsl)
                vz = c.emit("vector", lambda e, gq=gq, bank=bank: e.tensor_scalar(
                    sb["zb"][:, gq], bank[0:HPC, 0:TPG], sb["gvec_s"][:, 2:3], None, ALU.add), "V")
                bank_free[s % 3] = ("V", vz)


def _t3_scan(c: Ctx, q):
    _gate_prep(c, q, 0)
    for g in range(GPQ):
        if STAGE >= 4:
            pend = None
            prev_ttr = None
            for h in range(HPC):
                t_a = _hg_head(c, q, g, h, prev_ttr)
                if pend is not None:
                    _hg_opost(c, q, g, h - 1, pend)
                r = _hg_body(c, q, g, h, t_a)
                pend, prev_ttr = (r if r is not None else (None, None))
            if pend is not None:
                _hg_opost(c, q, g, HPC - 1, pend)
        if g + 1 < GPQ:
            _gate_prep(c, q, g + 1)
        if STAGE < 4 or STAGE < 5 or HSTAGE < 9:
            continue
        _wo_partial(c, q, g)


def _gate_prep(c: Ctx, q, g):
    sb, dr, ps = c.sb, c.dr, c.ps
    gq = gq_of(g)

    c.wait("vector", "V")
    c.wait("scalar", "V")
    c.wait("scalar", "T")
    sp = None
    for j in range(3):
        sp = c.emit("scalar", lambda e, j=j, gq=gq: e.activation(
            sb["sqs"][:, j, :], sb["qkv"][:, j, gq], AF.Square), "S")
    c.wait("tensor", "S", sp)
    c.wait("tensor", "V")
    c.emit("tensor", lambda e: e.matmul(ps[7][0:3, 0:TPG], sb["c_ssqind"][:, 0], sb["sqs"][:, 0, :],
                                        start=True, stop=False), None)
    t_q = c.emit("tensor", lambda e: e.matmul(ps[7][0:3, 0:TPG], sb["c_ssqind"][:, 1], sb["sqs"][:, 1, :],
                                              start=False, stop=True), "T")
    c.emit("tensor", lambda e: e.matmul(ps[6][0:3, 0:TPG], sb["c_ssqind"][:, 2], sb["sqs"][:, 1, :],
                                        start=True, stop=False), None)
    t_k = c.emit("tensor", lambda e: e.matmul(ps[6][0:3, 0:TPG], sb["c_ssqind"][:, 3], sb["sqs"][:, 2, :],
                                              start=False, stop=True), "T")
    c.wait("scalar", "T", t_k)
    s_q = c.emit("scalar", lambda e: e.activation(sb["lnsq_q"][:], ps[7][0:3, 0:TPG], AF.Ln, bias=sb["c_cbias"][0:HPC, 1:2]), "S")
    s_k = c.emit("scalar", lambda e: e.activation(sb["lnsq_k"][:], ps[6][0:3, 0:TPG], AF.Ln, bias=sb["c_cbias"][0:HPC, 1:2]), "S")

    sa = c.emit("scalar", lambda e, gq=gq: e.activation(sb["r3a"][:], sb["zgk"][:, gq], AF.Abs), "S")
    c.wait("scalar", "S", sa)
    sb1 = c.emit("scalar", lambda e: e.activation(sb["r3b"][:], sb["r3a"][:], AF.Exp, scale=-1.0), "S")
    c.wait("scalar", "S", sb1)
    c.emit("scalar", lambda e: e.activation(sb["r3b"][:], sb["r3b"][:], AF.Ln, bias=sb["c_cbias"][0:HPC, 2:3]), "S")
    sc1 = c.emit("scalar", lambda e, gq=gq: e.activation(sb["r3c"][:], sb["zgk"][:, gq], AF.Relu), "S")
    c.wait("vector", "S", sc1)
    v1 = c.emit("vector", lambda e: e.tensor_tensor(sb["r3b"][:], sb["r3b"][:], sb["r3c"][:], ALU.add), "V")
    c.wait("vector", "V", v1)
    v2 = c.emit("vector", lambda e: e.tensor_scalar(sb["r3a"][:], sb["r3b"][:], sb["gknegA"][:], None, ALU.mult), "V")
    c.wait("vector", "V", v2)
    vgc = None
    for cc in range(CPG):
        vgc = c.emit("vector", lambda e, cc=cc: e.tensor_tensor_scan(
            sb["gcum"][:, cc * CS:(cc + 1) * CS], sb["c_onesrow3"][:],
            sb["r3a"][:, cc * CS:(cc + 1) * CS], 0.0, ALU.mult, ALU.add), "V")

    c.wait("scalar", "V", v2)
    sd = c.emit("scalar", lambda e, gq=gq: e.activation(sb["r3b"][:], sb["zb"][:, gq], AF.Abs), "S")
    c.wait("scalar", "S", sd)
    se = c.emit("scalar", lambda e: e.activation(sb["r3b"][:], sb["r3b"][:], AF.Exp, scale=-1.0), "S")
    c.wait("scalar", "S", se)
    c.emit("scalar", lambda e: e.activation(sb["r3b"][:], sb["r3b"][:], AF.Ln, bias=sb["c_cbias"][0:HPC, 2:3]), "S")
    sg = c.emit("scalar", lambda e, gq=gq: e.activation(sb["r3c"][:], sb["zb"][:, gq], AF.Relu, scale=-1.0), "S")
    c.wait("vector", "S", sg)
    vlb = c.emit("vector", lambda e: e.scalar_tensor_tensor(
        sb["lnbeta"][:], sb["r3b"][:], -1.0, sb["r3c"][:], ALU.mult, ALU.subtract), "V")
    c.wait("scalar", "V", vlb)
    sbeta = c.emit("scalar", lambda e: e.activation(sb["betar"][:], sb["lnbeta"][:], AF.Exp), "S")

    c.wait("vector", "S", s_k)
    c.wait("vector", "V", max(vgc, vlb))
    vc1 = c.emit("vector", lambda e: e.scalar_tensor_tensor(
        sb["colM"][:], sb["lnsq_k"][:], -0.5, sb["gcum"][:], ALU.mult, ALU.subtract), "V")
    vc2 = c.emit("vector", lambda e: e.scalar_tensor_tensor(
        sb["r3b"][:], sb["lnsq_k"][:], -0.5, sb["gcum"][:], ALU.mult, ALU.add), "V")
    c.wait("vector", "V", max(vc2, vlb))
    vrm = c.emit("vector", lambda e: e.tensor_tensor(sb["rowM"][:], sb["r3b"][:], sb["lnbeta"][:], ALU.add), "V")
    c.wait("vector", "S", s_q)
    vc3 = c.emit("vector", lambda e: e.scalar_tensor_tensor(
        sb["r3c"][:], sb["lnsq_q"][:], -0.5, sb["gcum"][:], ALU.mult, ALU.add), "V")
    c.wait("vector", "V", vc3)
    vra = c.emit("vector", lambda e: e.tensor_scalar(sb["rowA"][:], sb["r3c"][:], LNS, None, ALU.add), "V")
    c.wait("vector", "V", vc1)
    vkpr = None
    for cc in range(CPG):
        vkpr = c.emit("vector", lambda e, cc=cc: e.tensor_scalar(
            sb["kprow"][:, cc * CS:(cc + 1) * CS], sb["colM"][:, cc * CS:(cc + 1) * CS],
            sb["gcum"][:, cc * CS + CS - 1:cc * CS + CS], None, ALU.add), "V")
    c.wait("scalar", "V", max(vra, vkpr, vrm))
    slq = c.emit("scalar", lambda e: e.activation(sb["lqrow"][:], sb["rowA"][:], AF.Exp), "S")
    skw = c.emit("scalar", lambda e: e.activation(sb["kwrow"][:], sb["rowM"][:], AF.Exp), "S")
    skp = c.emit("scalar", lambda e: e.activation(sb["kprow"][:], sb["kprow"][:], AF.Exp), "S")
    vkw = c.val("V")

    c.wait("sync", "T")
    c.wait("sync", "V", max(vc1, vrm, vra))
    c.wait("sync", "S", slq)
    for a, b in (("colM", "sp_colM"), ("rowM", "sp_rowM"), ("rowA", "sp_rowA"), ("lqrow", "sp_lq")):
        c.emit("sync", lambda e, a=a, b=b: e.dma_start(
            out=c.sb[b][0:HPC * 32:32, :], in_=c.sb[a][:]), "SPR", 16)

    c.wait("sync", "V", vgc)
    dlc = None
    for hh in range(HPC):
        dlc = c.emit("sync", lambda e, hh=hh: e.dma_start(
            out=sb["lcrow"][0:1, CPG * hh:CPG * (hh + 1)],
            in_=sb["gcum"][hh:hh + 1, CS - 1::CS],
            allow_slow_non_contiguous=True), "DMV", 16)
    c.wait("scalar", "DMV", dlc)
    slc = c.emit("scalar", lambda e: e.activation(sb["lcrow"][:], sb["lcrow"][:], AF.Exp), "S")
    c.wait("tensor", "S", slc)
    c.wait("tensor", "V")
    tlc = c.emit("tensor", lambda e: e.matmul(
        ps[7][0:64, 0:HPC * CPG], sb["c_ones1x128"][:, 0:64], sb["lcrow"][:],
        start=True, stop=True), "T")
    c.wait("vector", "T", tlc)
    c.emit("vector", lambda e: e.tensor_copy(sb["lcbc"][:], ps[7][0:64, 0:HPC * CPG]), "V")

    c.wait("vector", "S", max(skp, sbeta))
    c.wait("vector", "V", vkw)
    vrb = None
    for si, src in enumerate(("betar", "kwrow", "kprow")):
        vrb = c.emit("vector", lambda e, si=si, src=src: e.tensor_copy(
            sb["rows_bf"][:, si, :], sb[src][:]), "V")
    c.wait("tensor", "V", vrb)
    tcs = None
    for cc in range(CPG):
        for si in range(3):
            tcs = c.emit("tensor", lambda e, cc=cc, si=si: e.transpose(
                ps[6][:, 2 * (3 * cc + si):2 * (3 * cc + si) + 2].bitcast(BF16)[:, 0:HPC],
                sb["rows_bf"][:, si, cc * CS:(cc + 1) * CS], sb["c_ident"][0:HPC, 0:HPC]),
                "T" if (cc == CPG - 1 and si == 2) else None)
    c.wait("vector", "T", tcs)
    c.emit("vector", lambda e: e.tensor_copy(
        sb["colsh"].rearrange("p cc s hh -> p (cc s) hh"),
        ps[6][:, 0:2 * 3 * CPG].bitcast(BF16).rearrange(
            "p (g four) -> p g four", four=4)[:, :, 0:HPC]), "V")


def _hg_head(c: Ctx, q, g, h, prev_ttr):
    sb, ps = c.sb, c.ps
    gq = gq_of(g)
    sq, bq = KQ_SLICE["q"][h]
    sk, bk = KQ_SLICE["k"][h]

    def kfm(cc):
        return sb["khh"][:, cc * CS:(cc + 1) * CS]

    def qfm(cc):
        return sb["qhh"][:, cc * CS:(cc + 1) * CS]

    sgate = c.val("S")

    c.wait("vector", "S", sgate)
    if prev_ttr is None:
        c.wait("vector", "T")
    else:
        c.wait("vector", "T", prev_ttr)
    c.emit("vector", lambda e: e.tensor_copy(
        sb["khh"][:], sb["qkv"][bk:bk + 64, sk, gq]), "V")
    c.emit("vector", lambda e: e.tensor_copy(
        sb["qhh"][:], sb["qkv"][bq:bq + 64, sq, gq]), "V")
    vcp = c.emit("vector", lambda e: e.tensor_copy(
        sb["vhh"][:], sb["qkv"][:, 3 + h, gq]), "V")
    vgate = c.val("V")

    c.wait("tensor", "V", vgate)
    c.wait("tensor", "S", sgate)
    t_a = None
    for cc in range(CPG):
        c.emit("tensor", lambda e, cc=cc: e.matmul(
            ps[0][:, cc * CS:(cc + 1) * CS], kfm(cc), kfm(cc), start=True, stop=True), None)
        t_a = c.emit("tensor", lambda e, cc=cc: e.matmul(
            ps[1][:, cc * CS:(cc + 1) * CS], kfm(cc), qfm(cc), start=True, stop=True),
            "T" if cc == CPG - 1 else None)
    return t_a


def _hg_body(c: Ctx, q, g, h, t_a):
    sb, dr, ps = c.sb, c.dr, c.ps
    gq = gq_of(g)

    def kfm(cc):
        return sb["khh"][:, cc * CS:(cc + 1) * CS]

    def qfm(cc):
        return sb["qhh"][:, cc * CS:(cc + 1) * CS]

    def vfm(cc):
        return sb["vhh"][:, cc * CS:(cc + 1) * CS]

    spr = c.val("SPR")
    c.wait("tensor", "SPR", spr)

    def arg_form(colsp, rowsp, mask_lhs, mask_rhs, expname, bank):
        for cc in range(CPG):
            c.emit("tensor", lambda e, cc=cc, colsp=colsp, bank=bank: e.matmul(
                bank[:, cc * CS:(cc + 1) * CS],
                sb[colsp][32 * h:32 * h + 1, cc * CS:(cc + 1) * CS],
                sb["c_onesf"][32 * h:32 * h + 1, 0:128], start=(cc == 0), stop=False), None)
        c.emit("tensor", lambda e, rowsp=rowsp, bank=bank: e.matmul(
            bank[:, 0:TPG], sb["c_onesf"][32 * h:32 * h + 1, 0:128],
            sb[rowsp][32 * h:32 * h + 1, :], start=False, stop=False), None)
        tlast = None
        for cc in range(CPG):
            tlast = c.emit("tensor", lambda e, cc=cc, mask_lhs=mask_lhs, mask_rhs=mask_rhs, bank=bank: e.matmul(
                bank[:, cc * CS:(cc + 1) * CS], sb[mask_lhs][:], sb[mask_rhs][:],
                start=False, stop=(cc == CPG - 1)), "T" if cc == CPG - 1 else None)
        c.wait("scalar", "T", tlast)
        return c.emit("scalar", lambda e, expname=expname, bank=bank: e.activation(
            sb[expname][:], bank[:, 0:TPG], AF.Exp), "S")

    if HSTAGE < 1:
        return
    s1 = arg_form("sp_colM", "sp_rowM", "c_stepi", "c_negi", "eMT", ps[2])

    ttr = None
    for cc in range(CPG):
        c.emit("tensor", lambda e, cc=cc: e.transpose(
            ps[5][:, cc * 64:(cc + 1) * 64].bitcast(BF16), vfm(cc), sb["c_ident"][:]), None)
        ttr = c.emit("tensor", lambda e, cc=cc: e.transpose(
            ps[5][:, 256 + cc * 32:256 + (cc + 1) * 32].bitcast(BF16), kfm(cc),
            sb["c_ident"][0:64, 0:64]), "T" if cc == CPG - 1 else None)

    s2 = arg_form("sp_rowM", "sp_colM", "c_negi", "c_stepi", "eM", ps[3])

    tlq = c.emit("tensor", lambda e: e.matmul(
        ps[4][0:64, 0:TPG], sb["c_onesf"][32 * h:32 * h + 1, 0:64],
        sb["sp_lq"][32 * h:32 * h + 1, :], start=True, stop=True), "T")

    c.wait("tensor", "S", s1)
    s3 = arg_form("sp_colM", "sp_rowA", "c_steps", "c_negi", "eA", ps[2])

    c.wait("vector", "S", s1)
    c.wait("vector", "T", t_a)
    v_mT = c.emit("vector", lambda e: e.tensor_tensor(sb["mT"][:], sb["eMT"][:], ps[0][:, 0:TPG], ALU.mult), "V")
    c.wait("vector", "S", s2)
    v_mN = c.emit("vector", lambda e: e.tensor_tensor(sb["mN"][:], sb["eM"][:], ps[0][:, 0:TPG], ALU.mult), "V")
    c.wait("vector", "S", s3)
    v_att = c.emit("vector", lambda e: e.tensor_tensor(sb["attT"][:], sb["eA"][:], ps[1][:, 0:TPG], ALU.mult), "V")

    if HSTAGE < 2:
        return
    c.wait("vector", "T", ttr)
    vkp = None
    for cc in range(CPG):
        vkp = c.emit("vector", lambda e, cc=cc: e.tensor_scalar(
            sb["kptm"][:, cc, :], ps[5][:, 256 + cc * 32:256 + (cc + 1) * 32].bitcast(BF16),
            sb["colsh"][:, cc, 2, h:h + 1], None, ALU.mult), "V")
    c.wait("vector", "T", tlq)
    vqs = None
    for cc in range(CPG):
        vqs = c.emit("vector", lambda e, cc=cc: e.tensor_tensor(
            sb["qst"][:, cc * CS:(cc + 1) * CS], qfm(cc), ps[4][0:64, cc * CS:(cc + 1) * CS],
            ALU.mult), "V")

    if HSTAGE < 4:
        return
    vrhs = None
    for cc in range(CPG):
        c.emit("vector", lambda e, cc=cc: e.tensor_scalar(
            sb["wcur"][:, cc, 0:128], ps[5][:, cc * 64:(cc + 1) * 64].bitcast(BF16),
            sb["colsh"][:, cc, 0, h:h + 1], None, ALU.mult), "V")
        vrhs = c.emit("vector", lambda e, cc=cc: e.tensor_scalar(
            sb["wcur"][:, cc, 128:192], ps[5][:, 256 + cc * 32:256 + (cc + 1) * 32].bitcast(BF16),
            sb["colsh"][:, cc, 1, h:h + 1], None, ALU.mult), "V")

    def srcT(ji):
        return sb["mT"][:] if ji < 0 else sb["powT"][:, ji, :]

    def srcN(ji):
        return sb["mN"][:] if ji < 0 else sb["powN"][:, ji, :]

    prevNtok, prevTtok = v_mN, v_mT
    prevNeng = prevTeng = "V"
    powT_tok = {}
    last_pow_s = None
    vw = vrhs
    for si, j in enumerate([1] + POWJ):
        c.wait("tensor", "V", vw)
        if si > 0:
            c.wait("tensor", "S", powT_tok[si - 1])
        tj = None
        for cc in range(CPG):
            lhs = (sb["mT"][:, cc * CS:(cc + 1) * CS] if j == 1
                   else sb["powT"][:, si - 1, cc * CS:(cc + 1) * CS])
            bank = ps[6] if cc < 2 else ps[2]
            tj = c.emit("tensor", lambda e, lhs=lhs, cc=cc, bank=bank: e.matmul(
                bank[:, (cc % 2) * 192:(cc % 2) * 192 + 192], lhs,
                sb["wcur"][:, cc, :], start=True, stop=True),
                "T" if cc == CPG - 1 else None)
        tn = tt = None
        if si < 6:
            j2 = POWJ[si]
            if si == 0:
                c.wait("tensor", "V", max(prevNtok, prevTtok, vqs))
            else:
                c.wait("tensor", "S", max(prevNtok, prevTtok))
            if j2 != 64:
                for cc in range(CPG):
                    tn = c.emit("tensor", lambda e, si=si, cc=cc: e.matmul(
                        ps[3][:, cc * CS:(cc + 1) * CS],
                        srcT(si - 1)[:, cc * CS:(cc + 1) * CS],
                        srcN(si - 1)[:, cc * CS:(cc + 1) * CS],
                        start=True, stop=True), "T" if cc == CPG - 1 else None)
            for cc in range(CPG):
                tt = c.emit("tensor", lambda e, si=si, cc=cc: e.matmul(
                    ps[4][:, cc * CS:(cc + 1) * CS],
                    srcN(si - 1)[:, cc * CS:(cc + 1) * CS],
                    srcT(si - 1)[:, cc * CS:(cc + 1) * CS],
                    start=True, stop=True), "T" if cc == CPG - 1 else None)
        c.wait("vector", "T", tj)
        op = ALU.subtract if j == 1 else ALU.add
        c.emit("vector", lambda e, op=op: e.tensor_tensor(
            sb["wcur"].rearrange("p cc w -> p (cc w)")[:, 0:384],
            sb["wcur"].rearrange("p cc w -> p (cc w)")[:, 0:384],
            ps[6][:, 0:384], op), "V")
        vw = c.emit("vector", lambda e, op=op: e.tensor_tensor(
            sb["wcur"].rearrange("p cc w -> p (cc w)")[:, 384:768],
            sb["wcur"].rearrange("p cc w -> p (cc w)")[:, 384:768],
            ps[2][:, 0:384], op), "V")
        if si < 6:
            if POWJ[si] != 64:
                c.wait("scalar", "T", tn)
                newN = c.emit("scalar", lambda e, si=si: e.activation(
                    sb["powN"][:, si, :], ps[3][:, 0:TPG], AF.Copy), "S")
            else:
                newN = prevNtok
            c.wait("scalar", "T", tt)
            newT = c.emit("scalar", lambda e, si=si: e.activation(
                sb["powT"][:, si, :], ps[4][:, 0:TPG], AF.Copy), "S")
            prevNtok, prevTtok = newN, newT
            powT_tok[si] = newT
            last_pow_s = newT
    c.wait("vector", "V", vw)
    vwb = c.emit("vector", lambda e: e.tensor_copy(
        sb["w_bf"].rearrange("p cc w -> p (cc w)"),
        sb["wcur"].rearrange("p cc w -> p (cc w)")), "V")

    c.wait("tensor", "V", vwb)
    takw = None
    for cc in range(CPG):
        takw = c.emit("tensor", lambda e, cc=cc: e.matmul(
            ps[2][0:64, cc * CS:(cc + 1) * CS], sb["w_bf"][:, cc, 128:192],
            sb["attT"][:, cc * CS:(cc + 1) * CS], start=True, stop=True),
            "T" if cc == CPG - 1 else None)
    c.wait("vector", "T", takw)
    vqt = c.emit("vector", lambda e: e.scalar_tensor_tensor(
        sb["qst"][:], ps[2][0:64, 0:TPG], -1.0, sb["qst"][:], ALU.mult, ALU.add), "V")

    tpt = None
    for cc in range(CPG):
        tpt = c.emit("tensor", lambda e, cc=cc: e.matmul(
            ps[7][0:64, 256 + cc * 64:256 + (cc + 1) * 64], sb["w_bf"][:, cc, 128:192],
            sb["kptm"][:, cc, :], start=True, stop=True),
            "T" if cc == CPG - 1 else None)
    vlcid = None
    for cc in range(CPG):
        vlcid = c.emit("vector", lambda e, cc=cc: e.tensor_scalar(
            sb["lcid"][:, cc, :], sb["c_ident"][0:64, 0:64],
            sb["lcbc"][:, h * CPG + cc:h * CPG + cc + 1], None, ALU.mult), "V")
    c.wait("vector", "T", tpt)
    c.wait("vector", "V", vlcid)
    vpt = None
    for cc in range(CPG):
        vpt = c.emit("vector", lambda e, cc=cc: e.scalar_tensor_tensor(
            sb["pt_bf"][:, cc, :], ps[7][0:64, 256 + cc * 64:256 + (cc + 1) * 64],
            -1.0, sb["lcid"][:, cc, :], ALU.mult, ALU.add), "V")

    if last_pow_s is not None:
        c.wait("tensor", "S", last_pow_s)
    o_last_t = None
    for cc in range(CPG):
        vprev = c.val("V")
        c.wait("tensor", "V", max(vprev, vqt, vpt, v_att))
        c.emit("tensor", lambda e, cc=cc: e.matmul(
            ps[6][:, cc * CS:(cc + 1) * CS], sb["s_st"][:, h, :],
            sb["qst"][:, cc * CS:(cc + 1) * CS], start=True, stop=False), None)
        o_last_t = c.emit("tensor", lambda e, cc=cc: e.matmul(
            ps[6][:, cc * CS:(cc + 1) * CS], sb["w_bf"][:, cc, 0:128],
            sb["attT"][:, cc * CS:(cc + 1) * CS], start=False, stop=True), "T")
        c.emit("tensor", lambda e, cc=cc: e.matmul(
            ps[3][0:64, 0:CS], sb["kptm"][:, cc, :], sb["w_bf"][:, cc, 0:128],
            start=True, stop=False), None)
        tsu = c.emit("tensor", lambda e, cc=cc: e.matmul(
            ps[3][0:64, 0:CS], sb["pt_bf"][:, cc, :], sb["s_st"][:, h, :],
            start=False, stop=True), "T")
        c.wait("vector", "T", tsu)
        c.emit("vector", lambda e: e.tensor_copy(sb["s_st"][:, h, :], ps[3][0:64, 0:CS]), "V")

    return o_last_t, ttr


def _hg_opost(c: Ctx, q, g, h, o_last_t):
    sb, ps = c.sb, c.ps
    gq = gq_of(g)
    c.wait("scalar", "T", o_last_t)
    c.wait("scalar", "V")
    if HSTAGE >= 51:
        so2 = c.emit("scalar", lambda e: e.activation(sb["o2"][:], ps[6][:, 0:TPG], AF.Square), "S")
    else:
        so2 = c.emit("scalar", lambda e: e.activation(sb["o2"][:], sb["o_bf"][:], AF.Square), "S")
    c.wait("vector", "T", o_last_t)
    c.wait("vector", "G")
    if HSTAGE >= 52:
        c.wait("scalar", "T", o_last_t)
        c.wait("scalar", "G")
        vob = c.emit("scalar", lambda e: e.activation(sb["o_bf"][:], ps[6][:, 0:TPG], AF.Copy), "S")
        c.wait("vector", "S", vob)
        vob = c.val("V")
    else:
        vob = c.emit("vector", lambda e: e.tensor_copy(sb["o_bf"][:], sb["o2"][:]), "V")
    if HSTAGE < 6:
        return
    c.wait("tensor", "S", so2)
    c.wait("tensor", "V")
    tos = c.emit("tensor", lambda e: e.matmul(
        ps[7][0:1, 0:TPG], sb["c_ones128bf"][:], sb["o2"][:], start=True, stop=True), "T")
    c.wait("scalar", "T", tos)
    sr1 = c.emit("scalar", lambda e: e.activation(
        sb["r1a"][:], ps[7][0:1, 0:TPG], AF.Ln, bias=sb["c_cbias"][0:1, 0:1], scale=1.0 / DVH), "S")
    c.wait("scalar", "S", sr1)
    sr2 = c.emit("scalar", lambda e: e.activation(sb["r1a"][:], sb["r1a"][:], AF.Exp, scale=-0.5), "S")
    if HSTAGE < 7:
        return
    c.wait("tensor", "S", sr2)
    trs = c.emit("tensor", lambda e: e.matmul(
        ps[7][:, 0:TPG], sb["c_ones1x128"][:], sb["r1a"][:], start=True, stop=True), "T")
    c.wait("vector", "T", trs)
    c.wait("vector", "V", vob)
    vg1 = c.emit("vector", lambda e: e.tensor_tensor(sb["o_bf"][:], sb["o_bf"][:], ps[7][:, 0:TPG], ALU.mult), "V")
    if HSTAGE < 8:
        return
    c.wait("gpsimd", "V", vg1)
    c.emit("gpsimd", lambda e, h=h, gq=gq: e.tensor_tensor(
        sb["ogh"][:, h, :], sb["o_bf"][:], sb["gg"][:, h, gq], ALU.mult), "G")


def _wo_partial(c: Ctx, q, g):
    sb, dr, ps = c.sb, c.dr, c.ps
    gog = c.val("G")
    c.wait("tensor", "G", gog)
    c.wait("tensor", "V")
    c.wait("vector", "OG")
    bank_ev = {0: 0, 1: 0}
    for mc in range(8):
        bank = ps[mc % 2]
        c.wait("tensor", "V", bank_ev[mc % 2])
        tm = None
        for h in range(HPC):
            tm = c.emit("tensor", lambda e, mc=mc, h=h, bank=bank: e.matmul(
                bank[:, 0:TPG], sb["wo_bf"][:, h, mc * 128:(mc + 1) * 128], sb["ogh"][:, h, :],
                start=(h == 0), stop=(h == HPC - 1)), "T" if h == HPC - 1 else None)
        c.wait("vector", "T", tm)
        bank_ev[mc % 2] = c.emit("vector", lambda e, mc=mc, bank=bank: e.tensor_copy(
            sb["sgrp"][:, mc, :], bank[:, 0:TPG]), "V")
    vs = c.val("V")
    c.wait("gpsimd", "V", vs)
    npieces = max(1, TPG // ST)
    psz = min(ST, TPG)
    for pp in range(npieces):
        stripe = (g * TPG + pp * psz) // ST
        soff = (g * TPG + pp * psz) % ST
        c.emit("gpsimd", lambda e, pp=pp, stripe=stripe, soff=soff, psz=psz, q=q: e.dma_start(
            out=dr["sbounce"][q][1024 * stripe:1024 * (stripe + 1), soff:soff + psz]
                .rearrange("(m p) t -> p m t", p=128),
            in_=sb["sgrp"][:, :, pp * psz:(pp + 1) * psz]), "OG", 16)



def emit_phase2(c: Ctx):
    sb, dr, ps = c.sb, c.dr, c.ps
    if STAGE < 6:
        c.emit("vector", lambda e: e.memset(sb["outt"][:], 0.0), "V")
        for tt in range(NQ):
            c.wait("gpsimd", "V")
            c.emit("gpsimd", lambda e, tt=tt: e.dma_start(
                out=dr["out_fm"][:, tt, :].rearrange("(m p) t -> p m t", p=128),
                in_=sb["outt"][:]), "ST", 16)
        c.wait("gpsimd", "ST")
        if STAGE >= 5:
            c.wait("gpsimd", "CC")
        return

    c.emit("sync", lambda e: e.dma_start(
        out=sb["bias_f"][:], in_=dr["biases"].rearrange("(m p) l -> p m l", p=128)), "LQ", 16)
    c.emit("sync", lambda e: e.dma_start(
        out=sb["lng_f"][:], in_=dr["ln_g"].rearrange("(m p) l -> p m l", p=128)), "LQ", 16)
    c.emit("sync", lambda e: e.dma_start(
        out=sb["lnb_f"][:], in_=dr["ln_b"].rearrange("(m p) l -> p m l", p=128)), "LQ", 16)

    if STAGE >= 7:
        _dequant_layer(c, 0, 0)
        _dequant_layer(c, 1, 1)

    for tt in range(NQ):
        c.wait("sync", "V")
        lxq = c.emit("sync", lambda e, tt=tt: e.dma_start(
            out=sb["xq_s"][:], in_=dr["x_q"][:, tt, :].rearrange("(k p) t -> p k t", p=128)), "LQ", 16)
        c.wait("vector", "LQ", lxq)
        c.wait("vector", "T")
        vx = None
        for kc in range(8):
            vx = c.emit("vector", lambda e, kc=kc: e.tensor_copy(
                sb["ybuf"][:, kc, :], sb["xq_s"][:, kc, :]), "V")
        _p2_ln(c, "ybuf", ("hbuf", tt), None, vx, gelu=False)
        if tt == 0:
            c.tap(0, c.sb["hbuf"][:, :, 0:ST])

    if STAGE >= 7:
        for blk in range(3):
            _ffn_block(c, blk)
    c.tap(5, c.sb["hbuf"][:, :, 0:ST])

    for tt in range(NQ):
        c.wait("sync", "CC", tt + 1)
        c.wait("sync", "V")
        lxq = c.emit("sync", lambda e, tt=tt: e.dma_start(
            out=sb["xq_s"][:], in_=dr["x_q"][:, tt, :].rearrange("(k p) t -> p k t", p=128)), "LQ", 16)
        c.wait("sync", "G")
        lgq = c.emit("sync", lambda e, tt=tt: e.dma_start(
            out=sb["s_bf"][:], in_=dr["sshard"][tt].rearrange("(m p) t -> p m t", p=128)), "LG", 16)
        c.wait("scalar", "V")
        sgl = None
        for mc in range(8):
            sgl = c.emit("scalar", lambda e, mc=mc, tt=tt: e.activation(
                sb["ygel"][:, mc, :], sb["hbuf"][:, mc, tt * ST:(tt + 1) * ST], AF.Silu), "S")
        c.wait("vector", "LG", lgq)
        c.wait("vector", "LQ", lxq)
        c.wait("vector", "S", sgl)
        c.wait("vector", "ST")
        vlast = None
        for mc in range(8):
            v1 = c.emit("vector", lambda e, mc=mc: e.tensor_tensor(
                sb["outt"][:, mc, :], sb["s_bf"][:, mc, :], sb["ygel"][:, mc, :], ALU.mult), "V")
            c.wait("vector", "V", v1)
            vlast = c.emit("vector", lambda e, mc=mc: e.tensor_tensor(
                sb["outt"][:, mc, :], sb["outt"][:, mc, :], sb["xq_s"][:, mc, :], ALU.add), "V")
        c.wait("gpsimd", "V", vlast)
        c.emit("gpsimd", lambda e, tt=tt: e.dma_start(
            out=dr["out_fm"][:, tt, :].rearrange("(m p) t -> p m t", p=128),
            in_=sb["outt"][:]), "ST", 16)
    c.wait("gpsimd", "ST")
    c.wait("gpsimd", "CC")


def _p2_ln(c: Ctx, srcname, dst, blk, vsrc, gelu):
    sb, ps = c.sb, c.ps
    dname, tt = dst
    dsl = slice(tt * ST, (tt + 1) * ST) if dname == "hbuf" else slice(0, ST)
    n = ST
    c.wait("scalar", "V", vsrc)
    c.wait("scalar", "T")
    c.wait("scalar", "G")
    sx2 = c.emit("scalar", lambda e: e.activation(
        sb["f2s"].rearrange("p k t -> p (k t)"),
        sb[srcname].rearrange("p k t -> p (k t)"), AF.Square), "S")
    c.wait("tensor", "V", vsrc)
    c.wait("tensor", "S", sx2)
    for kc in range(8):
        c.emit("tensor", lambda e, kc=kc: e.matmul(
            ps[7][0:1, 0:n], sb["c_ones128bf"][:], sb[srcname][:, kc, :],
            start=(kc == 0), stop=(kc == 7)), None)
    t2 = None
    for kc in range(8):
        t2 = c.emit("tensor", lambda e, kc=kc: e.matmul(
            ps[6][0:1, 0:n], sb["c_ones128bf"][:], sb["f2s"][:, kc, :],
            start=(kc == 0), stop=(kc == 7)), "T" if kc == 7 else None)
    c.wait("vector", "T", t2)
    v0 = c.emit("vector", lambda e: e.tensor_scalar(sb["p2ra"][:], ps[7][0:1, 0:n], 1.0 / D, None, ALU.mult), "V")
    v1 = c.emit("vector", lambda e: e.tensor_scalar(sb["p2rb"][:], ps[6][0:1, 0:n], 1.0 / D, None, ALU.mult), "V")
    c.wait("vector", "V", v1)
    v2 = c.emit("vector", lambda e: e.tensor_tensor(sb["p2ra"][:], sb["p2ra"][:], sb["p2ra"][:], ALU.mult), "V")
    c.wait("vector", "V", v2)
    v3 = c.emit("vector", lambda e: e.tensor_tensor(sb["p2rb"][:], sb["p2rb"][:], sb["p2ra"][:], ALU.subtract), "V")
    c.wait("scalar", "V", v3)
    s1 = c.emit("scalar", lambda e: e.activation(sb["p2ra"][:], sb["p2rb"][:], AF.Ln, bias=sb["c_cbias"][0:1, 0:1]), "S")
    c.wait("scalar", "S", s1)
    s2 = c.emit("scalar", lambda e: e.activation(sb["p2rb"][:], sb["p2ra"][:], AF.Exp, scale=-0.5), "S")
    c.wait("vector", "S", s2)
    v4 = c.emit("vector", lambda e: e.tensor_scalar(sb["p2ra"][:], ps[7][0:1, 0:n], -1.0 / D, None, ALU.mult), "V")
    c.wait("vector", "V", v4)
    v5 = c.emit("vector", lambda e: e.tensor_tensor(sb["p2ra"][:], sb["p2ra"][:], sb["p2rb"][:], ALU.mult), "V")
    c.wait("tensor", "V", v5)
    c.emit("tensor", lambda e: e.matmul(ps[7][:, 0:n], sb["c_ones1x128"][:], sb["p2rb"][:],
                                        start=True, stop=True), None)
    t3 = c.emit("tensor", lambda e: e.matmul(ps[6][:, 0:n], sb["c_ones1x128"][:], sb["p2ra"][:],
                                             start=True, stop=True), "T")
    c.wait("vector", "T", t3)
    v6 = c.emit("vector", lambda e: e.tensor_copy(sb["bcr_bf"][:], ps[7][:, 0:n]), "V")
    v7 = c.emit("vector", lambda e: e.tensor_copy(sb["bcm_bf"][:], ps[6][:, 0:n]), "V")
    c.wait("vector", "V", v7)
    c.wait("vector", "G")
    for kc in range(8):
        vz = c.emit("vector", lambda e, kc=kc: e.tensor_tensor(
            sb["f2s"][:, kc, :], sb[srcname][:, kc, :], sb["bcr_bf"][:], ALU.mult), "V")
        c.wait("vector", "V", vz)
        vu = c.emit("vector", lambda e, kc=kc: e.tensor_tensor(
            sb["f2s"][:, kc, :], sb["f2s"][:, kc, :], sb["bcm_bf"][:], ALU.add), "V")
        if gelu:
            c.wait("scalar", "V", vu)
            c.emit("scalar", lambda e, kc=kc, blk=blk: e.activation(
                sb[dname][:, kc, dsl], sb["f2s"][:, kc, :], AF.Gelu,
                bias=sb["lnb_f"][:, kc, blk:blk + 1],
                scale=sb["lng_f"][:, kc, blk:blk + 1]), "S")
        else:
            c.wait("gpsimd", "V", vu)
            c.emit("gpsimd", lambda e, kc=kc, dsl=dsl: e.tensor_copy(
                sb[dname][:, kc, dsl], sb["f2s"][:, kc, :]), "G")


def _dequant_layer(c: Ctx, l, slot):
    sb, dr, ps = c.sb, c.dr, c.ps
    c.wait("sync", "G")
    c.wait("sync", "V")
    c.emit("sync", lambda e, l=l: e.dma_start(
        out=sb["q8s"][:], in_=dr["q8"][l].rearrange("(k p) o -> p k o", p=128)), "LF", 16)
    c.emit("sync", lambda e, l=l: e.dma_start(out=sb["la_f"][:], in_=dr["la"][l]), "LF", 16)
    lf1 = c.emit("sync", lambda e, l=l: e.dma_start(out=sb["lbT_f"][:], in_=dr["lbT"][l]), "LF", 16)
    c.wait("vector", "LF", lf1)
    c.wait("gpsimd", "LF", lf1)
    vla = c.emit("vector", lambda e: e.tensor_copy(sb["la_bf"][:], sb["la_f"][:]), "V")
    vlb = c.emit("vector", lambda e: e.tensor_copy(sb["lbT_bf"][:], sb["lbT_f"][:]), "V")
    prev_weff = {0: 0, 1: 0}
    prev_g2 = {0: 0, 1: 0}
    for kc in range(8):
        pp = kc % 2
        c.wait("sync", "G", prev_g2[pp])
        lsr = c.emit("sync", lambda e, l=l, kc=kc, pp=pp: e.dma_start(
            out=sb["sreps"][:, pp, :], in_=dr["srep"][l, 128 * kc:128 * (kc + 1), :]), f"LR{pp}", 16)
        c.wait("tensor", "V", max(vla, vlb, prev_weff[0], prev_weff[1]))
        tl = None
        for half in range(2):
            tl = c.emit("tensor", lambda e, kc=kc, half=half: e.matmul(
                ps[4 + half][:, 0:512], sb["la_bf"][:, kc * 128:(kc + 1) * 128],
                sb["lbT_bf"][:, half * 512:(half + 1) * 512], start=True, stop=True),
                "T" if half == 1 else None)
        c.wait("gpsimd", f"LR{pp}", lsr)
        c.wait("gpsimd", "V", prev_weff[pp])
        g1 = c.emit("gpsimd", lambda e, kc=kc, pp=pp: e.tensor_scalar(
            sb["w1t"][:, pp, :], sb["q8s"][:, kc, :], 1.0 / 7.5, -1.0, ALU.mult, ALU.add), "G")
        c.wait("gpsimd", "G", g1)
        g2 = c.emit("gpsimd", lambda e, pp=pp: e.tensor_tensor(
            sb["w1t"][:, pp, :], sb["w1t"][:, pp, :], sb["sreps"][:, pp, :], ALU.mult), "G")
        prev_g2[pp] = g2
        c.wait("vector", "G", g2)
        c.wait("vector", "T", tl)
        vv = None
        for half in range(2):
            vv = c.emit("vector", lambda e, kc=kc, slot=slot, half=half, pp=pp: e.tensor_tensor(
                sb["weff"][:, slot, kc, half * 512:(half + 1) * 512],
                sb["w1t"][:, pp, half * 512:(half + 1) * 512],
                ps[4 + half][:, 0:512], ALU.add), "V")
        prev_weff[pp] = vv
    return c.val("V")


def _qlora_mm(c: Ctx, slot, srcname, src_tt, l, dst="ybuf", vw=None):
    sb, ps = c.sb, c.ps
    ssl = slice(src_tt * ST, (src_tt + 1) * ST) if srcname == "hbuf" else slice(0, ST)
    if vw is None:
        vw = c.val("V")
    bank_ev = {0: 0, 1: 0}
    for mc in range(8):
        bank = ps[mc % 2]
        c.wait("tensor", "V", max(vw, bank_ev[mc % 2]))
        tm = None
        for kc in range(8):
            tm = c.emit("tensor", lambda e, mc=mc, kc=kc, slot=slot, ssl=ssl, bank=bank, srcname=srcname: e.matmul(
                bank[:, 0:ST], sb["weff"][:, slot, kc, mc * 128:(mc + 1) * 128],
                sb[srcname][:, kc, ssl], start=(kc == 0), stop=(kc == 7)),
                "T" if kc == 7 else None)
        c.wait("vector", "T", tm)
        bank_ev[mc % 2] = c.emit("vector", lambda e, mc=mc, l=l, bank=bank, dst=dst: e.tensor_scalar(
            sb[dst][:, mc, 0:ST], bank[:, 0:ST], sb["bias_f"][:, mc, l:l + 1], None, ALU.add), "V")
    return c.val("V")


def _ffn_block(c: Ctx, blk):
    sb = c.sb
    l0, l1 = 2 * blk, 2 * blk + 1
    if blk > 0:
        _dequant_layer(c, l0, 0)
        _dequant_layer(c, l1, 1)
    dsts = ["ybuf", "s_bf"]
    if blk == 0:
        c.wait("tensor", "G")
    vy = _qlora_mm(c, 0, "hbuf", 0, l0, dst=dsts[0])
    for tt in range(NQ):
        if blk == 0 and tt == 0:
            c.tap(1, c.sb["ybuf"][:])
        vfree = c.val("V")
        _p2_ln(c, dsts[tt % 2], ("ygel", tt), blk, vy, gelu=True)
        if blk == 0 and tt == 0:
            c.tap(2, c.sb["ygel"][:])
        if tt + 1 < NQ:
            vy = _qlora_mm(c, 0, "hbuf", tt + 1, l0, dst=dsts[(tt + 1) % 2], vw=vfree)
        sgel = c.val("S")
        c.wait("vector", "S", sgel)
        c.wait("tensor", "S", sgel)
        vy2 = _qlora_mm(c, 1, "ygel", tt, l1, dst=dsts[tt % 2])
        if blk == 0 and tt == 0:
            c.tap(3, c.sb["ybuf"][:])
        c.wait("vector", "V", vy2)
        for kc in range(8):
            c.emit("vector", lambda e, kc=kc, tt=tt, d=dsts[tt % 2]: e.tensor_tensor(
                sb["hbuf"][:, kc, tt * ST:(tt + 1) * ST], sb[d][:, kc, 0:ST],
                sb["hbuf"][:, kc, tt * ST:(tt + 1) * ST], ALU.add), "V")
        if blk == 0 and tt == 0:
            c.tap(4, c.sb["hbuf"][:, :, 0:ST])



def kernel(**inputs):
    consts = _host_consts()
    nc = build_nc()
    in_maps = [_prep_core_inputs(core, inputs, consts) for core in range(8)]
    res = run_bass_kernel_spmd(nc, in_maps, core_ids=list(range(8)))
    x = np.asarray(inputs["L0_attn_hidden_states"], np.float32)
    out = np.array(x)
    for core in range(8):
        b, j = core // 4, core % 4
        o = np.asarray(res.results[core]["out_fm"], np.float32)
        for qq in range(NQ):
            out[b, qq * QT + j * ST: qq * QT + (j + 1) * ST] = o[:, qq].T
    return out


# version 11
# speedup vs baseline: 1.0088x; 1.0088x over previous
import numpy as np
import ml_dtypes
import concourse.bass as bass
import concourse.mybir as mybir
from concourse.bass_utils import run_bass_kernel_spmd
from contextlib import ExitStack

F32 = mybir.dt.float32
BF16 = mybir.dt.bfloat16
I8 = mybir.dt.int8
AF = mybir.ActivationFunctionType
ALU = mybir.AluOpType
NPBF = ml_dtypes.bfloat16

B = 2
L_TOT = 8192
D = 1024
NH, DK, DVH = 12, 64, 128
HPC = 3
SCALE = float(DK) ** -0.5
CS = 128
TPG = 512
CPG = TPG // CS
NQ = 4
QT = L_TOT // NQ
GPQ = QT // TPG
ST = QT // 4
NCOL = 1158
NEG = -1e30
LNS = float(np.log(SCALE))
POWJ = [2, 4, 8, 16, 32, 64]
DBG = False
STAGE = 99
HSTAGE = 99



def _bf(x):
    return np.ascontiguousarray(np.asarray(x).astype(NPBF))


def _host_consts():
    c = {}
    i = np.arange(CS)
    c["stepi"] = _bf((i[:, None] <= i[None, :]).astype(np.float32))
    c["steps"] = _bf((i[:, None] < i[None, :]).astype(np.float32))
    c["negi"] = _bf((np.eye(CS) * NEG).astype(np.float32))
    c["ident"] = _bf(np.eye(CS, dtype=np.float32))
    c["identf"] = np.eye(CS, dtype=np.float32)
    c["ones_1x128"] = np.ones((1, 128), np.float32)
    c["ones_1x128b"] = _bf(np.ones((1, 128), np.float32))
    c["ones_row3"] = np.ones((HPC, CS), np.float32)
    c["onesf"] = np.ones((128, TPG), np.float32)
    c["cbias"] = np.tile(np.array([[1e-5, 1e-6, 1.0]], np.float32), (128, 1))
    c["ones128_bf"] = _bf(np.ones((128, 1), np.float32))
    m = np.zeros((4, 128, 3), np.float32)
    m[0, 0:64, 0] = 1; m[0, 64:128, 1] = 1
    m[1, 0:64, 2] = 1
    m[2, 64:128, 0] = 1
    m[3, 0:64, 1] = 1; m[3, 64:128, 2] = 1
    for j in range(4):
        c[f"ssqind{j}"] = _bf(m[j])
    return c


def _prep_core_inputs(core, inp, consts):
    b = core // 4
    j = core % 4
    heads = [HPC * j + hh for hh in range(HPC)]
    x = np.asarray(inp["L0_attn_hidden_states"][b, :L_TOT], np.float32)
    x_fm = np.ascontiguousarray(x.T)
    xq = np.stack([x_fm[:, q * QT + j * ST: q * QT + (j + 1) * ST] for q in range(NQ)], axis=1)
    x_q = np.ascontiguousarray(xq)

    hk = np.concatenate([np.arange(h * DK, (h + 1) * DK) for h in heads])
    hv = np.concatenate([np.arange(h * DVH, (h + 1) * DVH) for h in heads])
    wproj = np.zeros((D, NCOL), np.float32)
    wproj[:, 0:192] = inp["L0_attn_Wq"][:, hk]
    wproj[:, 192:384] = inp["L0_attn_Wk"][:, hk]
    wproj[:, 384:768] = inp["L0_attn_Wv"][:, hv]
    wproj[:, 768:1152] = inp["L0_attn_Wg"][:, hv]
    wproj[:, 1152:1155] = inp["L0_attn_Wgk"][:, heads]
    wproj[:, 1155:1158] = inp["L0_attn_Wb"][:, heads]

    conv_all = np.concatenate([
        np.asarray(inp["L0_attn_conv_q"], np.float32)[hk],
        np.asarray(inp["L0_attn_conv_k"], np.float32)[hk],
        np.asarray(inp["L0_attn_conv_v"], np.float32)[hv]], axis=0)

    gvec = np.zeros((HPC, 4), np.float32)
    gvec[:, 0] = inp["L0_attn_A_log"][heads]
    gvec[:, 1] = inp["L0_attn_dt_bias"][heads]
    gvec[:, 2] = inp["L0_attn_b_b"][heads]

    wo_mine = np.ascontiguousarray(np.asarray(inp["L0_attn_Wo"], np.float32)[hv])

    qw = np.asarray(inp["L0_ffn_qweights"], np.int32)
    q8 = np.ascontiguousarray(
        np.ascontiguousarray(qw.transpose(0, 2, 1)).view(np.int8)[..., ::4])
    scales = np.asarray(inp["L0_ffn_scales"], np.float32)
    srep = np.ascontiguousarray(np.repeat(scales.transpose(0, 2, 1), 16, axis=1))
    lbT = np.ascontiguousarray(np.asarray(inp["L0_ffn_lora_b"], np.float32).transpose(0, 2, 1))

    d = dict(
        x_fm=x_fm, x_q=x_q, wproj=wproj, conv_all=conv_all, gvec=gvec,
        wo=wo_mine,
        gnw=np.asarray(inp["L0_attn_g_norm_weight"], np.float32).reshape(128, 1),
        q8=q8, srep=srep,
        la=np.asarray(inp["L0_ffn_lora_a"], np.float32),
        lbT=lbT,
        biases=np.ascontiguousarray(np.asarray(inp["L0_ffn_biases"], np.float32).T),
        ln_g=np.ascontiguousarray(np.asarray(inp["L0_ffn_ln_gamma"], np.float32).T),
        ln_b=np.ascontiguousarray(np.asarray(inp["L0_ffn_ln_beta"], np.float32).T),
    )
    d.update(consts)
    return d



class RowView:

    def __init__(self, tile, base):
        self.tile = tile
        self.base = base

    def __getitem__(self, idx):
        if not isinstance(idx, tuple):
            idx = (idx,)
        p = idx[0]
        if isinstance(p, slice):
            st = 0 if p.start is None else p.start
            en = HPC if p.stop is None else p.stop
            np_ = slice(self.base + st, self.base + en)
        else:
            np_ = self.base + p
        return self.tile.__getitem__((np_,) + tuple(idx[1:]))


class Ctx:
    def __init__(self, nc, cur, sems, dr, sb, ps):
        self.nc = nc
        self.cur = cur
        self.eng = None
        self.sems = sems
        self.dr = dr
        self.sb = sb
        self.ps = ps
        self.cnt = {}
        self.waited = {}
        self.xstg_last = {0: 0, 1: 0}

    def inc(self, name, n=1):
        self.cnt[name] = self.cnt.get(name, 0) + n
        return self.cnt[name]

    def val(self, name):
        return self.cnt.get(name, 0)

    def wait(self, engname, name, v=None):
        if v is None:
            v = self.cnt.get(name, 0)
        if self.cur != engname or v <= 0:
            return
        if self.waited.get(name, -1) >= v:
            return
        self.eng.wait_ge(self.sems[name], v)
        self.waited[name] = v

    def tap(self, idx, tile_ap):
        if not DBG:
            return
        self.wait("gpsimd", "V")
        self.wait("gpsimd", "S")
        self.wait("gpsimd", "G")
        self.emit("gpsimd", lambda e: e.dma_start(
            out=self.dr["dbg"][idx], in_=tile_ap), "ST", 16)
        self.wait("sync", "ST")
        self.wait("tensor", "ST")
        self.wait("vector", "ST")
        self.wait("scalar", "ST")
        self.wait("gpsimd", "ST")

    def emit(self, engname, fn, sem=None, n=1):
        v = self.inc(sem, n) if sem is not None else None
        if self.cur == engname:
            ins = fn(self.eng)
            if sem is not None:
                ins.then_inc(self.sems[sem], n)
        return v


SEMNAMES = ["T", "V", "S", "G", "LX0", "LX1", "LW", "SPR", "OG", "CC", "LF",
            "LR0", "LR1", "LG", "ST", "DMV", "LQ", "PH"]
ENGS = ("sync", "tensor", "vector", "scalar", "gpsimd")


def build_nc():
    nc = bass.Bass("TRN2", target_bir_lowering=False, debug=False)
    dr = {}

    def din(name, shape, dt):
        dr[name] = nc.dram_tensor(name, list(shape), dt, kind="ExternalInput").ap()

    din("x_fm", (D, L_TOT), F32)
    din("x_q", (D, NQ, ST), F32)
    din("wproj", (D, NCOL), F32)
    din("conv_all", (768, 4), F32)
    din("gvec", (HPC, 4), F32)
    din("wo", (HPC * DVH, D), F32)
    din("gnw", (128, 1), F32)
    din("q8", (6, D, D), I8)
    din("srep", (6, D, D), F32)
    din("la", (6, 32, D), F32)
    din("lbT", (6, 32, D), F32)
    din("biases", (D, 6), F32)
    din("ln_g", (D, 3), F32)
    din("ln_b", (D, 3), F32)
    for k, shape, dt in [
        ("stepi", (CS, CS), BF16), ("steps", (CS, CS), BF16), ("negi", (CS, CS), BF16),
        ("ident", (CS, CS), BF16), ("identf", (CS, CS), F32),
        ("ones_1x128", (1, 128), F32), ("ones_1x128b", (1, 128), BF16),
        ("ones_row3", (HPC, CS), F32), ("ones128_bf", (128, 1), BF16),
        ("onesf", (128, TPG), F32), ("cbias", (128, 3), F32),
        ("ssqind0", (128, 3), BF16), ("ssqind1", (128, 3), BF16),
        ("ssqind2", (128, 3), BF16), ("ssqind3", (128, 3), BF16),
    ]:
        din(k, shape, dt)
    dr["out_fm"] = nc.dram_tensor("out_fm", [D, NQ, ST], F32, kind="ExternalOutput").ap()
    dr["sbounce"] = [nc.dram_tensor(f"sbn{q}", [4 * D, ST], BF16).ap() for q in range(NQ)]
    dr["sshard"] = [nc.dram_tensor(f"ssh{q}", [D, ST], BF16).ap() for q in range(NQ)]
    if DBG:
        dr["dbg"] = nc.dram_tensor("dbg", [6, 128, 8, ST], F32, kind="ExternalOutput").ap()

    es = ExitStack()
    with es:
        sb = {}

        def T(name, shape, dt, stack=None):
            sb[name] = (stack or es).enter_context(nc.sbuf_tensor(name, list(shape), dt))

        T("wproj_bf", (128, 8, NCOL), BF16)
        T("convw", (128, 6, 4), F32)
        T("gvec_s", (HPC, 4), F32)
        T("gknegA", (HPC, 1), F32)
        T("wo_bf", (128, HPC, D), BF16)
        T("gnw_s", (128, 1), F32)
        T("c_stepi", (CS, CS), BF16)
        T("c_steps", (CS, CS), BF16)
        T("c_negi", (CS, CS), BF16)
        T("c_ident", (CS, CS), BF16)
        T("c_identf", (CS, CS), F32)
        T("c_ones1x128", (1, 128), F32)
        T("c_ones1x128b", (1, 128), BF16)
        T("c_onesrow3", (HPC, CS), F32)
        T("c_ones128bf", (128, 1), BF16)
        T("c_onesf", (128, TPG), F32)
        T("c_cbias", (128, 3), F32)
        T("c_ssqind", (128, 4, 3), BF16)

        ps = [es.enter_context(nc.psum_tensor(f"pb{i}", [128, 512], F32)) for i in range(8)]
        sems = {n: es.enter_context(nc.semaphore(n)) for n in SEMNAMES}
        ctxs = {en: Ctx(nc, en, sems, dr, sb, ps) for en in ENGS}

        with ExitStack() as es1:
            for name, shape, dt in [
                ("stg", (128, 640), F32), ("xstg", (128, 2, TPG), F32),
                ("x_bf", (128, 8, TPG), BF16), ("xn", (128, 8, TPG), BF16),
                ("bcA_bf", (128, TPG), BF16), ("bcB_bf", (128, TPG), BF16),
                ("rstdq", (1, QT), BF16), ("nmurq", (1, QT), BF16),
                ("r1a", (1, TPG), F32), ("r1b", (1, TPG), F32),
                ("praw", (128, 6, TPG + 3), BF16), ("convt", (128, TPG), BF16),
                ("qkv", (128, 6, QT), BF16), ("gg", (128, HPC, QT), BF16),
                ("zgk", (HPC, QT), F32), ("zb", (HPC, QT), BF16),
                ("sqs", (128, 3, TPG), BF16),
                ("gcum", (HPC, TPG), F32), ("lnsq_q", (HPC, TPG), F32),
                ("lnsq_k", (HPC, TPG), F32), ("lnbeta", (HPC, TPG), F32),
                ("betar", (HPC, TPG), F32), ("r3a", (HPC, TPG), F32),
                ("r3b", (HPC, TPG), F32), ("r3c", (HPC, TPG), F32),
                ("colM", (HPC, TPG), F32), ("rowM", (HPC, TPG), F32),
                ("rowA", (HPC, TPG), F32), ("lqrow", (HPC, TPG), F32),
                ("kwrow", (HPC, TPG), F32), ("kprow", (HPC, TPG), F32),
                ("lcrow", (1, HPC * CPG), F32), ("lcbc", (64, HPC * CPG), F32),
                ("sp_colM", (128, TPG), F32), ("sp_rowM", (128, TPG), F32),
                ("sp_rowA", (128, TPG), F32), ("sp_lq", (128, TPG), F32),
                ("colsh", (128, CPG, 3, HPC), F32), ("rows_bf", (3, 3, TPG), BF16),
                ("eMT", (128, TPG), BF16), ("eM", (128, TPG), BF16), ("eA", (128, TPG), BF16),
                ("mT", (128, TPG), F32), ("mN", (128, TPG), F32),
                ("attT", (128, TPG), BF16),
                ("powT", (128, 6, TPG), F32), ("powN", (128, 5, TPG), F32),
                ("qst", (64, TPG), BF16),
                ("khh", (64, TPG), BF16), ("qhh", (64, TPG), BF16),
                ("vhh", (128, TPG), BF16),
                ("wcur", (128, CPG, 192), F32), ("w_bf", (128, CPG, 192), BF16),
                ("pt_bf", (64, CPG, 64), BF16), ("lcid", (64, CPG, 64), BF16),
                ("kptm", (128, CPG, DK), BF16), ("s_st", (64, HPC, DVH), BF16),
                ("o_bf", (128, TPG), BF16),
                ("ogh", (128, HPC, TPG), BF16),
            ]:
                T(name, shape, dt, es1)

            sb["sgrp"] = sb["xn"]
            sb["o2"] = sb["eA"]

            with nc.Block() as b1:
                for en in ENGS:
                    def body(eng, en=en):
                        ctx = ctxs[en]
                        ctx.eng = eng
                        emit_phase1(ctx)
                        ctx.emit(en, lambda e: e.nop(), "PH", 1)
                    getattr(b1, en)(body)

        with ExitStack() as es2:
            for name, shape, dt in [
                ("hbuf", (128, 8, QT), BF16), ("ybuf", (128, 8, ST), BF16),
                ("ygel", (128, 8, ST), BF16), ("f2s", (128, 8, ST), BF16),
                ("q8s", (128, 8, D), I8), ("sreps", (128, 2, D), F32),
                ("w1t", (128, 2, D), BF16), ("weff", (128, 2, 8, D), BF16),
                ("la_f", (32, D), F32), ("lbT_f", (32, D), F32),
                ("la_bf", (32, D), BF16), ("lbT_bf", (32, D), BF16),
                ("bias_f", (128, 8, 6), F32), ("lng_f", (128, 8, 3), F32),
                ("lnb_f", (128, 8, 3), F32),
                ("p2ra", (1, ST), F32), ("p2rb", (1, ST), F32),
                ("bcr_bf", (128, ST), BF16), ("bcm_bf", (128, ST), BF16),
                ("s_bf", (128, 8, ST), BF16), ("outt", (128, 8, ST), F32),
                ("xq_s", (128, 8, ST), F32),
            ]:
                T(name, shape, dt, es2)

            with nc.Block() as b2:
                for en in ENGS:
                    def body(eng, en=en):
                        ctx = ctxs[en]
                        ctx.eng = eng
                        ctx.wait(en, "PH", 5)
                        emit_phase2(ctx)
                    getattr(b2, en)(body)

    return nc



def gq_of(g):
    return slice(g * TPG, (g + 1) * TPG)


KQ_SLICE = {
    "q": [(0, 0), (0, 64), (1, 0)],
    "k": [(1, 64), (2, 0), (2, 64)],
}



def emit_phase1(c: Ctx):
    emit_phase0(c)
    for q in range(NQ):
        if STAGE < 1:
            break
        _t1_stats(c, q)
        if STAGE < 2:
            continue
        _t2_proj(c, q)
        if STAGE < 3:
            continue
        _t3_scan(c, q)
        if STAGE < 5:
            continue
        nog = c.val("OG")
        c.wait("gpsimd", "OG", nog)
        c.emit("gpsimd", lambda e, q=q: e.collective_compute(
            "ReduceScatter", ALU.add, replica_groups=[[0, 1, 2, 3], [4, 5, 6, 7]],
            ins=[c.dr["sbounce"][q].opt()], outs=[c.dr["sshard"][q].opt()]), "CC", 1)


def emit_phase0(c: Ctx):
    sb, dr, ps = c.sb, c.dr, c.ps

    def dma(dst, src):
        return c.emit("sync", lambda e: e.dma_start(out=dst, in_=src), "LW", 16)

    dma(sb["c_stepi"][:], dr["stepi"])
    dma(sb["c_steps"][:], dr["steps"])
    dma(sb["c_negi"][:], dr["negi"])
    dma(sb["c_ident"][:], dr["ident"])
    dma(sb["c_identf"][:], dr["identf"])
    dma(sb["c_ones1x128"][:], dr["ones_1x128"])
    dma(sb["c_ones1x128b"][:], dr["ones_1x128b"])
    dma(sb["c_onesrow3"][:], dr["ones_row3"])
    dma(sb["c_ones128bf"][:], dr["ones128_bf"])
    dma(sb["c_onesf"][:], dr["onesf"])
    dma(sb["c_cbias"][:], dr["cbias"])
    for j in range(4):
        dma(sb["c_ssqind"][:, j], dr[f"ssqind{j}"])
    dma(sb["gvec_s"][:], dr["gvec"])
    dma(sb["gnw_s"][:], dr["gnw"])
    dma(sb["convw"][:, 0:6], dr["conv_all"].rearrange("(s p) k -> p s k", p=128))

    for kc in range(8):
        for half in range(2):
            o0 = half * 640
            w = min(640, NCOL - o0)
            c.wait("sync", "V")
            lw = dma(sb["stg"][:, 0:w], dr["wproj"][128 * kc:128 * (kc + 1), o0:o0 + w])
            c.wait("vector", "LW", lw)
            c.emit("vector", lambda e, kc=kc, o0=o0, w=w: e.tensor_copy(
                sb["wproj_bf"][:, kc, o0:o0 + w], sb["stg"][:, 0:w]), "V")
    for h in range(HPC):
        for half in range(2):
            o0 = half * 512
            c.wait("sync", "V")
            lw = dma(sb["stg"][:, 0:512], dr["wo"][128 * h:128 * (h + 1), o0:o0 + 512])
            c.wait("vector", "LW", lw)
            c.emit("vector", lambda e, h=h, o0=o0: e.tensor_scalar(
                sb["wo_bf"][:, h, o0:o0 + 512], sb["stg"][:, 0:512], sb["gnw_s"][:], None, ALU.mult), "V")
    c.wait("scalar", "LW")
    t1 = c.emit("scalar", lambda e: e.activation(sb["gknegA"][:], sb["gvec_s"][:, 0:1], AF.Exp), "S")
    c.wait("vector", "S", t1)
    c.emit("vector", lambda e: e.tensor_scalar(sb["gknegA"][:], sb["gknegA"][:], -1.0, None, ALU.mult), "V")
    c.emit("vector", lambda e: e.memset(sb["praw"][:, :, 0:3], 0.0), "V")
    c.emit("vector", lambda e: e.memset(sb["s_st"][:], 0.0), "V")


def _load_x_group(c: Ctx, t0):
    sb, dr = c.sb, c.dr
    vtoks = []
    for kc in range(8):
        pp = kc % 2
        c.wait("sync", "V", vtoks[kc - 2] if kc >= 2 else c.xstg_last[pp])
        lx = c.emit("sync", lambda e, kc=kc, t0=t0, pp=pp: e.dma_start(
            out=sb["xstg"][:, pp, :],
            in_=dr["x_fm"][128 * kc:128 * (kc + 1), t0:t0 + TPG]), f"LX{pp}", 16)
        c.wait("vector", f"LX{pp}", lx)
        vtoks.append(c.emit("vector", lambda e, kc=kc, pp=pp: e.tensor_copy(
            sb["x_bf"][:, kc, :], sb["xstg"][:, pp, :]), "V"))
    c.xstg_last[0] = vtoks[6]
    c.xstg_last[1] = vtoks[7]
    return vtoks[-1]


def _t1_stats(c: Ctx, q):
    sb, dr, ps = c.sb, c.dr, c.ps
    for g in range(GPQ):
        t0 = q * QT + g * TPG
        gq = gq_of(g)
        c.wait("vector", "V")
        c.wait("vector", "T")
        c.wait("vector", "S")
        vx = _load_x_group(c, t0)
        c.wait("scalar", "V", vx)
        c.wait("scalar", "T")
        c.wait("scalar", "OG")
        sx2 = c.emit("scalar", lambda e: e.activation(
            sb["xn"].rearrange("p k t -> p (k t)"),
            sb["x_bf"].rearrange("p k t -> p (k t)"), AF.Square), "S")
        c.wait("tensor", "V", vx)
        c.wait("tensor", "S", sx2)
        for kc in range(8):
            c.emit("tensor", lambda e, kc=kc: e.matmul(
                ps[7][0:1, 0:TPG], sb["c_ones128bf"][:], sb["x_bf"][:, kc, :],
                start=(kc == 0), stop=(kc == 7)), None)
        tsx = None
        for kc in range(8):
            tsx = c.emit("tensor", lambda e, kc=kc: e.matmul(
                ps[6][0:1, 0:TPG], sb["c_ones128bf"][:], sb["xn"][:, kc, :],
                start=(kc == 0), stop=(kc == 7)), "T" if kc == 7 else None)
        c.wait("vector", "T", tsx)
        v0 = c.emit("vector", lambda e: e.tensor_scalar(
            sb["r1a"][:], ps[7][0:1, 0:TPG], 1.0 / D, None, ALU.mult), "V")
        v1 = c.emit("vector", lambda e: e.tensor_scalar(
            sb["r1b"][:], ps[6][0:1, 0:TPG], 1.0 / D, None, ALU.mult), "V")
        c.wait("vector", "V", v1)
        v2 = c.emit("vector", lambda e: e.tensor_tensor(sb["r1a"][:], sb["r1a"][:], sb["r1a"][:], ALU.mult), "V")
        c.wait("vector", "V", v2)
        v3 = c.emit("vector", lambda e: e.tensor_tensor(sb["r1b"][:], sb["r1b"][:], sb["r1a"][:], ALU.subtract), "V")
        c.wait("scalar", "V", v3)
        s1 = c.emit("scalar", lambda e: e.activation(sb["r1a"][:], sb["r1b"][:], AF.Ln, bias=sb["c_cbias"][0:1, 0:1]), "S")
        c.wait("scalar", "S", s1)
        s2 = c.emit("scalar", lambda e, gq=gq: e.activation(
            sb["rstdq"][:, gq], sb["r1a"][:], AF.Exp, scale=-0.5), "S")
        c.wait("vector", "S", s2)
        v4 = c.emit("vector", lambda e: e.tensor_scalar(
            sb["r1a"][:], ps[7][0:1, 0:TPG], -1.0 / D, None, ALU.mult), "V")
        c.wait("vector", "V", v4)
        c.emit("vector", lambda e, gq=gq: e.tensor_tensor(
            sb["nmurq"][:, gq], sb["r1a"][:], sb["rstdq"][:, gq], ALU.mult), "V")


def _t2_proj(c: Ctx, q):
    sb, dr, ps = c.sb, c.dr, c.ps
    NSLICE = [(s * 128, 128) for s in range(9)] + [(1152, 3), (1155, 3)]
    prev_ssl = 0
    bank_free = {0: None, 1: None, 2: None}
    for g in range(GPQ):
        t0 = q * QT + g * TPG
        gq = gq_of(g)
        c.wait("vector", "V")
        c.wait("vector", "S")
        c.wait("vector", "T")
        c.wait("scalar", "T")
        c.wait("scalar", "G")
        vx = _load_x_group(c, t0)
        vrows = c.val("V")
        c.wait("tensor", "V", vrows)
        c.emit("tensor", lambda e, gq=gq: e.matmul(
            ps[3][:, 0:TPG], sb["c_ones1x128b"][:], sb["rstdq"][:, gq], start=True, stop=True), None)
        tbc = c.emit("tensor", lambda e, gq=gq: e.matmul(
            ps[4][:, 0:TPG], sb["c_ones1x128b"][:], sb["nmurq"][:, gq], start=True, stop=True), "T")
        c.wait("vector", "T", tbc)
        va = c.emit("vector", lambda e: e.tensor_copy(sb["bcA_bf"][:], ps[3][:, 0:TPG]), "V")
        vb = c.emit("vector", lambda e: e.tensor_copy(sb["bcB_bf"][:], ps[4][:, 0:TPG]), "V")
        c.wait("vector", "V", vb)
        c.wait("gpsimd", "T")
        vxn1 = None
        for kc in range(8):
            vxn1 = c.emit("vector", lambda e, kc=kc: e.tensor_tensor(
                sb["xn"][:, kc, :], sb["x_bf"][:, kc, :], sb["bcA_bf"][:], ALU.mult), "V")
        c.wait("gpsimd", "V", vxn1)
        gxn = None
        for kc in range(8):
            gxn = c.emit("gpsimd", lambda e, kc=kc: e.tensor_tensor(
                sb["xn"][:, kc, :], sb["xn"][:, kc, :], sb["bcB_bf"][:], ALU.add), "G")
        c.wait("tensor", "G", gxn)
        for s, (off, w) in enumerate(NSLICE):
            bank = ps[s % 3]
            bf = bank_free[s % 3]
            if bf is not None:
                c.wait("tensor", bf[0], bf[1])
            tsl = None
            for kc in range(8):
                tsl = c.emit("tensor", lambda e, kc=kc, off=off, w=w, bank=bank: e.matmul(
                    bank[0:w, 0:TPG], sb["wproj_bf"][:, kc, off:off + w], sb["xn"][:, kc, :],
                    start=(kc == 0), stop=(kc == 7)), "T" if kc == 7 else None)
            if s <= 5:
                c.wait("vector", "T", tsl)
                vev = c.emit("vector", lambda e, s=s, bank=bank: e.tensor_copy(
                    sb["praw"][:, s, 3:3 + TPG], bank[0:128, 0:TPG]), "V")
                bank_free[s % 3] = ("V", vev)
                c.wait("vector", "V", vev)
                c.wait("vector", "S", prev_ssl)
                vcv = c.emit("vector", lambda e, s=s: e.tensor_scalar(
                    sb["convt"][:], sb["praw"][:, s, 0:TPG], sb["convw"][:, s, 0:1], None, ALU.mult), "V")
                for jj in (1, 2, 3):
                    c.wait("vector", "V", vcv)
                    vcv = c.emit("vector", lambda e, s=s, jj=jj: e.scalar_tensor_tensor(
                        sb["convt"][:], sb["praw"][:, s, jj:jj + TPG], sb["convw"][:, s, jj:jj + 1],
                        sb["convt"][:], ALU.mult, ALU.add), "V")
                c.wait("scalar", "V", vcv)
                prev_ssl = c.emit("scalar", lambda e, s=s, gq=gq: e.activation(
                    sb["qkv"][:, s, gq], sb["convt"][:], AF.Silu), "S")
                c.wait("vector", "V", vcv)
                c.emit("vector", lambda e, s=s: e.tensor_copy(
                    sb["praw"][:, s, 0:3], sb["praw"][:, s, TPG:TPG + 3]), "V")
            elif s <= 8:
                h = s - 6
                c.wait("scalar", "T", tsl)
                ssil = c.emit("scalar", lambda e, h=h, gq=gq, bank=bank: e.activation(
                    sb["gg"][:, h, gq], bank[0:128, 0:TPG], AF.Silu), "S")
                bank_free[s % 3] = ("S", ssil)
            elif s == 9:
                c.wait("vector", "T", tsl)
                vz = c.emit("vector", lambda e, gq=gq, bank=bank: e.tensor_scalar(
                    sb["zgk"][:, gq], bank[0:HPC, 0:TPG], sb["gvec_s"][:, 1:2], None, ALU.add), "V")
                bank_free[s % 3] = ("V", vz)
            else:
                c.wait("vector", "T", tsl)
                vz = c.emit("vector", lambda e, gq=gq, bank=bank: e.tensor_scalar(
                    sb["zb"][:, gq], bank[0:HPC, 0:TPG], sb["gvec_s"][:, 2:3], None, ALU.add), "V")
                bank_free[s % 3] = ("V", vz)


def _t3_scan(c: Ctx, q):
    _gate_prep(c, q, 0)
    for g in range(GPQ):
        if STAGE >= 4:
            pend = None
            prev_ttr = None
            for h in range(HPC):
                t_a = _hg_head(c, q, g, h, prev_ttr)
                if pend is not None:
                    _hg_opost(c, q, g, h - 1, pend)
                r = _hg_body(c, q, g, h, t_a)
                pend, prev_ttr = (r if r is not None else (None, None))
            if pend is not None:
                _hg_opost(c, q, g, HPC - 1, pend)
        if g + 1 < GPQ:
            _gate_prep(c, q, g + 1)
        if STAGE < 4 or STAGE < 5 or HSTAGE < 9:
            continue
        _wo_partial(c, q, g)


def _gate_prep(c: Ctx, q, g):
    sb, dr, ps = c.sb, c.dr, c.ps
    gq = gq_of(g)

    c.wait("vector", "V")
    c.wait("scalar", "V")
    c.wait("scalar", "T")
    sp = None
    for j in range(3):
        sp = c.emit("scalar", lambda e, j=j, gq=gq: e.activation(
            sb["sqs"][:, j, :], sb["qkv"][:, j, gq], AF.Square), "S")
    c.wait("tensor", "S", sp)
    c.wait("tensor", "V")
    c.emit("tensor", lambda e: e.matmul(ps[7][0:3, 0:TPG], sb["c_ssqind"][:, 0], sb["sqs"][:, 0, :],
                                        start=True, stop=False), None)
    t_q = c.emit("tensor", lambda e: e.matmul(ps[7][0:3, 0:TPG], sb["c_ssqind"][:, 1], sb["sqs"][:, 1, :],
                                              start=False, stop=True), "T")
    c.emit("tensor", lambda e: e.matmul(ps[6][0:3, 0:TPG], sb["c_ssqind"][:, 2], sb["sqs"][:, 1, :],
                                        start=True, stop=False), None)
    t_k = c.emit("tensor", lambda e: e.matmul(ps[6][0:3, 0:TPG], sb["c_ssqind"][:, 3], sb["sqs"][:, 2, :],
                                              start=False, stop=True), "T")
    c.wait("scalar", "T", t_k)
    s_q = c.emit("scalar", lambda e: e.activation(sb["lnsq_q"][:], ps[7][0:3, 0:TPG], AF.Ln, bias=sb["c_cbias"][0:HPC, 1:2]), "S")
    s_k = c.emit("scalar", lambda e: e.activation(sb["lnsq_k"][:], ps[6][0:3, 0:TPG], AF.Ln, bias=sb["c_cbias"][0:HPC, 1:2]), "S")

    sa = c.emit("scalar", lambda e, gq=gq: e.activation(sb["r3a"][:], sb["zgk"][:, gq], AF.Abs), "S")
    c.wait("scalar", "S", sa)
    sb1 = c.emit("scalar", lambda e: e.activation(sb["r3b"][:], sb["r3a"][:], AF.Exp, scale=-1.0), "S")
    c.wait("scalar", "S", sb1)
    c.emit("scalar", lambda e: e.activation(sb["r3b"][:], sb["r3b"][:], AF.Ln, bias=sb["c_cbias"][0:HPC, 2:3]), "S")
    sc1 = c.emit("scalar", lambda e, gq=gq: e.activation(sb["r3c"][:], sb["zgk"][:, gq], AF.Relu), "S")
    c.wait("vector", "S", sc1)
    v1 = c.emit("vector", lambda e: e.tensor_tensor(sb["r3b"][:], sb["r3b"][:], sb["r3c"][:], ALU.add), "V")
    c.wait("vector", "V", v1)
    v2 = c.emit("vector", lambda e: e.tensor_scalar(sb["r3a"][:], sb["r3b"][:], sb["gknegA"][:], None, ALU.mult), "V")
    c.wait("vector", "V", v2)
    vgc = None
    for cc in range(CPG):
        vgc = c.emit("vector", lambda e, cc=cc: e.tensor_tensor_scan(
            sb["gcum"][:, cc * CS:(cc + 1) * CS], sb["c_onesrow3"][:],
            sb["r3a"][:, cc * CS:(cc + 1) * CS], 0.0, ALU.mult, ALU.add), "V")

    c.wait("scalar", "V", v2)
    sd = c.emit("scalar", lambda e, gq=gq: e.activation(sb["r3b"][:], sb["zb"][:, gq], AF.Abs), "S")
    c.wait("scalar", "S", sd)
    se = c.emit("scalar", lambda e: e.activation(sb["r3b"][:], sb["r3b"][:], AF.Exp, scale=-1.0), "S")
    c.wait("scalar", "S", se)
    c.emit("scalar", lambda e: e.activation(sb["r3b"][:], sb["r3b"][:], AF.Ln, bias=sb["c_cbias"][0:HPC, 2:3]), "S")
    sg = c.emit("scalar", lambda e, gq=gq: e.activation(sb["r3c"][:], sb["zb"][:, gq], AF.Relu, scale=-1.0), "S")
    c.wait("vector", "S", sg)
    vlb = c.emit("vector", lambda e: e.scalar_tensor_tensor(
        sb["lnbeta"][:], sb["r3b"][:], -1.0, sb["r3c"][:], ALU.mult, ALU.subtract), "V")
    c.wait("scalar", "V", vlb)
    sbeta = c.emit("scalar", lambda e: e.activation(sb["betar"][:], sb["lnbeta"][:], AF.Exp), "S")

    c.wait("vector", "S", s_k)
    c.wait("vector", "V", max(vgc, vlb))
    vc1 = c.emit("vector", lambda e: e.scalar_tensor_tensor(
        sb["colM"][:], sb["lnsq_k"][:], -0.5, sb["gcum"][:], ALU.mult, ALU.subtract), "V")
    vc2 = c.emit("vector", lambda e: e.scalar_tensor_tensor(
        sb["r3b"][:], sb["lnsq_k"][:], -0.5, sb["gcum"][:], ALU.mult, ALU.add), "V")
    c.wait("vector", "V", max(vc2, vlb))
    vrm = c.emit("vector", lambda e: e.tensor_tensor(sb["rowM"][:], sb["r3b"][:], sb["lnbeta"][:], ALU.add), "V")
    c.wait("vector", "S", s_q)
    vc3 = c.emit("vector", lambda e: e.scalar_tensor_tensor(
        sb["r3c"][:], sb["lnsq_q"][:], -0.5, sb["gcum"][:], ALU.mult, ALU.add), "V")
    c.wait("vector", "V", vc3)
    vra = c.emit("vector", lambda e: e.tensor_scalar(sb["rowA"][:], sb["r3c"][:], LNS, None, ALU.add), "V")
    c.wait("vector", "V", vc1)
    vkpr = None
    for cc in range(CPG):
        vkpr = c.emit("vector", lambda e, cc=cc: e.tensor_scalar(
            sb["kprow"][:, cc * CS:(cc + 1) * CS], sb["colM"][:, cc * CS:(cc + 1) * CS],
            sb["gcum"][:, cc * CS + CS - 1:cc * CS + CS], None, ALU.add), "V")
    c.wait("scalar", "V", max(vra, vkpr, vrm))
    slq = c.emit("scalar", lambda e: e.activation(sb["lqrow"][:], sb["rowA"][:], AF.Exp), "S")
    skw = c.emit("scalar", lambda e: e.activation(sb["kwrow"][:], sb["rowM"][:], AF.Exp), "S")
    skp = c.emit("scalar", lambda e: e.activation(sb["kprow"][:], sb["kprow"][:], AF.Exp), "S")
    vkw = c.val("V")

    c.wait("sync", "T")
    c.wait("sync", "V", max(vc1, vrm, vra))
    c.wait("sync", "S", slq)
    for a, b in (("colM", "sp_colM"), ("rowM", "sp_rowM"), ("rowA", "sp_rowA"), ("lqrow", "sp_lq")):
        c.emit("sync", lambda e, a=a, b=b: e.dma_start(
            out=c.sb[b][0:HPC * 32:32, :], in_=c.sb[a][:]), "SPR", 16)

    c.wait("sync", "V", vgc)
    dlc = None
    for hh in range(HPC):
        dlc = c.emit("sync", lambda e, hh=hh: e.dma_start(
            out=sb["lcrow"][0:1, CPG * hh:CPG * (hh + 1)],
            in_=sb["gcum"][hh:hh + 1, CS - 1::CS],
            allow_slow_non_contiguous=True), "DMV", 16)
    c.wait("scalar", "DMV", dlc)
    slc = c.emit("scalar", lambda e: e.activation(sb["lcrow"][:], sb["lcrow"][:], AF.Exp), "S")
    c.wait("tensor", "S", slc)
    c.wait("tensor", "V")
    tlc = c.emit("tensor", lambda e: e.matmul(
        ps[7][0:64, 0:HPC * CPG], sb["c_ones1x128"][:, 0:64], sb["lcrow"][:],
        start=True, stop=True), "T")
    c.wait("vector", "T", tlc)
    c.emit("vector", lambda e: e.tensor_copy(sb["lcbc"][:], ps[7][0:64, 0:HPC * CPG]), "V")

    c.wait("vector", "S", max(skp, sbeta))
    c.wait("vector", "V", vkw)
    vrb = None
    for si, src in enumerate(("betar", "kwrow", "kprow")):
        vrb = c.emit("vector", lambda e, si=si, src=src: e.tensor_copy(
            sb["rows_bf"][:, si, :], sb[src][:]), "V")
    c.wait("tensor", "V", vrb)
    tcs = None
    for cc in range(CPG):
        for si in range(3):
            tcs = c.emit("tensor", lambda e, cc=cc, si=si: e.transpose(
                ps[6][:, 2 * (3 * cc + si):2 * (3 * cc + si) + 2].bitcast(BF16)[:, 0:HPC],
                sb["rows_bf"][:, si, cc * CS:(cc + 1) * CS], sb["c_ident"][0:HPC, 0:HPC]),
                "T" if (cc == CPG - 1 and si == 2) else None)
    c.wait("vector", "T", tcs)
    c.emit("vector", lambda e: e.tensor_copy(
        sb["colsh"].rearrange("p cc s hh -> p (cc s) hh"),
        ps[6][:, 0:2 * 3 * CPG].bitcast(BF16).rearrange(
            "p (g four) -> p g four", four=4)[:, :, 0:HPC]), "V")


def _hg_head(c: Ctx, q, g, h, prev_ttr):
    sb, ps = c.sb, c.ps
    gq = gq_of(g)
    sq, bq = KQ_SLICE["q"][h]
    sk, bk = KQ_SLICE["k"][h]

    def kfm(cc):
        return sb["khh"][:, cc * CS:(cc + 1) * CS]

    def qfm(cc):
        return sb["qhh"][:, cc * CS:(cc + 1) * CS]

    sgate = c.val("S")

    c.wait("vector", "S", sgate)
    if prev_ttr is None:
        c.wait("vector", "T")
    else:
        c.wait("vector", "T", prev_ttr)
    c.emit("vector", lambda e: e.tensor_copy(
        sb["khh"][:], sb["qkv"][bk:bk + 64, sk, gq]), "V")
    c.emit("vector", lambda e: e.tensor_copy(
        sb["qhh"][:], sb["qkv"][bq:bq + 64, sq, gq]), "V")
    vcp = c.emit("vector", lambda e: e.tensor_copy(
        sb["vhh"][:], sb["qkv"][:, 3 + h, gq]), "V")
    vgate = c.val("V")

    c.wait("tensor", "V", vgate)
    c.wait("tensor", "S", sgate)
    t_a = None
    for cc in range(CPG):
        c.emit("tensor", lambda e, cc=cc: e.matmul(
            ps[0][:, cc * CS:(cc + 1) * CS], kfm(cc), kfm(cc), start=True, stop=True), None)
        t_a = c.emit("tensor", lambda e, cc=cc: e.matmul(
            ps[1][:, cc * CS:(cc + 1) * CS], kfm(cc), qfm(cc), start=True, stop=True),
            "T" if cc == CPG - 1 else None)
    return t_a


def _hg_body(c: Ctx, q, g, h, t_a):
    sb, dr, ps = c.sb, c.dr, c.ps
    gq = gq_of(g)

    def kfm(cc):
        return sb["khh"][:, cc * CS:(cc + 1) * CS]

    def qfm(cc):
        return sb["qhh"][:, cc * CS:(cc + 1) * CS]

    def vfm(cc):
        return sb["vhh"][:, cc * CS:(cc + 1) * CS]

    spr = c.val("SPR")
    c.wait("tensor", "SPR", spr)

    def arg_form(colsp, rowsp, mask_lhs, mask_rhs, expname, bank):
        for cc in range(CPG):
            c.emit("tensor", lambda e, cc=cc, colsp=colsp, bank=bank: e.matmul(
                bank[:, cc * CS:(cc + 1) * CS],
                sb[colsp][32 * h:32 * h + 1, cc * CS:(cc + 1) * CS],
                sb["c_onesf"][32 * h:32 * h + 1, 0:128], start=(cc == 0), stop=False), None)
        c.emit("tensor", lambda e, rowsp=rowsp, bank=bank: e.matmul(
            bank[:, 0:TPG], sb["c_onesf"][32 * h:32 * h + 1, 0:128],
            sb[rowsp][32 * h:32 * h + 1, :], start=False, stop=False), None)
        tlast = None
        for cc in range(CPG):
            tlast = c.emit("tensor", lambda e, cc=cc, mask_lhs=mask_lhs, mask_rhs=mask_rhs, bank=bank: e.matmul(
                bank[:, cc * CS:(cc + 1) * CS], sb[mask_lhs][:], sb[mask_rhs][:],
                start=False, stop=(cc == CPG - 1)), "T" if cc == CPG - 1 else None)
        c.wait("scalar", "T", tlast)
        return c.emit("scalar", lambda e, expname=expname, bank=bank: e.activation(
            sb[expname][:], bank[:, 0:TPG], AF.Exp), "S")

    if HSTAGE < 1:
        return
    s1 = arg_form("sp_colM", "sp_rowM", "c_stepi", "c_negi", "eMT", ps[2])

    ttr = None
    for cc in range(CPG):
        c.emit("tensor", lambda e, cc=cc: e.transpose(
            ps[5][:, cc * 64:(cc + 1) * 64].bitcast(BF16), vfm(cc), sb["c_ident"][:]), None)
        ttr = c.emit("tensor", lambda e, cc=cc: e.transpose(
            ps[5][:, 256 + cc * 32:256 + (cc + 1) * 32].bitcast(BF16), kfm(cc),
            sb["c_ident"][0:64, 0:64]), "T" if cc == CPG - 1 else None)

    s2 = arg_form("sp_rowM", "sp_colM", "c_negi", "c_stepi", "eM", ps[3])

    tlq = c.emit("tensor", lambda e: e.matmul(
        ps[4][0:64, 0:TPG], sb["c_onesf"][32 * h:32 * h + 1, 0:64],
        sb["sp_lq"][32 * h:32 * h + 1, :], start=True, stop=True), "T")

    c.wait("tensor", "S", s1)
    s3 = arg_form("sp_colM", "sp_rowA", "c_steps", "c_negi", "eA", ps[2])

    c.wait("vector", "S", s1)
    c.wait("vector", "T", t_a)
    v_mT = c.emit("vector", lambda e: e.tensor_tensor(sb["mT"][:], sb["eMT"][:], ps[0][:, 0:TPG], ALU.mult), "V")
    c.wait("vector", "S", s2)
    v_mN = c.emit("vector", lambda e: e.tensor_tensor(sb["mN"][:], sb["eM"][:], ps[0][:, 0:TPG], ALU.mult), "V")
    c.wait("vector", "S", s3)
    v_att = c.emit("vector", lambda e: e.tensor_tensor(sb["attT"][:], sb["eA"][:], ps[1][:, 0:TPG], ALU.mult), "V")

    if HSTAGE < 2:
        return
    c.wait("vector", "T", ttr)
    vkp = None
    for cc in range(CPG):
        vkp = c.emit("vector", lambda e, cc=cc: e.tensor_scalar(
            sb["kptm"][:, cc, :], ps[5][:, 256 + cc * 32:256 + (cc + 1) * 32].bitcast(BF16),
            sb["colsh"][:, cc, 2, h:h + 1], None, ALU.mult), "V")
    c.wait("vector", "T", tlq)
    vqs = None
    for cc in range(CPG):
        vqs = c.emit("vector", lambda e, cc=cc: e.tensor_tensor(
            sb["qst"][:, cc * CS:(cc + 1) * CS], qfm(cc), ps[4][0:64, cc * CS:(cc + 1) * CS],
            ALU.mult), "V")

    if HSTAGE < 4:
        return
    vrhs = None
    for cc in range(CPG):
        c.emit("vector", lambda e, cc=cc: e.tensor_scalar(
            sb["wcur"][:, cc, 0:128], ps[5][:, cc * 64:(cc + 1) * 64].bitcast(BF16),
            sb["colsh"][:, cc, 0, h:h + 1], None, ALU.mult), "V")
        vrhs = c.emit("vector", lambda e, cc=cc: e.tensor_scalar(
            sb["wcur"][:, cc, 128:192], ps[5][:, 256 + cc * 32:256 + (cc + 1) * 32].bitcast(BF16),
            sb["colsh"][:, cc, 1, h:h + 1], None, ALU.mult), "V")

    def srcT(ji):
        return sb["mT"][:] if ji < 0 else sb["powT"][:, ji, :]

    def srcN(ji):
        return sb["mN"][:] if ji < 0 else sb["powN"][:, ji, :]

    prevNtok, prevTtok = v_mN, v_mT
    prevNeng = prevTeng = "V"
    powT_tok = {}
    last_pow_s = None
    vw = vrhs
    for si, j in enumerate([1] + POWJ):
        c.wait("tensor", "V", vw)
        if si > 0:
            c.wait("tensor", "S", powT_tok[si - 1])
        tj = None
        for cc in range(CPG):
            lhs = (sb["mT"][:, cc * CS:(cc + 1) * CS] if j == 1
                   else sb["powT"][:, si - 1, cc * CS:(cc + 1) * CS])
            bank = ps[6] if cc < 2 else ps[2]
            tj = c.emit("tensor", lambda e, lhs=lhs, cc=cc, bank=bank: e.matmul(
                bank[:, (cc % 2) * 192:(cc % 2) * 192 + 192], lhs,
                sb["wcur"][:, cc, :], start=True, stop=True),
                "T" if cc == CPG - 1 else None)
        tn = tt = None
        if si < 6:
            j2 = POWJ[si]
            if si == 0:
                c.wait("tensor", "V", max(prevNtok, prevTtok, vqs))
            else:
                c.wait("tensor", "S", max(prevNtok, prevTtok))
            if j2 != 64:
                for cc in range(CPG):
                    tn = c.emit("tensor", lambda e, si=si, cc=cc: e.matmul(
                        ps[3][:, cc * CS:(cc + 1) * CS],
                        srcT(si - 1)[:, cc * CS:(cc + 1) * CS],
                        srcN(si - 1)[:, cc * CS:(cc + 1) * CS],
                        start=True, stop=True), "T" if cc == CPG - 1 else None)
            for cc in range(CPG):
                tt = c.emit("tensor", lambda e, si=si, cc=cc: e.matmul(
                    ps[4][:, cc * CS:(cc + 1) * CS],
                    srcN(si - 1)[:, cc * CS:(cc + 1) * CS],
                    srcT(si - 1)[:, cc * CS:(cc + 1) * CS],
                    start=True, stop=True), "T" if cc == CPG - 1 else None)
        c.wait("vector", "T", tj)
        op = ALU.subtract if j == 1 else ALU.add
        c.emit("vector", lambda e, op=op: e.tensor_tensor(
            sb["wcur"].rearrange("p cc w -> p (cc w)")[:, 0:384],
            sb["wcur"].rearrange("p cc w -> p (cc w)")[:, 0:384],
            ps[6][:, 0:384], op), "V")
        vw = c.emit("vector", lambda e, op=op: e.tensor_tensor(
            sb["wcur"].rearrange("p cc w -> p (cc w)")[:, 384:768],
            sb["wcur"].rearrange("p cc w -> p (cc w)")[:, 384:768],
            ps[2][:, 0:384], op), "V")
        if si < 6:
            if POWJ[si] != 64:
                c.wait("scalar", "T", tn)
                newN = c.emit("scalar", lambda e, si=si: e.activation(
                    sb["powN"][:, si, :], ps[3][:, 0:TPG], AF.Copy), "S")
            else:
                newN = prevNtok
            c.wait("scalar", "T", tt)
            newT = c.emit("scalar", lambda e, si=si: e.activation(
                sb["powT"][:, si, :], ps[4][:, 0:TPG], AF.Copy), "S")
            prevNtok, prevTtok = newN, newT
            powT_tok[si] = newT
            last_pow_s = newT
    c.wait("vector", "V", vw)
    vwb = c.emit("vector", lambda e: e.tensor_copy(
        sb["w_bf"].rearrange("p cc w -> p (cc w)"),
        sb["wcur"].rearrange("p cc w -> p (cc w)")), "V")

    c.wait("tensor", "V", vwb)
    takw = None
    for cc in range(CPG):
        takw = c.emit("tensor", lambda e, cc=cc: e.matmul(
            ps[2][0:64, cc * CS:(cc + 1) * CS], sb["w_bf"][:, cc, 128:192],
            sb["attT"][:, cc * CS:(cc + 1) * CS], start=True, stop=True),
            "T" if cc == CPG - 1 else None)
    c.wait("vector", "T", takw)
    vqt = c.emit("vector", lambda e: e.scalar_tensor_tensor(
        sb["qst"][:], ps[2][0:64, 0:TPG], -1.0, sb["qst"][:], ALU.mult, ALU.add), "V")

    tpt = None
    for cc in range(CPG):
        tpt = c.emit("tensor", lambda e, cc=cc: e.matmul(
            ps[7][0:64, 256 + cc * 64:256 + (cc + 1) * 64], sb["w_bf"][:, cc, 128:192],
            sb["kptm"][:, cc, :], start=True, stop=True),
            "T" if cc == CPG - 1 else None)
    vlcid = None
    for cc in range(CPG):
        vlcid = c.emit("vector", lambda e, cc=cc: e.tensor_scalar(
            sb["lcid"][:, cc, :], sb["c_ident"][0:64, 0:64],
            sb["lcbc"][:, h * CPG + cc:h * CPG + cc + 1], None, ALU.mult), "V")
    c.wait("vector", "T", tpt)
    c.wait("vector", "V", vlcid)
    vpt = None
    for cc in range(CPG):
        vpt = c.emit("vector", lambda e, cc=cc: e.scalar_tensor_tensor(
            sb["pt_bf"][:, cc, :], ps[7][0:64, 256 + cc * 64:256 + (cc + 1) * 64],
            -1.0, sb["lcid"][:, cc, :], ALU.mult, ALU.add), "V")

    if last_pow_s is not None:
        c.wait("tensor", "S", last_pow_s)
    o_last_t = None
    for cc in range(CPG):
        vprev = c.val("V")
        c.wait("tensor", "V", max(vprev, vqt, vpt, v_att))
        c.emit("tensor", lambda e, cc=cc: e.matmul(
            ps[6][:, cc * CS:(cc + 1) * CS], sb["s_st"][:, h, :],
            sb["qst"][:, cc * CS:(cc + 1) * CS], start=True, stop=False), None)
        o_last_t = c.emit("tensor", lambda e, cc=cc: e.matmul(
            ps[6][:, cc * CS:(cc + 1) * CS], sb["w_bf"][:, cc, 0:128],
            sb["attT"][:, cc * CS:(cc + 1) * CS], start=False, stop=True), "T")
        c.emit("tensor", lambda e, cc=cc: e.matmul(
            ps[3][0:64, 0:CS], sb["kptm"][:, cc, :], sb["w_bf"][:, cc, 0:128],
            start=True, stop=False), None)
        tsu = c.emit("tensor", lambda e, cc=cc: e.matmul(
            ps[3][0:64, 0:CS], sb["pt_bf"][:, cc, :], sb["s_st"][:, h, :],
            start=False, stop=True), "T")
        c.wait("vector", "T", tsu)
        c.emit("vector", lambda e: e.tensor_copy(sb["s_st"][:, h, :], ps[3][0:64, 0:CS]), "V")

    return o_last_t, ttr


def _hg_opost(c: Ctx, q, g, h, o_last_t):
    sb, ps = c.sb, c.ps
    gq = gq_of(g)
    c.wait("scalar", "T", o_last_t)
    c.wait("scalar", "V")
    if HSTAGE >= 51:
        so2 = c.emit("scalar", lambda e: e.activation(sb["o2"][:], ps[6][:, 0:TPG], AF.Square), "S")
    else:
        so2 = c.emit("scalar", lambda e: e.activation(sb["o2"][:], sb["o_bf"][:], AF.Square), "S")
    c.wait("vector", "T", o_last_t)
    c.wait("vector", "G")
    if HSTAGE >= 52:
        c.wait("scalar", "T", o_last_t)
        c.wait("scalar", "G")
        vob = c.emit("scalar", lambda e: e.activation(sb["o_bf"][:], ps[6][:, 0:TPG], AF.Copy), "S")
        c.wait("vector", "S", vob)
        vob = c.val("V")
    else:
        vob = c.emit("vector", lambda e: e.tensor_copy(sb["o_bf"][:], sb["o2"][:]), "V")
    if HSTAGE < 6:
        return
    c.wait("tensor", "S", so2)
    c.wait("tensor", "V")
    tos = c.emit("tensor", lambda e: e.matmul(
        ps[7][0:1, 0:TPG], sb["c_ones128bf"][:], sb["o2"][:], start=True, stop=True), "T")
    c.wait("scalar", "T", tos)
    sr1 = c.emit("scalar", lambda e: e.activation(
        sb["r1a"][:], ps[7][0:1, 0:TPG], AF.Ln, bias=sb["c_cbias"][0:1, 0:1], scale=1.0 / DVH), "S")
    c.wait("scalar", "S", sr1)
    sr2 = c.emit("scalar", lambda e: e.activation(sb["r1a"][:], sb["r1a"][:], AF.Exp, scale=-0.5), "S")
    if HSTAGE < 7:
        return
    c.wait("tensor", "S", sr2)
    trs = c.emit("tensor", lambda e: e.matmul(
        ps[7][:, 0:TPG], sb["c_ones1x128"][:], sb["r1a"][:], start=True, stop=True), "T")
    c.wait("vector", "T", trs)
    c.wait("vector", "V", vob)
    vg1 = c.emit("vector", lambda e: e.tensor_tensor(sb["o_bf"][:], sb["o_bf"][:], ps[7][:, 0:TPG], ALU.mult), "V")
    if HSTAGE < 8:
        return
    c.wait("gpsimd", "V", vg1)
    c.emit("gpsimd", lambda e, h=h, gq=gq: e.tensor_tensor(
        sb["ogh"][:, h, :], sb["o_bf"][:], sb["gg"][:, h, gq], ALU.mult), "G")


def _wo_partial(c: Ctx, q, g):
    sb, dr, ps = c.sb, c.dr, c.ps
    gog = c.val("G")
    c.wait("tensor", "G", gog)
    c.wait("tensor", "V")
    c.wait("vector", "OG")
    bank_ev = {0: 0, 1: 0, 2: 0, 3: 0}
    for mc in range(8):
        bank = ps[mc % 4]
        c.wait("tensor", "V", bank_ev[mc % 4])
        tm = None
        for h in range(HPC):
            tm = c.emit("tensor", lambda e, mc=mc, h=h, bank=bank: e.matmul(
                bank[:, 0:TPG], sb["wo_bf"][:, h, mc * 128:(mc + 1) * 128], sb["ogh"][:, h, :],
                start=(h == 0), stop=(h == HPC - 1)), "T" if h == HPC - 1 else None)
        c.wait("vector", "T", tm)
        bank_ev[mc % 4] = c.emit("vector", lambda e, mc=mc, bank=bank: e.tensor_copy(
            sb["sgrp"][:, mc, :], bank[:, 0:TPG]), "V")
    vs = c.val("V")
    c.wait("gpsimd", "V", vs)
    npieces = max(1, TPG // ST)
    psz = min(ST, TPG)
    for pp in range(npieces):
        stripe = (g * TPG + pp * psz) // ST
        soff = (g * TPG + pp * psz) % ST
        c.emit("gpsimd", lambda e, pp=pp, stripe=stripe, soff=soff, psz=psz, q=q: e.dma_start(
            out=dr["sbounce"][q][1024 * stripe:1024 * (stripe + 1), soff:soff + psz]
                .rearrange("(m p) t -> p m t", p=128),
            in_=sb["sgrp"][:, :, pp * psz:(pp + 1) * psz]), "OG", 16)



def emit_phase2(c: Ctx):
    sb, dr, ps = c.sb, c.dr, c.ps
    if STAGE < 6:
        c.emit("vector", lambda e: e.memset(sb["outt"][:], 0.0), "V")
        for tt in range(NQ):
            c.wait("gpsimd", "V")
            c.emit("gpsimd", lambda e, tt=tt: e.dma_start(
                out=dr["out_fm"][:, tt, :].rearrange("(m p) t -> p m t", p=128),
                in_=sb["outt"][:]), "ST", 16)
        c.wait("gpsimd", "ST")
        if STAGE >= 5:
            c.wait("gpsimd", "CC")
        return

    c.emit("sync", lambda e: e.dma_start(
        out=sb["bias_f"][:], in_=dr["biases"].rearrange("(m p) l -> p m l", p=128)), "LQ", 16)
    c.emit("sync", lambda e: e.dma_start(
        out=sb["lng_f"][:], in_=dr["ln_g"].rearrange("(m p) l -> p m l", p=128)), "LQ", 16)
    c.emit("sync", lambda e: e.dma_start(
        out=sb["lnb_f"][:], in_=dr["ln_b"].rearrange("(m p) l -> p m l", p=128)), "LQ", 16)

    if STAGE >= 7:
        _dequant_layer(c, 0, 0)
        _dequant_layer(c, 1, 1)

    for tt in range(NQ):
        c.wait("sync", "V")
        lxq = c.emit("sync", lambda e, tt=tt: e.dma_start(
            out=sb["xq_s"][:], in_=dr["x_q"][:, tt, :].rearrange("(k p) t -> p k t", p=128)), "LQ", 16)
        c.wait("vector", "LQ", lxq)
        c.wait("vector", "T")
        vx = None
        for kc in range(8):
            vx = c.emit("vector", lambda e, kc=kc: e.tensor_copy(
                sb["ybuf"][:, kc, :], sb["xq_s"][:, kc, :]), "V")
        _p2_ln(c, "ybuf", ("hbuf", tt), None, vx, gelu=False)
        if tt == 0:
            c.tap(0, c.sb["hbuf"][:, :, 0:ST])

    if STAGE >= 7:
        for blk in range(3):
            _ffn_block(c, blk)
    c.tap(5, c.sb["hbuf"][:, :, 0:ST])

    for tt in range(NQ):
        c.wait("sync", "CC", tt + 1)
        c.wait("sync", "V")
        lxq = c.emit("sync", lambda e, tt=tt: e.dma_start(
            out=sb["xq_s"][:], in_=dr["x_q"][:, tt, :].rearrange("(k p) t -> p k t", p=128)), "LQ", 16)
        c.wait("sync", "G")
        lgq = c.emit("sync", lambda e, tt=tt: e.dma_start(
            out=sb["s_bf"][:], in_=dr["sshard"][tt].rearrange("(m p) t -> p m t", p=128)), "LG", 16)
        c.wait("scalar", "V")
        sgl = None
        for mc in range(8):
            sgl = c.emit("scalar", lambda e, mc=mc, tt=tt: e.activation(
                sb["ygel"][:, mc, :], sb["hbuf"][:, mc, tt * ST:(tt + 1) * ST], AF.Silu), "S")
        c.wait("vector", "LG", lgq)
        c.wait("vector", "LQ", lxq)
        c.wait("vector", "S", sgl)
        c.wait("vector", "ST")
        vlast = None
        for mc in range(8):
            v1 = c.emit("vector", lambda e, mc=mc: e.tensor_tensor(
                sb["outt"][:, mc, :], sb["s_bf"][:, mc, :], sb["ygel"][:, mc, :], ALU.mult), "V")
            c.wait("vector", "V", v1)
            vlast = c.emit("vector", lambda e, mc=mc: e.tensor_tensor(
                sb["outt"][:, mc, :], sb["outt"][:, mc, :], sb["xq_s"][:, mc, :], ALU.add), "V")
        c.wait("gpsimd", "V", vlast)
        c.emit("gpsimd", lambda e, tt=tt: e.dma_start(
            out=dr["out_fm"][:, tt, :].rearrange("(m p) t -> p m t", p=128),
            in_=sb["outt"][:]), "ST", 16)
    c.wait("gpsimd", "ST")
    c.wait("gpsimd", "CC")


def _p2_ln(c: Ctx, srcname, dst, blk, vsrc, gelu):
    sb, ps = c.sb, c.ps
    dname, tt = dst
    dsl = slice(tt * ST, (tt + 1) * ST) if dname == "hbuf" else slice(0, ST)
    n = ST
    c.wait("scalar", "V", vsrc)
    c.wait("scalar", "T")
    c.wait("scalar", "G")
    sx2 = c.emit("scalar", lambda e: e.activation(
        sb["f2s"].rearrange("p k t -> p (k t)"),
        sb[srcname].rearrange("p k t -> p (k t)"), AF.Square), "S")
    c.wait("tensor", "V", vsrc)
    c.wait("tensor", "S", sx2)
    for kc in range(8):
        c.emit("tensor", lambda e, kc=kc: e.matmul(
            ps[7][0:1, 0:n], sb["c_ones128bf"][:], sb[srcname][:, kc, :],
            start=(kc == 0), stop=(kc == 7)), None)
    t2 = None
    for kc in range(8):
        t2 = c.emit("tensor", lambda e, kc=kc: e.matmul(
            ps[6][0:1, 0:n], sb["c_ones128bf"][:], sb["f2s"][:, kc, :],
            start=(kc == 0), stop=(kc == 7)), "T" if kc == 7 else None)
    c.wait("vector", "T", t2)
    v0 = c.emit("vector", lambda e: e.tensor_scalar(sb["p2ra"][:], ps[7][0:1, 0:n], 1.0 / D, None, ALU.mult), "V")
    v1 = c.emit("vector", lambda e: e.tensor_scalar(sb["p2rb"][:], ps[6][0:1, 0:n], 1.0 / D, None, ALU.mult), "V")
    c.wait("vector", "V", v1)
    v2 = c.emit("vector", lambda e: e.tensor_tensor(sb["p2ra"][:], sb["p2ra"][:], sb["p2ra"][:], ALU.mult), "V")
    c.wait("vector", "V", v2)
    v3 = c.emit("vector", lambda e: e.tensor_tensor(sb["p2rb"][:], sb["p2rb"][:], sb["p2ra"][:], ALU.subtract), "V")
    c.wait("scalar", "V", v3)
    s1 = c.emit("scalar", lambda e: e.activation(sb["p2ra"][:], sb["p2rb"][:], AF.Ln, bias=sb["c_cbias"][0:1, 0:1]), "S")
    c.wait("scalar", "S", s1)
    s2 = c.emit("scalar", lambda e: e.activation(sb["p2rb"][:], sb["p2ra"][:], AF.Exp, scale=-0.5), "S")
    c.wait("vector", "S", s2)
    v4 = c.emit("vector", lambda e: e.tensor_scalar(sb["p2ra"][:], ps[7][0:1, 0:n], -1.0 / D, None, ALU.mult), "V")
    c.wait("vector", "V", v4)
    v5 = c.emit("vector", lambda e: e.tensor_tensor(sb["p2ra"][:], sb["p2ra"][:], sb["p2rb"][:], ALU.mult), "V")
    c.wait("tensor", "V", v5)
    c.emit("tensor", lambda e: e.matmul(ps[7][:, 0:n], sb["c_ones1x128"][:], sb["p2rb"][:],
                                        start=True, stop=True), None)
    t3 = c.emit("tensor", lambda e: e.matmul(ps[6][:, 0:n], sb["c_ones1x128"][:], sb["p2ra"][:],
                                             start=True, stop=True), "T")
    c.wait("vector", "T", t3)
    v6 = c.emit("vector", lambda e: e.tensor_copy(sb["bcr_bf"][:], ps[7][:, 0:n]), "V")
    v7 = c.emit("vector", lambda e: e.tensor_copy(sb["bcm_bf"][:], ps[6][:, 0:n]), "V")
    c.wait("vector", "V", v7)
    c.wait("vector", "G")
    for kc in range(8):
        vz = c.emit("vector", lambda e, kc=kc: e.tensor_tensor(
            sb["f2s"][:, kc, :], sb[srcname][:, kc, :], sb["bcr_bf"][:], ALU.mult), "V")
        c.wait("vector", "V", vz)
        vu = c.emit("vector", lambda e, kc=kc: e.tensor_tensor(
            sb["f2s"][:, kc, :], sb["f2s"][:, kc, :], sb["bcm_bf"][:], ALU.add), "V")
        if gelu:
            c.wait("scalar", "V", vu)
            c.emit("scalar", lambda e, kc=kc, blk=blk: e.activation(
                sb[dname][:, kc, dsl], sb["f2s"][:, kc, :], AF.Gelu,
                bias=sb["lnb_f"][:, kc, blk:blk + 1],
                scale=sb["lng_f"][:, kc, blk:blk + 1]), "S")
        else:
            c.wait("gpsimd", "V", vu)
            c.emit("gpsimd", lambda e, kc=kc, dsl=dsl: e.tensor_copy(
                sb[dname][:, kc, dsl], sb["f2s"][:, kc, :]), "G")


def _dequant_layer(c: Ctx, l, slot):
    sb, dr, ps = c.sb, c.dr, c.ps
    c.wait("sync", "G")
    c.wait("sync", "V")
    c.emit("sync", lambda e, l=l: e.dma_start(
        out=sb["q8s"][:], in_=dr["q8"][l].rearrange("(k p) o -> p k o", p=128)), "LF", 16)
    c.emit("sync", lambda e, l=l: e.dma_start(out=sb["la_f"][:], in_=dr["la"][l]), "LF", 16)
    lf1 = c.emit("sync", lambda e, l=l: e.dma_start(out=sb["lbT_f"][:], in_=dr["lbT"][l]), "LF", 16)
    c.wait("vector", "LF", lf1)
    c.wait("gpsimd", "LF", lf1)
    vla = c.emit("vector", lambda e: e.tensor_copy(sb["la_bf"][:], sb["la_f"][:]), "V")
    vlb = c.emit("vector", lambda e: e.tensor_copy(sb["lbT_bf"][:], sb["lbT_f"][:]), "V")
    prev_weff = {0: 0, 1: 0}
    prev_g2 = {0: 0, 1: 0}
    for kc in range(8):
        pp = kc % 2
        c.wait("sync", "G", prev_g2[pp])
        lsr = c.emit("sync", lambda e, l=l, kc=kc, pp=pp: e.dma_start(
            out=sb["sreps"][:, pp, :], in_=dr["srep"][l, 128 * kc:128 * (kc + 1), :]), f"LR{pp}", 16)
        c.wait("tensor", "V", max(vla, vlb, prev_weff[0], prev_weff[1]))
        tl = None
        for half in range(2):
            tl = c.emit("tensor", lambda e, kc=kc, half=half: e.matmul(
                ps[4 + half][:, 0:512], sb["la_bf"][:, kc * 128:(kc + 1) * 128],
                sb["lbT_bf"][:, half * 512:(half + 1) * 512], start=True, stop=True),
                "T" if half == 1 else None)
        c.wait("gpsimd", f"LR{pp}", lsr)
        c.wait("gpsimd", "V", prev_weff[pp])
        g1 = c.emit("gpsimd", lambda e, kc=kc, pp=pp: e.tensor_scalar(
            sb["w1t"][:, pp, :], sb["q8s"][:, kc, :], 1.0 / 7.5, -1.0, ALU.mult, ALU.add), "G")
        c.wait("gpsimd", "G", g1)
        g2 = c.emit("gpsimd", lambda e, pp=pp: e.tensor_tensor(
            sb["w1t"][:, pp, :], sb["w1t"][:, pp, :], sb["sreps"][:, pp, :], ALU.mult), "G")
        prev_g2[pp] = g2
        c.wait("vector", "G", g2)
        c.wait("vector", "T", tl)
        vv = None
        for half in range(2):
            vv = c.emit("vector", lambda e, kc=kc, slot=slot, half=half, pp=pp: e.tensor_tensor(
                sb["weff"][:, slot, kc, half * 512:(half + 1) * 512],
                sb["w1t"][:, pp, half * 512:(half + 1) * 512],
                ps[4 + half][:, 0:512], ALU.add), "V")
        prev_weff[pp] = vv
    return c.val("V")


def _qlora_mm(c: Ctx, slot, srcname, src_tt, l, dst="ybuf"):
    sb, ps = c.sb, c.ps
    ssl = slice(src_tt * ST, (src_tt + 1) * ST) if srcname == "hbuf" else slice(0, ST)
    vw = c.val("V")
    bank_ev = {0: 0, 1: 0, 2: 0, 3: 0}
    for mc in range(8):
        bank = ps[mc % 4]
        c.wait("tensor", "V", max(vw, bank_ev[mc % 4]))
        tm = None
        for kc in range(8):
            tm = c.emit("tensor", lambda e, mc=mc, kc=kc, slot=slot, ssl=ssl, bank=bank, srcname=srcname: e.matmul(
                bank[:, 0:ST], sb["weff"][:, slot, kc, mc * 128:(mc + 1) * 128],
                sb[srcname][:, kc, ssl], start=(kc == 0), stop=(kc == 7)),
                "T" if kc == 7 else None)
        c.wait("vector", "T", tm)
        bank_ev[mc % 4] = c.emit("vector", lambda e, mc=mc, l=l, bank=bank, dst=dst: e.tensor_scalar(
            sb[dst][:, mc, 0:ST], bank[:, 0:ST], sb["bias_f"][:, mc, l:l + 1], None, ALU.add), "V")
    return c.val("V")


def _ffn_block(c: Ctx, blk):
    sb = c.sb
    l0, l1 = 2 * blk, 2 * blk + 1
    if blk > 0:
        _dequant_layer(c, l0, 0)
        _dequant_layer(c, l1, 1)
    dsts = ["ybuf", "s_bf"]
    if blk == 0:
        c.wait("tensor", "G")
    vy = _qlora_mm(c, 0, "hbuf", 0, l0, dst=dsts[0])
    for tt in range(NQ):
        if blk == 0 and tt == 0:
            c.tap(1, c.sb["ybuf"][:])
        _p2_ln(c, dsts[tt % 2], ("ygel", tt), blk, vy, gelu=True)
        if blk == 0 and tt == 0:
            c.tap(2, c.sb["ygel"][:])
        if tt + 1 < NQ:
            vy = _qlora_mm(c, 0, "hbuf", tt + 1, l0, dst=dsts[(tt + 1) % 2])
        sgel = c.val("S")
        c.wait("vector", "S", sgel)
        c.wait("tensor", "S", sgel)
        vy2 = _qlora_mm(c, 1, "ygel", tt, l1, dst=dsts[tt % 2])
        if blk == 0 and tt == 0:
            c.tap(3, c.sb["ybuf"][:])
        c.wait("vector", "V", vy2)
        for kc in range(8):
            c.emit("vector", lambda e, kc=kc, tt=tt, d=dsts[tt % 2]: e.tensor_tensor(
                sb["hbuf"][:, kc, tt * ST:(tt + 1) * ST], sb[d][:, kc, 0:ST],
                sb["hbuf"][:, kc, tt * ST:(tt + 1) * ST], ALU.add), "V")
        if blk == 0 and tt == 0:
            c.tap(4, c.sb["hbuf"][:, :, 0:ST])



def kernel(**inputs):
    consts = _host_consts()
    nc = build_nc()
    in_maps = [_prep_core_inputs(core, inputs, consts) for core in range(8)]
    res = run_bass_kernel_spmd(nc, in_maps, core_ids=list(range(8)))
    x = np.asarray(inputs["L0_attn_hidden_states"], np.float32)
    out = np.array(x)
    for core in range(8):
        b, j = core // 4, core % 4
        o = np.asarray(res.results[core]["out_fm"], np.float32)
        for qq in range(NQ):
            out[b, qq * QT + j * ST: qq * QT + (j + 1) * ST] = o[:, qq].T
    return out


# version 12
# speedup vs baseline: 1.0112x; 1.0024x over previous
import numpy as np
import ml_dtypes
import concourse.bass as bass
import concourse.mybir as mybir
from concourse.bass_utils import run_bass_kernel_spmd
from contextlib import ExitStack

F32 = mybir.dt.float32
BF16 = mybir.dt.bfloat16
I8 = mybir.dt.int8
AF = mybir.ActivationFunctionType
ALU = mybir.AluOpType
NPBF = ml_dtypes.bfloat16

B = 2
L_TOT = 8192
D = 1024
NH, DK, DVH = 12, 64, 128
HPC = 3
SCALE = float(DK) ** -0.5
CS = 128
TPG = 512
CPG = TPG // CS
NQ = 4
QT = L_TOT // NQ
GPQ = QT // TPG
ST = QT // 4
NCOL = 1158
NEG = -1e30
LNS = float(np.log(SCALE))
POWJ = [2, 4, 8, 16, 32, 64]
DBG = False
STAGE = 99
HSTAGE = 99



def _bf(x):
    return np.ascontiguousarray(np.asarray(x).astype(NPBF))


def _host_consts():
    c = {}
    i = np.arange(CS)
    c["stepi"] = _bf((i[:, None] <= i[None, :]).astype(np.float32))
    c["steps"] = _bf((i[:, None] < i[None, :]).astype(np.float32))
    c["negi"] = _bf((np.eye(CS) * NEG).astype(np.float32))
    c["ident"] = _bf(np.eye(CS, dtype=np.float32))
    c["identf"] = np.eye(CS, dtype=np.float32)
    c["ones_1x128"] = np.ones((1, 128), np.float32)
    c["ones_1x128b"] = _bf(np.ones((1, 128), np.float32))
    c["ones_row3"] = np.ones((HPC, CS), np.float32)
    c["onesf"] = np.ones((128, TPG), np.float32)
    c["cbias"] = np.tile(np.array([[1e-5, 1e-6, 1.0]], np.float32), (128, 1))
    c["ones128_bf"] = _bf(np.ones((128, 1), np.float32))
    m = np.zeros((4, 128, 3), np.float32)
    m[0, 0:64, 0] = 1; m[0, 64:128, 1] = 1
    m[1, 0:64, 2] = 1
    m[2, 64:128, 0] = 1
    m[3, 0:64, 1] = 1; m[3, 64:128, 2] = 1
    for j in range(4):
        c[f"ssqind{j}"] = _bf(m[j])
    return c


def _prep_core_inputs(core, inp, consts):
    b = core // 4
    j = core % 4
    heads = [HPC * j + hh for hh in range(HPC)]
    x = np.asarray(inp["L0_attn_hidden_states"][b, :L_TOT], np.float32)
    x_fm = np.ascontiguousarray(x.T)
    xq = np.stack([x_fm[:, q * QT + j * ST: q * QT + (j + 1) * ST] for q in range(NQ)], axis=1)
    x_q = np.ascontiguousarray(xq)

    hk = np.concatenate([np.arange(h * DK, (h + 1) * DK) for h in heads])
    hv = np.concatenate([np.arange(h * DVH, (h + 1) * DVH) for h in heads])
    wproj = np.zeros((D, NCOL), np.float32)
    wproj[:, 0:192] = inp["L0_attn_Wq"][:, hk]
    wproj[:, 192:384] = inp["L0_attn_Wk"][:, hk]
    wproj[:, 384:768] = inp["L0_attn_Wv"][:, hv]
    wproj[:, 768:1152] = inp["L0_attn_Wg"][:, hv]
    wproj[:, 1152:1155] = inp["L0_attn_Wgk"][:, heads]
    wproj[:, 1155:1158] = inp["L0_attn_Wb"][:, heads]

    conv_all = np.concatenate([
        np.asarray(inp["L0_attn_conv_q"], np.float32)[hk],
        np.asarray(inp["L0_attn_conv_k"], np.float32)[hk],
        np.asarray(inp["L0_attn_conv_v"], np.float32)[hv]], axis=0)

    gvec = np.zeros((HPC, 4), np.float32)
    gvec[:, 0] = inp["L0_attn_A_log"][heads]
    gvec[:, 1] = inp["L0_attn_dt_bias"][heads]
    gvec[:, 2] = inp["L0_attn_b_b"][heads]

    wo_mine = np.ascontiguousarray(np.asarray(inp["L0_attn_Wo"], np.float32)[hv])

    qw = np.asarray(inp["L0_ffn_qweights"], np.int32)
    q8 = np.ascontiguousarray(
        np.ascontiguousarray(qw.transpose(0, 2, 1)).view(np.int8)[..., ::4])
    scales = np.asarray(inp["L0_ffn_scales"], np.float32)
    srep = np.ascontiguousarray(np.repeat(scales.transpose(0, 2, 1), 16, axis=1))
    lbT = np.ascontiguousarray(np.asarray(inp["L0_ffn_lora_b"], np.float32).transpose(0, 2, 1))

    d = dict(
        x_fm=x_fm, x_q=x_q, wproj=wproj, conv_all=conv_all, gvec=gvec,
        wo=wo_mine,
        gnw=np.asarray(inp["L0_attn_g_norm_weight"], np.float32).reshape(128, 1),
        q8=q8, srep=srep,
        la=np.asarray(inp["L0_ffn_lora_a"], np.float32),
        lbT=lbT,
        biases=np.ascontiguousarray(np.asarray(inp["L0_ffn_biases"], np.float32).T),
        ln_g=np.ascontiguousarray(np.asarray(inp["L0_ffn_ln_gamma"], np.float32).T),
        ln_b=np.ascontiguousarray(np.asarray(inp["L0_ffn_ln_beta"], np.float32).T),
    )
    d.update(consts)
    return d



class RowView:

    def __init__(self, tile, base):
        self.tile = tile
        self.base = base

    def __getitem__(self, idx):
        if not isinstance(idx, tuple):
            idx = (idx,)
        p = idx[0]
        if isinstance(p, slice):
            st = 0 if p.start is None else p.start
            en = HPC if p.stop is None else p.stop
            np_ = slice(self.base + st, self.base + en)
        else:
            np_ = self.base + p
        return self.tile.__getitem__((np_,) + tuple(idx[1:]))


class Ctx:
    def __init__(self, nc, cur, sems, dr, sb, ps):
        self.nc = nc
        self.cur = cur
        self.eng = None
        self.sems = sems
        self.dr = dr
        self.sb = sb
        self.ps = ps
        self.cnt = {}
        self.waited = {}
        self.xstg_last = {0: 0, 1: 0}

    def inc(self, name, n=1):
        self.cnt[name] = self.cnt.get(name, 0) + n
        return self.cnt[name]

    def val(self, name):
        return self.cnt.get(name, 0)

    def wait(self, engname, name, v=None):
        if v is None:
            v = self.cnt.get(name, 0)
        if self.cur != engname or v <= 0:
            return
        if self.waited.get(name, -1) >= v:
            return
        self.eng.wait_ge(self.sems[name], v)
        self.waited[name] = v

    def tap(self, idx, tile_ap):
        if not DBG:
            return
        self.wait("gpsimd", "V")
        self.wait("gpsimd", "S")
        self.wait("gpsimd", "G")
        self.emit("gpsimd", lambda e: e.dma_start(
            out=self.dr["dbg"][idx], in_=tile_ap), "ST", 16)
        self.wait("sync", "ST")
        self.wait("tensor", "ST")
        self.wait("vector", "ST")
        self.wait("scalar", "ST")
        self.wait("gpsimd", "ST")

    def emit(self, engname, fn, sem=None, n=1):
        v = self.inc(sem, n) if sem is not None else None
        if self.cur == engname:
            ins = fn(self.eng)
            if sem is not None:
                ins.then_inc(self.sems[sem], n)
        return v


SEMNAMES = ["T", "V", "S", "G", "LX0", "LX1", "LW", "SPR", "OG", "CC", "LF",
            "LR0", "LR1", "LG", "ST", "DMV", "LQ", "PH"]
ENGS = ("sync", "tensor", "vector", "scalar", "gpsimd")


def build_nc():
    nc = bass.Bass("TRN2", target_bir_lowering=False, debug=False)
    dr = {}

    def din(name, shape, dt):
        dr[name] = nc.dram_tensor(name, list(shape), dt, kind="ExternalInput").ap()

    din("x_fm", (D, L_TOT), F32)
    din("x_q", (D, NQ, ST), F32)
    din("wproj", (D, NCOL), F32)
    din("conv_all", (768, 4), F32)
    din("gvec", (HPC, 4), F32)
    din("wo", (HPC * DVH, D), F32)
    din("gnw", (128, 1), F32)
    din("q8", (6, D, D), I8)
    din("srep", (6, D, D), F32)
    din("la", (6, 32, D), F32)
    din("lbT", (6, 32, D), F32)
    din("biases", (D, 6), F32)
    din("ln_g", (D, 3), F32)
    din("ln_b", (D, 3), F32)
    for k, shape, dt in [
        ("stepi", (CS, CS), BF16), ("steps", (CS, CS), BF16), ("negi", (CS, CS), BF16),
        ("ident", (CS, CS), BF16), ("identf", (CS, CS), F32),
        ("ones_1x128", (1, 128), F32), ("ones_1x128b", (1, 128), BF16),
        ("ones_row3", (HPC, CS), F32), ("ones128_bf", (128, 1), BF16),
        ("onesf", (128, TPG), F32), ("cbias", (128, 3), F32),
        ("ssqind0", (128, 3), BF16), ("ssqind1", (128, 3), BF16),
        ("ssqind2", (128, 3), BF16), ("ssqind3", (128, 3), BF16),
    ]:
        din(k, shape, dt)
    dr["out_fm"] = nc.dram_tensor("out_fm", [D, NQ, ST], F32, kind="ExternalOutput").ap()
    dr["sbounce"] = [nc.dram_tensor(f"sbn{q}", [4 * D, ST], BF16).ap() for q in range(NQ)]
    dr["sshard"] = [nc.dram_tensor(f"ssh{q}", [D, ST], BF16).ap() for q in range(NQ)]
    if DBG:
        dr["dbg"] = nc.dram_tensor("dbg", [6, 128, 8, ST], F32, kind="ExternalOutput").ap()

    es = ExitStack()
    with es:
        sb = {}

        def T(name, shape, dt, stack=None):
            sb[name] = (stack or es).enter_context(nc.sbuf_tensor(name, list(shape), dt))

        T("wproj_bf", (128, 8, NCOL), BF16)
        T("convw", (128, 6, 4), F32)
        T("gvec_s", (HPC, 4), F32)
        T("gknegA", (HPC, 1), F32)
        T("wo_bf", (128, HPC, D), BF16)
        T("gnw_s", (128, 1), F32)
        T("c_stepi", (CS, CS), BF16)
        T("c_steps", (CS, CS), BF16)
        T("c_negi", (CS, CS), BF16)
        T("c_ident", (CS, CS), BF16)
        T("c_identf", (CS, CS), F32)
        T("c_ones1x128", (1, 128), F32)
        T("c_ones1x128b", (1, 128), BF16)
        T("c_onesrow3", (HPC, CS), F32)
        T("c_ones128bf", (128, 1), BF16)
        T("c_onesf", (128, TPG), F32)
        T("c_cbias", (128, 3), F32)
        T("c_ssqind", (128, 4, 3), BF16)

        ps = [es.enter_context(nc.psum_tensor(f"pb{i}", [128, 512], F32)) for i in range(8)]
        sems = {n: es.enter_context(nc.semaphore(n)) for n in SEMNAMES}
        ctxs = {en: Ctx(nc, en, sems, dr, sb, ps) for en in ENGS}

        with ExitStack() as es1:
            for name, shape, dt in [
                ("stg", (128, 640), F32), ("xstg", (128, 2, TPG), F32),
                ("x_bf", (128, 8, TPG), BF16), ("xn", (128, 8, TPG), BF16),
                ("bcA_bf", (128, TPG), BF16), ("bcB_bf", (128, TPG), BF16),
                ("rstdq", (1, QT), BF16), ("nmurq", (1, QT), BF16),
                ("r1a", (1, TPG), F32), ("r1b", (1, TPG), F32),
                ("praw", (128, 6, TPG + 3), BF16), ("convt", (128, TPG), BF16),
                ("qkv", (128, 6, QT), BF16), ("gg", (128, HPC, QT), BF16),
                ("zgk", (HPC, QT), F32), ("zb", (HPC, QT), BF16),
                ("sqs", (128, 3, TPG), BF16),
                ("gcum", (HPC, TPG), F32), ("lnsq_q", (HPC, TPG), F32),
                ("lnsq_k", (HPC, TPG), F32), ("lnbeta", (HPC, TPG), F32),
                ("betar", (HPC, TPG), F32), ("r3a", (HPC, TPG), F32),
                ("r3b", (HPC, TPG), F32), ("r3c", (HPC, TPG), F32),
                ("colM", (HPC, TPG), F32), ("rowM", (HPC, TPG), F32),
                ("rowA", (HPC, TPG), F32), ("lqrow", (HPC, TPG), F32),
                ("kwrow", (HPC, TPG), F32), ("kprow", (HPC, TPG), F32),
                ("lcrow", (1, HPC * CPG), F32), ("lcbc", (64, HPC * CPG), F32),
                ("sp_colM", (128, TPG), F32), ("sp_rowM", (128, TPG), F32),
                ("sp_rowA", (128, TPG), F32), ("sp_lq", (128, TPG), F32),
                ("colsh", (128, CPG, 3, HPC), F32), ("rows_bf", (3, 3, TPG), BF16),
                ("eMT", (128, TPG), BF16), ("eM", (128, TPG), BF16), ("eA", (128, TPG), BF16),
                ("mT", (128, TPG), F32), ("mN", (128, TPG), F32),
                ("attT", (128, TPG), BF16),
                ("powT", (128, 6, TPG), F32), ("powN", (128, 5, TPG), F32),
                ("qst", (64, TPG), BF16),
                ("khh", (64, TPG), BF16), ("qhh", (64, TPG), BF16),
                ("vhh", (128, TPG), BF16),
                ("wcur", (128, CPG, 192), F32), ("w_bf", (128, CPG, 192), BF16),
                ("pt_bf", (64, CPG, 64), BF16), ("lcid", (64, CPG, 64), BF16),
                ("kptm", (128, CPG, DK), BF16), ("s_st", (64, HPC, DVH), BF16),
                ("o_bf", (128, TPG), BF16),
                ("ogh", (128, HPC, TPG), BF16),
            ]:
                T(name, shape, dt, es1)

            sb["sgrp"] = sb["xn"]
            sb["o2"] = sb["eA"]

            with nc.Block() as b1:
                for en in ENGS:
                    def body(eng, en=en):
                        ctx = ctxs[en]
                        ctx.eng = eng
                        emit_phase1(ctx)
                        ctx.emit(en, lambda e: e.nop(), "PH", 1)
                    getattr(b1, en)(body)

        with ExitStack() as es2:
            for name, shape, dt in [
                ("hbuf", (128, 8, QT), BF16), ("ybuf", (128, 8, ST), BF16),
                ("ygel", (128, 8, ST), BF16), ("f2s", (128, 8, ST), BF16),
                ("q8s", (128, 8, D), I8), ("sreps", (128, 2, D), F32),
                ("w1t", (128, 2, D), BF16), ("weff", (128, 2, 8, D), BF16),
                ("la_f", (32, D), F32), ("lbT_f", (32, D), F32),
                ("la_bf", (32, D), BF16), ("lbT_bf", (32, D), BF16),
                ("bias_f", (128, 8, 6), F32), ("lng_f", (128, 8, 3), F32),
                ("lnb_f", (128, 8, 3), F32),
                ("p2ra", (1, ST), F32), ("p2rb", (1, ST), F32),
                ("bcr_bf", (128, ST), BF16), ("bcm_bf", (128, ST), BF16),
                ("s_bf", (128, 8, ST), BF16), ("outt", (128, 8, ST), F32),
                ("xq_s", (128, 8, ST), F32),
            ]:
                T(name, shape, dt, es2)

            with nc.Block() as b2:
                for en in ENGS:
                    def body(eng, en=en):
                        ctx = ctxs[en]
                        ctx.eng = eng
                        ctx.wait(en, "PH", 5)
                        emit_phase2(ctx)
                    getattr(b2, en)(body)

    return nc



def gq_of(g):
    return slice(g * TPG, (g + 1) * TPG)


KQ_SLICE = {
    "q": [(0, 0), (0, 64), (1, 0)],
    "k": [(1, 64), (2, 0), (2, 64)],
}



def emit_phase1(c: Ctx):
    emit_phase0(c)
    for q in range(NQ):
        if STAGE < 1:
            break
        _t1_stats(c, q)
        if STAGE < 2:
            continue
        _t2_proj(c, q)
        if STAGE < 3:
            continue
        _t3_scan(c, q)
        if STAGE < 5:
            continue
        nog = c.val("OG")
        c.wait("gpsimd", "OG", nog)
        c.emit("gpsimd", lambda e, q=q: e.collective_compute(
            "ReduceScatter", ALU.add, replica_groups=[[0, 1, 2, 3], [4, 5, 6, 7]],
            ins=[c.dr["sbounce"][q].opt()], outs=[c.dr["sshard"][q].opt()]), "CC", 1)


def emit_phase0(c: Ctx):
    sb, dr, ps = c.sb, c.dr, c.ps

    def dma(dst, src):
        return c.emit("sync", lambda e: e.dma_start(out=dst, in_=src), "LW", 16)

    dma(sb["c_stepi"][:], dr["stepi"])
    dma(sb["c_steps"][:], dr["steps"])
    dma(sb["c_negi"][:], dr["negi"])
    dma(sb["c_ident"][:], dr["ident"])
    dma(sb["c_identf"][:], dr["identf"])
    dma(sb["c_ones1x128"][:], dr["ones_1x128"])
    dma(sb["c_ones1x128b"][:], dr["ones_1x128b"])
    dma(sb["c_onesrow3"][:], dr["ones_row3"])
    dma(sb["c_ones128bf"][:], dr["ones128_bf"])
    dma(sb["c_onesf"][:], dr["onesf"])
    dma(sb["c_cbias"][:], dr["cbias"])
    for j in range(4):
        dma(sb["c_ssqind"][:, j], dr[f"ssqind{j}"])
    dma(sb["gvec_s"][:], dr["gvec"])
    dma(sb["gnw_s"][:], dr["gnw"])
    dma(sb["convw"][:, 0:6], dr["conv_all"].rearrange("(s p) k -> p s k", p=128))

    for kc in range(8):
        for half in range(2):
            o0 = half * 640
            w = min(640, NCOL - o0)
            c.wait("sync", "V")
            lw = dma(sb["stg"][:, 0:w], dr["wproj"][128 * kc:128 * (kc + 1), o0:o0 + w])
            c.wait("vector", "LW", lw)
            c.emit("vector", lambda e, kc=kc, o0=o0, w=w: e.tensor_copy(
                sb["wproj_bf"][:, kc, o0:o0 + w], sb["stg"][:, 0:w]), "V")
    for h in range(HPC):
        for half in range(2):
            o0 = half * 512
            c.wait("sync", "V")
            lw = dma(sb["stg"][:, 0:512], dr["wo"][128 * h:128 * (h + 1), o0:o0 + 512])
            c.wait("vector", "LW", lw)
            c.emit("vector", lambda e, h=h, o0=o0: e.tensor_scalar(
                sb["wo_bf"][:, h, o0:o0 + 512], sb["stg"][:, 0:512], sb["gnw_s"][:], None, ALU.mult), "V")
    c.wait("scalar", "LW")
    t1 = c.emit("scalar", lambda e: e.activation(sb["gknegA"][:], sb["gvec_s"][:, 0:1], AF.Exp), "S")
    c.wait("vector", "S", t1)
    c.emit("vector", lambda e: e.tensor_scalar(sb["gknegA"][:], sb["gknegA"][:], -1.0, None, ALU.mult), "V")
    c.emit("vector", lambda e: e.memset(sb["praw"][:, :, 0:3], 0.0), "V")
    c.emit("vector", lambda e: e.memset(sb["s_st"][:], 0.0), "V")


def _load_x_group(c: Ctx, t0):
    sb, dr = c.sb, c.dr
    vtoks = []
    for kc in range(8):
        pp = kc % 2
        c.wait("sync", "V", vtoks[kc - 2] if kc >= 2 else c.xstg_last[pp])
        lx = c.emit("sync", lambda e, kc=kc, t0=t0, pp=pp: e.dma_start(
            out=sb["xstg"][:, pp, :],
            in_=dr["x_fm"][128 * kc:128 * (kc + 1), t0:t0 + TPG]), f"LX{pp}", 16)
        c.wait("vector", f"LX{pp}", lx)
        vtoks.append(c.emit("vector", lambda e, kc=kc, pp=pp: e.tensor_copy(
            sb["x_bf"][:, kc, :], sb["xstg"][:, pp, :]), "V"))
    c.xstg_last[0] = vtoks[6]
    c.xstg_last[1] = vtoks[7]
    return vtoks[-1]


def _t1_stats(c: Ctx, q):
    sb, dr, ps = c.sb, c.dr, c.ps
    for g in range(GPQ):
        t0 = q * QT + g * TPG
        gq = gq_of(g)
        c.wait("vector", "V")
        c.wait("vector", "T")
        c.wait("vector", "S")
        vx = _load_x_group(c, t0)
        c.wait("scalar", "V", vx)
        c.wait("scalar", "T")
        c.wait("scalar", "OG")
        sx2 = c.emit("scalar", lambda e: e.activation(
            sb["xn"].rearrange("p k t -> p (k t)"),
            sb["x_bf"].rearrange("p k t -> p (k t)"), AF.Square), "S")
        c.wait("tensor", "V", vx)
        c.wait("tensor", "S", sx2)
        for kc in range(8):
            c.emit("tensor", lambda e, kc=kc: e.matmul(
                ps[7][0:1, 0:TPG], sb["c_ones128bf"][:], sb["x_bf"][:, kc, :],
                start=(kc == 0), stop=(kc == 7)), None)
        tsx = None
        for kc in range(8):
            tsx = c.emit("tensor", lambda e, kc=kc: e.matmul(
                ps[6][0:1, 0:TPG], sb["c_ones128bf"][:], sb["xn"][:, kc, :],
                start=(kc == 0), stop=(kc == 7)), "T" if kc == 7 else None)
        c.wait("vector", "T", tsx)
        v0 = c.emit("vector", lambda e: e.tensor_scalar(
            sb["r1a"][:], ps[7][0:1, 0:TPG], 1.0 / D, None, ALU.mult), "V")
        v1 = c.emit("vector", lambda e: e.tensor_scalar(
            sb["r1b"][:], ps[6][0:1, 0:TPG], 1.0 / D, None, ALU.mult), "V")
        c.wait("vector", "V", v1)
        v2 = c.emit("vector", lambda e: e.tensor_tensor(sb["r1a"][:], sb["r1a"][:], sb["r1a"][:], ALU.mult), "V")
        c.wait("vector", "V", v2)
        v3 = c.emit("vector", lambda e: e.tensor_tensor(sb["r1b"][:], sb["r1b"][:], sb["r1a"][:], ALU.subtract), "V")
        c.wait("scalar", "V", v3)
        s1 = c.emit("scalar", lambda e: e.activation(sb["r1a"][:], sb["r1b"][:], AF.Ln, bias=sb["c_cbias"][0:1, 0:1]), "S")
        c.wait("scalar", "S", s1)
        s2 = c.emit("scalar", lambda e, gq=gq: e.activation(
            sb["rstdq"][:, gq], sb["r1a"][:], AF.Exp, scale=-0.5), "S")
        c.wait("vector", "S", s2)
        v4 = c.emit("vector", lambda e: e.tensor_scalar(
            sb["r1a"][:], ps[7][0:1, 0:TPG], -1.0 / D, None, ALU.mult), "V")
        c.wait("vector", "V", v4)
        c.emit("vector", lambda e, gq=gq: e.tensor_tensor(
            sb["nmurq"][:, gq], sb["r1a"][:], sb["rstdq"][:, gq], ALU.mult), "V")


def _t2_proj(c: Ctx, q):
    sb, dr, ps = c.sb, c.dr, c.ps
    NSLICE = [(s * 128, 128) for s in range(9)] + [(1152, 3), (1155, 3)]
    prev_ssl = 0
    bank_free = {0: None, 1: None, 2: None}
    for g in range(GPQ):
        t0 = q * QT + g * TPG
        gq = gq_of(g)
        c.wait("vector", "V")
        c.wait("vector", "S")
        c.wait("vector", "T")
        c.wait("scalar", "T")
        c.wait("scalar", "G")
        vx = _load_x_group(c, t0)
        vrows = c.val("V")
        c.wait("tensor", "V", vrows)
        c.emit("tensor", lambda e, gq=gq: e.matmul(
            ps[3][:, 0:TPG], sb["c_ones1x128b"][:], sb["rstdq"][:, gq], start=True, stop=True), None)
        tbc = c.emit("tensor", lambda e, gq=gq: e.matmul(
            ps[4][:, 0:TPG], sb["c_ones1x128b"][:], sb["nmurq"][:, gq], start=True, stop=True), "T")
        c.wait("vector", "T", tbc)
        va = c.emit("vector", lambda e: e.tensor_copy(sb["bcA_bf"][:], ps[3][:, 0:TPG]), "V")
        vb = c.emit("vector", lambda e: e.tensor_copy(sb["bcB_bf"][:], ps[4][:, 0:TPG]), "V")
        c.wait("vector", "V", vb)
        c.wait("gpsimd", "T")
        vxn1 = None
        for kc in range(8):
            vxn1 = c.emit("vector", lambda e, kc=kc: e.tensor_tensor(
                sb["xn"][:, kc, :], sb["x_bf"][:, kc, :], sb["bcA_bf"][:], ALU.mult), "V")
        c.wait("gpsimd", "V", vxn1)
        gxn = None
        for kc in range(8):
            gxn = c.emit("gpsimd", lambda e, kc=kc: e.tensor_tensor(
                sb["xn"][:, kc, :], sb["xn"][:, kc, :], sb["bcB_bf"][:], ALU.add), "G")
        c.wait("tensor", "G", gxn)
        for s, (off, w) in enumerate(NSLICE):
            bank = ps[s % 3]
            bf = bank_free[s % 3]
            if bf is not None:
                c.wait("tensor", bf[0], bf[1])
            tsl = None
            for kc in range(8):
                tsl = c.emit("tensor", lambda e, kc=kc, off=off, w=w, bank=bank: e.matmul(
                    bank[0:w, 0:TPG], sb["wproj_bf"][:, kc, off:off + w], sb["xn"][:, kc, :],
                    start=(kc == 0), stop=(kc == 7)), "T" if kc == 7 else None)
            if s <= 5:
                c.wait("vector", "T", tsl)
                vev = c.emit("vector", lambda e, s=s, bank=bank: e.tensor_copy(
                    sb["praw"][:, s, 3:3 + TPG], bank[0:128, 0:TPG]), "V")
                bank_free[s % 3] = ("V", vev)
                c.wait("vector", "V", vev)
                c.wait("vector", "S", prev_ssl)
                vcv = c.emit("vector", lambda e, s=s: e.tensor_scalar(
                    sb["convt"][:], sb["praw"][:, s, 0:TPG], sb["convw"][:, s, 0:1], None, ALU.mult), "V")
                for jj in (1, 2, 3):
                    c.wait("vector", "V", vcv)
                    vcv = c.emit("vector", lambda e, s=s, jj=jj: e.scalar_tensor_tensor(
                        sb["convt"][:], sb["praw"][:, s, jj:jj + TPG], sb["convw"][:, s, jj:jj + 1],
                        sb["convt"][:], ALU.mult, ALU.add), "V")
                c.wait("scalar", "V", vcv)
                prev_ssl = c.emit("scalar", lambda e, s=s, gq=gq: e.activation(
                    sb["qkv"][:, s, gq], sb["convt"][:], AF.Silu), "S")
                c.wait("vector", "V", vcv)
                c.emit("vector", lambda e, s=s: e.tensor_copy(
                    sb["praw"][:, s, 0:3], sb["praw"][:, s, TPG:TPG + 3]), "V")
            elif s <= 8:
                h = s - 6
                c.wait("scalar", "T", tsl)
                ssil = c.emit("scalar", lambda e, h=h, gq=gq, bank=bank: e.activation(
                    sb["gg"][:, h, gq], bank[0:128, 0:TPG], AF.Silu), "S")
                bank_free[s % 3] = ("S", ssil)
            elif s == 9:
                c.wait("vector", "T", tsl)
                vz = c.emit("vector", lambda e, gq=gq, bank=bank: e.tensor_scalar(
                    sb["zgk"][:, gq], bank[0:HPC, 0:TPG], sb["gvec_s"][:, 1:2], None, ALU.add), "V")
                bank_free[s % 3] = ("V", vz)
            else:
                c.wait("vector", "T", tsl)
                vz = c.emit("vector", lambda e, gq=gq, bank=bank: e.tensor_scalar(
                    sb["zb"][:, gq], bank[0:HPC, 0:TPG], sb["gvec_s"][:, 2:3], None, ALU.add), "V")
                bank_free[s % 3] = ("V", vz)


def _t3_scan(c: Ctx, q):
    _gate_prep(c, q, 0)
    for g in range(GPQ):
        if STAGE >= 4:
            pend = None
            prev_ttr = None
            for h in range(HPC):
                t_a = _hg_head(c, q, g, h, prev_ttr)
                if pend is not None:
                    _hg_opost(c, q, g, h - 1, pend)
                r = _hg_body(c, q, g, h, t_a)
                pend, prev_ttr = (r if r is not None else (None, None))
            if pend is not None:
                _hg_opost(c, q, g, HPC - 1, pend)
        if g + 1 < GPQ:
            _gate_prep(c, q, g + 1)
        if STAGE < 4 or STAGE < 5 or HSTAGE < 9:
            continue
        _wo_partial(c, q, g)


def _gate_prep(c: Ctx, q, g):
    sb, dr, ps = c.sb, c.dr, c.ps
    gq = gq_of(g)

    c.wait("vector", "V")
    c.wait("scalar", "V")
    c.wait("scalar", "T")
    sp = None
    for j in range(3):
        sp = c.emit("scalar", lambda e, j=j, gq=gq: e.activation(
            sb["sqs"][:, j, :], sb["qkv"][:, j, gq], AF.Square), "S")
    c.wait("tensor", "S", sp)
    c.wait("tensor", "V")
    c.emit("tensor", lambda e: e.matmul(ps[7][0:3, 0:TPG], sb["c_ssqind"][:, 0], sb["sqs"][:, 0, :],
                                        start=True, stop=False), None)
    t_q = c.emit("tensor", lambda e: e.matmul(ps[7][0:3, 0:TPG], sb["c_ssqind"][:, 1], sb["sqs"][:, 1, :],
                                              start=False, stop=True), "T")
    c.emit("tensor", lambda e: e.matmul(ps[6][0:3, 0:TPG], sb["c_ssqind"][:, 2], sb["sqs"][:, 1, :],
                                        start=True, stop=False), None)
    t_k = c.emit("tensor", lambda e: e.matmul(ps[6][0:3, 0:TPG], sb["c_ssqind"][:, 3], sb["sqs"][:, 2, :],
                                              start=False, stop=True), "T")
    c.wait("scalar", "T", t_k)
    s_q = c.emit("scalar", lambda e: e.activation(sb["lnsq_q"][:], ps[7][0:3, 0:TPG], AF.Ln, bias=sb["c_cbias"][0:HPC, 1:2]), "S")
    s_k = c.emit("scalar", lambda e: e.activation(sb["lnsq_k"][:], ps[6][0:3, 0:TPG], AF.Ln, bias=sb["c_cbias"][0:HPC, 1:2]), "S")

    sa = c.emit("scalar", lambda e, gq=gq: e.activation(sb["r3a"][:], sb["zgk"][:, gq], AF.Abs), "S")
    c.wait("scalar", "S", sa)
    sb1 = c.emit("scalar", lambda e: e.activation(sb["r3b"][:], sb["r3a"][:], AF.Exp, scale=-1.0), "S")
    c.wait("scalar", "S", sb1)
    c.emit("scalar", lambda e: e.activation(sb["r3b"][:], sb["r3b"][:], AF.Ln, bias=sb["c_cbias"][0:HPC, 2:3]), "S")
    sc1 = c.emit("scalar", lambda e, gq=gq: e.activation(sb["r3c"][:], sb["zgk"][:, gq], AF.Relu), "S")
    c.wait("vector", "S", sc1)
    v1 = c.emit("vector", lambda e: e.tensor_tensor(sb["r3b"][:], sb["r3b"][:], sb["r3c"][:], ALU.add), "V")
    c.wait("vector", "V", v1)
    v2 = c.emit("vector", lambda e: e.tensor_scalar(sb["r3a"][:], sb["r3b"][:], sb["gknegA"][:], None, ALU.mult), "V")
    c.wait("vector", "V", v2)
    vgc = None
    for cc in range(CPG):
        vgc = c.emit("vector", lambda e, cc=cc: e.tensor_tensor_scan(
            sb["gcum"][:, cc * CS:(cc + 1) * CS], sb["c_onesrow3"][:],
            sb["r3a"][:, cc * CS:(cc + 1) * CS], 0.0, ALU.mult, ALU.add), "V")

    c.wait("scalar", "V", v2)
    sd = c.emit("scalar", lambda e, gq=gq: e.activation(sb["r3b"][:], sb["zb"][:, gq], AF.Abs), "S")
    c.wait("scalar", "S", sd)
    se = c.emit("scalar", lambda e: e.activation(sb["r3b"][:], sb["r3b"][:], AF.Exp, scale=-1.0), "S")
    c.wait("scalar", "S", se)
    c.emit("scalar", lambda e: e.activation(sb["r3b"][:], sb["r3b"][:], AF.Ln, bias=sb["c_cbias"][0:HPC, 2:3]), "S")
    sg = c.emit("scalar", lambda e, gq=gq: e.activation(sb["r3c"][:], sb["zb"][:, gq], AF.Relu, scale=-1.0), "S")
    c.wait("vector", "S", sg)
    vlb = c.emit("vector", lambda e: e.scalar_tensor_tensor(
        sb["lnbeta"][:], sb["r3b"][:], -1.0, sb["r3c"][:], ALU.mult, ALU.subtract), "V")
    c.wait("scalar", "V", vlb)
    sbeta = c.emit("scalar", lambda e: e.activation(sb["betar"][:], sb["lnbeta"][:], AF.Exp), "S")

    c.wait("vector", "S", s_k)
    c.wait("vector", "V", max(vgc, vlb))
    vc1 = c.emit("vector", lambda e: e.scalar_tensor_tensor(
        sb["colM"][:], sb["lnsq_k"][:], -0.5, sb["gcum"][:], ALU.mult, ALU.subtract), "V")
    vc2 = c.emit("vector", lambda e: e.scalar_tensor_tensor(
        sb["r3b"][:], sb["lnsq_k"][:], -0.5, sb["gcum"][:], ALU.mult, ALU.add), "V")
    c.wait("vector", "V", max(vc2, vlb))
    vrm = c.emit("vector", lambda e: e.tensor_tensor(sb["rowM"][:], sb["r3b"][:], sb["lnbeta"][:], ALU.add), "V")
    c.wait("vector", "S", s_q)
    vc3 = c.emit("vector", lambda e: e.scalar_tensor_tensor(
        sb["r3c"][:], sb["lnsq_q"][:], -0.5, sb["gcum"][:], ALU.mult, ALU.add), "V")
    c.wait("vector", "V", vc3)
    vra = c.emit("vector", lambda e: e.tensor_scalar(sb["rowA"][:], sb["r3c"][:], LNS, None, ALU.add), "V")
    c.wait("vector", "V", vc1)
    vkpr = None
    for cc in range(CPG):
        vkpr = c.emit("vector", lambda e, cc=cc: e.tensor_scalar(
            sb["kprow"][:, cc * CS:(cc + 1) * CS], sb["colM"][:, cc * CS:(cc + 1) * CS],
            sb["gcum"][:, cc * CS + CS - 1:cc * CS + CS], None, ALU.add), "V")
    c.wait("scalar", "V", max(vra, vkpr, vrm))
    slq = c.emit("scalar", lambda e: e.activation(sb["lqrow"][:], sb["rowA"][:], AF.Exp), "S")
    skw = c.emit("scalar", lambda e: e.activation(sb["kwrow"][:], sb["rowM"][:], AF.Exp), "S")
    skp = c.emit("scalar", lambda e: e.activation(sb["kprow"][:], sb["kprow"][:], AF.Exp), "S")
    vkw = c.val("V")

    c.wait("sync", "T")
    c.wait("sync", "V", max(vc1, vrm, vra))
    c.wait("sync", "S", slq)
    for a, b in (("colM", "sp_colM"), ("rowM", "sp_rowM"), ("rowA", "sp_rowA"), ("lqrow", "sp_lq")):
        c.emit("sync", lambda e, a=a, b=b: e.dma_start(
            out=c.sb[b][0:HPC * 32:32, :], in_=c.sb[a][:]), "SPR", 16)

    c.wait("sync", "V", vgc)
    dlc = None
    for hh in range(HPC):
        dlc = c.emit("sync", lambda e, hh=hh: e.dma_start(
            out=sb["lcrow"][0:1, CPG * hh:CPG * (hh + 1)],
            in_=sb["gcum"][hh:hh + 1, CS - 1::CS],
            allow_slow_non_contiguous=True), "DMV", 16)
    c.wait("scalar", "DMV", dlc)
    slc = c.emit("scalar", lambda e: e.activation(sb["lcrow"][:], sb["lcrow"][:], AF.Exp), "S")
    c.wait("tensor", "S", slc)
    c.wait("tensor", "V")
    tlc = c.emit("tensor", lambda e: e.matmul(
        ps[7][0:64, 0:HPC * CPG], sb["c_ones1x128"][:, 0:64], sb["lcrow"][:],
        start=True, stop=True), "T")
    c.wait("vector", "T", tlc)
    c.emit("vector", lambda e: e.tensor_copy(sb["lcbc"][:], ps[7][0:64, 0:HPC * CPG]), "V")

    c.wait("vector", "S", max(skp, sbeta))
    c.wait("vector", "V", vkw)
    vrb = None
    for si, src in enumerate(("betar", "kwrow", "kprow")):
        vrb = c.emit("vector", lambda e, si=si, src=src: e.tensor_copy(
            sb["rows_bf"][:, si, :], sb[src][:]), "V")
    c.wait("tensor", "V", vrb)
    tcs = None
    for cc in range(CPG):
        for si in range(3):
            tcs = c.emit("tensor", lambda e, cc=cc, si=si: e.transpose(
                ps[6][:, 2 * (3 * cc + si):2 * (3 * cc + si) + 2].bitcast(BF16)[:, 0:HPC],
                sb["rows_bf"][:, si, cc * CS:(cc + 1) * CS], sb["c_ident"][0:HPC, 0:HPC]),
                "T" if (cc == CPG - 1 and si == 2) else None)
    c.wait("vector", "T", tcs)
    c.emit("vector", lambda e: e.tensor_copy(
        sb["colsh"].rearrange("p cc s hh -> p (cc s) hh"),
        ps[6][:, 0:2 * 3 * CPG].bitcast(BF16).rearrange(
            "p (g four) -> p g four", four=4)[:, :, 0:HPC]), "V")


def _hg_head(c: Ctx, q, g, h, prev_ttr):
    sb, ps = c.sb, c.ps
    gq = gq_of(g)
    sq, bq = KQ_SLICE["q"][h]
    sk, bk = KQ_SLICE["k"][h]

    def kfm(cc):
        return sb["khh"][:, cc * CS:(cc + 1) * CS]

    def qfm(cc):
        return sb["qhh"][:, cc * CS:(cc + 1) * CS]

    sgate = c.val("S")

    c.wait("vector", "S", sgate)
    if prev_ttr is None:
        c.wait("vector", "T")
    else:
        c.wait("vector", "T", prev_ttr)
    c.emit("vector", lambda e: e.tensor_copy(
        sb["khh"][:], sb["qkv"][bk:bk + 64, sk, gq]), "V")
    c.emit("vector", lambda e: e.tensor_copy(
        sb["qhh"][:], sb["qkv"][bq:bq + 64, sq, gq]), "V")
    vcp = c.emit("vector", lambda e: e.tensor_copy(
        sb["vhh"][:], sb["qkv"][:, 3 + h, gq]), "V")
    vgate = c.val("V")

    c.wait("tensor", "V", vgate)
    c.wait("tensor", "S", sgate)
    t_a = None
    for cc in range(CPG):
        c.emit("tensor", lambda e, cc=cc: e.matmul(
            ps[0][:, cc * CS:(cc + 1) * CS], kfm(cc), kfm(cc), start=True, stop=True), None)
        t_a = c.emit("tensor", lambda e, cc=cc: e.matmul(
            ps[1][:, cc * CS:(cc + 1) * CS], kfm(cc), qfm(cc), start=True, stop=True),
            "T" if cc == CPG - 1 else None)
    return t_a


def _hg_body(c: Ctx, q, g, h, t_a):
    sb, dr, ps = c.sb, c.dr, c.ps
    gq = gq_of(g)

    def kfm(cc):
        return sb["khh"][:, cc * CS:(cc + 1) * CS]

    def qfm(cc):
        return sb["qhh"][:, cc * CS:(cc + 1) * CS]

    def vfm(cc):
        return sb["vhh"][:, cc * CS:(cc + 1) * CS]

    spr = c.val("SPR")
    c.wait("tensor", "SPR", spr)

    def arg_form(colsp, rowsp, mask_lhs, mask_rhs, expname, bank):
        for cc in range(CPG):
            c.emit("tensor", lambda e, cc=cc, colsp=colsp, bank=bank: e.matmul(
                bank[:, cc * CS:(cc + 1) * CS],
                sb[colsp][32 * h:32 * h + 1, cc * CS:(cc + 1) * CS],
                sb["c_onesf"][32 * h:32 * h + 1, 0:128], start=(cc == 0), stop=False), None)
        c.emit("tensor", lambda e, rowsp=rowsp, bank=bank: e.matmul(
            bank[:, 0:TPG], sb["c_onesf"][32 * h:32 * h + 1, 0:128],
            sb[rowsp][32 * h:32 * h + 1, :], start=False, stop=False), None)
        tlast = None
        for cc in range(CPG):
            tlast = c.emit("tensor", lambda e, cc=cc, mask_lhs=mask_lhs, mask_rhs=mask_rhs, bank=bank: e.matmul(
                bank[:, cc * CS:(cc + 1) * CS], sb[mask_lhs][:], sb[mask_rhs][:],
                start=False, stop=(cc == CPG - 1)), "T" if cc == CPG - 1 else None)
        c.wait("scalar", "T", tlast)
        return c.emit("scalar", lambda e, expname=expname, bank=bank: e.activation(
            sb[expname][:], bank[:, 0:TPG], AF.Exp), "S")

    if HSTAGE < 1:
        return
    s1 = arg_form("sp_colM", "sp_rowM", "c_stepi", "c_negi", "eMT", ps[2])

    ttr = None
    for cc in range(CPG):
        c.emit("tensor", lambda e, cc=cc: e.transpose(
            ps[5][:, cc * 64:(cc + 1) * 64].bitcast(BF16), vfm(cc), sb["c_ident"][:]), None)
        ttr = c.emit("tensor", lambda e, cc=cc: e.transpose(
            ps[5][:, 256 + cc * 32:256 + (cc + 1) * 32].bitcast(BF16), kfm(cc),
            sb["c_ident"][0:64, 0:64]), "T" if cc == CPG - 1 else None)

    s2 = arg_form("sp_rowM", "sp_colM", "c_negi", "c_stepi", "eM", ps[3])

    tlq = c.emit("tensor", lambda e: e.matmul(
        ps[4][0:64, 0:TPG], sb["c_onesf"][32 * h:32 * h + 1, 0:64],
        sb["sp_lq"][32 * h:32 * h + 1, :], start=True, stop=True), "T")

    c.wait("tensor", "S", s1)
    s3 = arg_form("sp_colM", "sp_rowA", "c_steps", "c_negi", "eA", ps[2])

    c.wait("vector", "S", s1)
    c.wait("vector", "T", t_a)
    v_mT = c.emit("vector", lambda e: e.tensor_tensor(sb["mT"][:], sb["eMT"][:], ps[0][:, 0:TPG], ALU.mult), "V")
    c.wait("vector", "S", s2)
    v_mN = c.emit("vector", lambda e: e.tensor_tensor(sb["mN"][:], sb["eM"][:], ps[0][:, 0:TPG], ALU.mult), "V")
    c.wait("vector", "S", s3)
    v_att = c.emit("vector", lambda e: e.tensor_tensor(sb["attT"][:], sb["eA"][:], ps[1][:, 0:TPG], ALU.mult), "V")

    if HSTAGE < 2:
        return
    c.wait("vector", "T", ttr)
    vkp = None
    for cc in range(CPG):
        vkp = c.emit("vector", lambda e, cc=cc: e.tensor_scalar(
            sb["kptm"][:, cc, :], ps[5][:, 256 + cc * 32:256 + (cc + 1) * 32].bitcast(BF16),
            sb["colsh"][:, cc, 2, h:h + 1], None, ALU.mult), "V")
    c.wait("vector", "T", tlq)
    vqs = None
    for cc in range(CPG):
        vqs = c.emit("vector", lambda e, cc=cc: e.tensor_tensor(
            sb["qst"][:, cc * CS:(cc + 1) * CS], qfm(cc), ps[4][0:64, cc * CS:(cc + 1) * CS],
            ALU.mult), "V")

    if HSTAGE < 4:
        return
    vrhs = None
    for cc in range(CPG):
        c.emit("vector", lambda e, cc=cc: e.tensor_scalar(
            sb["wcur"][:, cc, 0:128], ps[5][:, cc * 64:(cc + 1) * 64].bitcast(BF16),
            sb["colsh"][:, cc, 0, h:h + 1], None, ALU.mult), "V")
        vrhs = c.emit("vector", lambda e, cc=cc: e.tensor_scalar(
            sb["wcur"][:, cc, 128:192], ps[5][:, 256 + cc * 32:256 + (cc + 1) * 32].bitcast(BF16),
            sb["colsh"][:, cc, 1, h:h + 1], None, ALU.mult), "V")

    def srcT(ji):
        return sb["mT"][:] if ji < 0 else sb["powT"][:, ji, :]

    def srcN(ji):
        return sb["mN"][:] if ji < 0 else sb["powN"][:, ji, :]

    prevNtok, prevTtok = v_mN, v_mT
    prevNeng = prevTeng = "V"
    powT_tok = {}
    last_pow_s = None
    vw = vrhs
    for si, j in enumerate([1] + POWJ):
        c.wait("tensor", "V", vw)
        if si > 0:
            c.wait("tensor", "S", powT_tok[si - 1])
        tj = None
        for cc in range(CPG):
            lhs = (sb["mT"][:, cc * CS:(cc + 1) * CS] if j == 1
                   else sb["powT"][:, si - 1, cc * CS:(cc + 1) * CS])
            bank = ps[6] if cc < 2 else ps[2]
            tj = c.emit("tensor", lambda e, lhs=lhs, cc=cc, bank=bank: e.matmul(
                bank[:, (cc % 2) * 192:(cc % 2) * 192 + 192], lhs,
                sb["wcur"][:, cc, :], start=True, stop=True),
                "T" if cc == CPG - 1 else None)
        tn = tt = None
        if si < 6:
            j2 = POWJ[si]
            if si == 0:
                c.wait("tensor", "V", max(prevNtok, prevTtok, vqs))
            else:
                c.wait("tensor", "S", max(prevNtok, prevTtok))
            if j2 != 64:
                for cc in range(CPG):
                    tn = c.emit("tensor", lambda e, si=si, cc=cc: e.matmul(
                        ps[3][:, cc * CS:(cc + 1) * CS],
                        srcT(si - 1)[:, cc * CS:(cc + 1) * CS],
                        srcN(si - 1)[:, cc * CS:(cc + 1) * CS],
                        start=True, stop=True), "T" if cc == CPG - 1 else None)
            for cc in range(CPG):
                tt = c.emit("tensor", lambda e, si=si, cc=cc: e.matmul(
                    ps[4][:, cc * CS:(cc + 1) * CS],
                    srcN(si - 1)[:, cc * CS:(cc + 1) * CS],
                    srcT(si - 1)[:, cc * CS:(cc + 1) * CS],
                    start=True, stop=True), "T" if cc == CPG - 1 else None)
        c.wait("vector", "T", tj)
        op = ALU.subtract if j == 1 else ALU.add
        c.emit("vector", lambda e, op=op: e.tensor_tensor(
            sb["wcur"].rearrange("p cc w -> p (cc w)")[:, 0:384],
            sb["wcur"].rearrange("p cc w -> p (cc w)")[:, 0:384],
            ps[6][:, 0:384], op), "V")
        vw = c.emit("vector", lambda e, op=op: e.tensor_tensor(
            sb["wcur"].rearrange("p cc w -> p (cc w)")[:, 384:768],
            sb["wcur"].rearrange("p cc w -> p (cc w)")[:, 384:768],
            ps[2][:, 0:384], op), "V")
        if si < 6:
            if POWJ[si] != 64:
                c.wait("scalar", "T", tn)
                newN = c.emit("scalar", lambda e, si=si: e.activation(
                    sb["powN"][:, si, :], ps[3][:, 0:TPG], AF.Copy), "S")
            else:
                newN = prevNtok
            c.wait("scalar", "T", tt)
            newT = c.emit("scalar", lambda e, si=si: e.activation(
                sb["powT"][:, si, :], ps[4][:, 0:TPG], AF.Copy), "S")
            prevNtok, prevTtok = newN, newT
            powT_tok[si] = newT
            last_pow_s = newT
    c.wait("vector", "V", vw)
    vwb = c.emit("vector", lambda e: e.tensor_copy(
        sb["w_bf"].rearrange("p cc w -> p (cc w)"),
        sb["wcur"].rearrange("p cc w -> p (cc w)")), "V")

    c.wait("tensor", "V", vwb)
    takw = None
    for cc in range(CPG):
        takw = c.emit("tensor", lambda e, cc=cc: e.matmul(
            ps[2][0:64, cc * CS:(cc + 1) * CS], sb["w_bf"][:, cc, 128:192],
            sb["attT"][:, cc * CS:(cc + 1) * CS], start=True, stop=True),
            "T" if cc == CPG - 1 else None)
    c.wait("vector", "T", takw)
    vqt = c.emit("vector", lambda e: e.scalar_tensor_tensor(
        sb["qst"][:], ps[2][0:64, 0:TPG], -1.0, sb["qst"][:], ALU.mult, ALU.add), "V")

    tpt = None
    for cc in range(CPG):
        tpt = c.emit("tensor", lambda e, cc=cc: e.matmul(
            ps[7][0:64, 256 + cc * 64:256 + (cc + 1) * 64], sb["w_bf"][:, cc, 128:192],
            sb["kptm"][:, cc, :], start=True, stop=True),
            "T" if cc == CPG - 1 else None)
    vlcid = None
    for cc in range(CPG):
        vlcid = c.emit("vector", lambda e, cc=cc: e.tensor_scalar(
            sb["lcid"][:, cc, :], sb["c_ident"][0:64, 0:64],
            sb["lcbc"][:, h * CPG + cc:h * CPG + cc + 1], None, ALU.mult), "V")
    c.wait("vector", "T", tpt)
    c.wait("vector", "V", vlcid)
    vpt = None
    for cc in range(CPG):
        vpt = c.emit("vector", lambda e, cc=cc: e.scalar_tensor_tensor(
            sb["pt_bf"][:, cc, :], ps[7][0:64, 256 + cc * 64:256 + (cc + 1) * 64],
            -1.0, sb["lcid"][:, cc, :], ALU.mult, ALU.add), "V")

    if last_pow_s is not None:
        c.wait("tensor", "S", last_pow_s)
    o_last_t = None
    for cc in range(CPG):
        vprev = c.val("V")
        c.wait("tensor", "V", max(vprev, vqt, vpt, v_att))
        c.emit("tensor", lambda e, cc=cc: e.matmul(
            ps[6][:, cc * CS:(cc + 1) * CS], sb["s_st"][:, h, :],
            sb["qst"][:, cc * CS:(cc + 1) * CS], start=True, stop=False), None)
        o_last_t = c.emit("tensor", lambda e, cc=cc: e.matmul(
            ps[6][:, cc * CS:(cc + 1) * CS], sb["w_bf"][:, cc, 0:128],
            sb["attT"][:, cc * CS:(cc + 1) * CS], start=False, stop=True), "T")
        c.emit("tensor", lambda e, cc=cc: e.matmul(
            ps[3][0:64, 0:CS], sb["kptm"][:, cc, :], sb["w_bf"][:, cc, 0:128],
            start=True, stop=False), None)
        tsu = c.emit("tensor", lambda e, cc=cc: e.matmul(
            ps[3][0:64, 0:CS], sb["pt_bf"][:, cc, :], sb["s_st"][:, h, :],
            start=False, stop=True), "T")
        c.wait("vector", "T", tsu)
        c.emit("vector", lambda e: e.tensor_copy(sb["s_st"][:, h, :], ps[3][0:64, 0:CS]), "V")

    return o_last_t, ttr


def _hg_opost(c: Ctx, q, g, h, o_last_t):
    sb, ps = c.sb, c.ps
    gq = gq_of(g)
    c.wait("scalar", "T", o_last_t)
    c.wait("scalar", "V")
    if HSTAGE >= 51:
        so2 = c.emit("scalar", lambda e: e.activation(sb["o2"][:], ps[6][:, 0:TPG], AF.Square), "S")
    else:
        so2 = c.emit("scalar", lambda e: e.activation(sb["o2"][:], sb["o_bf"][:], AF.Square), "S")
    c.wait("vector", "T", o_last_t)
    c.wait("vector", "G")
    if HSTAGE >= 52:
        c.wait("scalar", "T", o_last_t)
        c.wait("scalar", "G")
        vob = c.emit("scalar", lambda e: e.activation(sb["o_bf"][:], ps[6][:, 0:TPG], AF.Copy), "S")
        c.wait("vector", "S", vob)
        vob = c.val("V")
    else:
        vob = c.emit("vector", lambda e: e.tensor_copy(sb["o_bf"][:], sb["o2"][:]), "V")
    if HSTAGE < 6:
        return
    c.wait("tensor", "S", so2)
    c.wait("tensor", "V")
    tos = c.emit("tensor", lambda e: e.matmul(
        ps[7][0:1, 0:TPG], sb["c_ones128bf"][:], sb["o2"][:], start=True, stop=True), "T")
    c.wait("scalar", "T", tos)
    sr1 = c.emit("scalar", lambda e: e.activation(
        sb["r1a"][:], ps[7][0:1, 0:TPG], AF.Ln, bias=sb["c_cbias"][0:1, 0:1], scale=1.0 / DVH), "S")
    c.wait("scalar", "S", sr1)
    sr2 = c.emit("scalar", lambda e: e.activation(sb["r1a"][:], sb["r1a"][:], AF.Exp, scale=-0.5), "S")
    if HSTAGE < 7:
        return
    c.wait("tensor", "S", sr2)
    trs = c.emit("tensor", lambda e: e.matmul(
        ps[7][:, 0:TPG], sb["c_ones1x128"][:], sb["r1a"][:], start=True, stop=True), "T")
    c.wait("vector", "T", trs)
    c.wait("vector", "V", vob)
    vg1 = c.emit("vector", lambda e: e.tensor_tensor(sb["o_bf"][:], sb["o_bf"][:], ps[7][:, 0:TPG], ALU.mult), "V")
    if HSTAGE < 8:
        return
    c.wait("gpsimd", "V", vg1)
    c.emit("gpsimd", lambda e, h=h, gq=gq: e.tensor_tensor(
        sb["ogh"][:, h, :], sb["o_bf"][:], sb["gg"][:, h, gq], ALU.mult), "G")


def _wo_partial(c: Ctx, q, g):
    sb, dr, ps = c.sb, c.dr, c.ps
    gog = c.val("G")
    c.wait("tensor", "G", gog)
    c.wait("tensor", "V")
    c.wait("vector", "OG")
    c.wait("scalar", "OG")
    bank_ev = {0: None, 1: None}
    sevac = None
    for mc in range(8):
        bank = ps[mc % 2]
        if bank_ev[mc % 2] is not None:
            c.wait("tensor", bank_ev[mc % 2][0], bank_ev[mc % 2][1])
        tm = None
        for h in range(HPC):
            tm = c.emit("tensor", lambda e, mc=mc, h=h, bank=bank: e.matmul(
                bank[:, 0:TPG], sb["wo_bf"][:, h, mc * 128:(mc + 1) * 128], sb["ogh"][:, h, :],
                start=(h == 0), stop=(h == HPC - 1)), "T" if h == HPC - 1 else None)
        if mc % 2 == 0:
            c.wait("vector", "T", tm)
            bank_ev[0] = ("V", c.emit("vector", lambda e, mc=mc, bank=bank: e.tensor_copy(
                sb["sgrp"][:, mc, :], bank[:, 0:TPG]), "V"))
        else:
            c.wait("scalar", "T", tm)
            sevac = c.emit("scalar", lambda e, mc=mc, bank=bank: e.activation(
                sb["sgrp"][:, mc, :], bank[:, 0:TPG], AF.Copy), "S")
            bank_ev[1] = ("S", sevac)
    vs = c.val("V")
    c.wait("gpsimd", "V", vs)
    c.wait("gpsimd", "S", sevac)
    npieces = max(1, TPG // ST)
    psz = min(ST, TPG)
    for pp in range(npieces):
        stripe = (g * TPG + pp * psz) // ST
        soff = (g * TPG + pp * psz) % ST
        c.emit("gpsimd", lambda e, pp=pp, stripe=stripe, soff=soff, psz=psz, q=q: e.dma_start(
            out=dr["sbounce"][q][1024 * stripe:1024 * (stripe + 1), soff:soff + psz]
                .rearrange("(m p) t -> p m t", p=128),
            in_=sb["sgrp"][:, :, pp * psz:(pp + 1) * psz]), "OG", 16)



def emit_phase2(c: Ctx):
    sb, dr, ps = c.sb, c.dr, c.ps
    if STAGE < 6:
        c.emit("vector", lambda e: e.memset(sb["outt"][:], 0.0), "V")
        for tt in range(NQ):
            c.wait("gpsimd", "V")
            c.emit("gpsimd", lambda e, tt=tt: e.dma_start(
                out=dr["out_fm"][:, tt, :].rearrange("(m p) t -> p m t", p=128),
                in_=sb["outt"][:]), "ST", 16)
        c.wait("gpsimd", "ST")
        if STAGE >= 5:
            c.wait("gpsimd", "CC")
        return

    c.emit("sync", lambda e: e.dma_start(
        out=sb["bias_f"][:], in_=dr["biases"].rearrange("(m p) l -> p m l", p=128)), "LQ", 16)
    c.emit("sync", lambda e: e.dma_start(
        out=sb["lng_f"][:], in_=dr["ln_g"].rearrange("(m p) l -> p m l", p=128)), "LQ", 16)
    c.emit("sync", lambda e: e.dma_start(
        out=sb["lnb_f"][:], in_=dr["ln_b"].rearrange("(m p) l -> p m l", p=128)), "LQ", 16)

    if STAGE >= 7:
        _dequant_layer(c, 0, 0)
        _dequant_layer(c, 1, 1)

    for tt in range(NQ):
        c.wait("sync", "V")
        lxq = c.emit("sync", lambda e, tt=tt: e.dma_start(
            out=sb["xq_s"][:], in_=dr["x_q"][:, tt, :].rearrange("(k p) t -> p k t", p=128)), "LQ", 16)
        c.wait("vector", "LQ", lxq)
        c.wait("vector", "T")
        vx = None
        for kc in range(8):
            vx = c.emit("vector", lambda e, kc=kc: e.tensor_copy(
                sb["ybuf"][:, kc, :], sb["xq_s"][:, kc, :]), "V")
        _p2_ln(c, "ybuf", ("hbuf", tt), None, vx, gelu=False)
        if tt == 0:
            c.tap(0, c.sb["hbuf"][:, :, 0:ST])

    if STAGE >= 7:
        for blk in range(3):
            _ffn_block(c, blk)
    c.tap(5, c.sb["hbuf"][:, :, 0:ST])

    for tt in range(NQ):
        c.wait("sync", "CC", tt + 1)
        c.wait("sync", "V")
        lxq = c.emit("sync", lambda e, tt=tt: e.dma_start(
            out=sb["xq_s"][:], in_=dr["x_q"][:, tt, :].rearrange("(k p) t -> p k t", p=128)), "LQ", 16)
        c.wait("sync", "G")
        lgq = c.emit("sync", lambda e, tt=tt: e.dma_start(
            out=sb["s_bf"][:], in_=dr["sshard"][tt].rearrange("(m p) t -> p m t", p=128)), "LG", 16)
        c.wait("scalar", "V")
        sgl = None
        for mc in range(8):
            sgl = c.emit("scalar", lambda e, mc=mc, tt=tt: e.activation(
                sb["ygel"][:, mc, :], sb["hbuf"][:, mc, tt * ST:(tt + 1) * ST], AF.Silu), "S")
        c.wait("vector", "LG", lgq)
        c.wait("vector", "LQ", lxq)
        c.wait("vector", "S", sgl)
        c.wait("vector", "ST")
        vlast = None
        for mc in range(8):
            v1 = c.emit("vector", lambda e, mc=mc: e.tensor_tensor(
                sb["outt"][:, mc, :], sb["s_bf"][:, mc, :], sb["ygel"][:, mc, :], ALU.mult), "V")
            c.wait("vector", "V", v1)
            vlast = c.emit("vector", lambda e, mc=mc: e.tensor_tensor(
                sb["outt"][:, mc, :], sb["outt"][:, mc, :], sb["xq_s"][:, mc, :], ALU.add), "V")
        c.wait("gpsimd", "V", vlast)
        c.emit("gpsimd", lambda e, tt=tt: e.dma_start(
            out=dr["out_fm"][:, tt, :].rearrange("(m p) t -> p m t", p=128),
            in_=sb["outt"][:]), "ST", 16)
    c.wait("gpsimd", "ST")
    c.wait("gpsimd", "CC")


def _p2_ln(c: Ctx, srcname, dst, blk, vsrc, gelu):
    sb, ps = c.sb, c.ps
    dname, tt = dst
    dsl = slice(tt * ST, (tt + 1) * ST) if dname == "hbuf" else slice(0, ST)
    n = ST
    c.wait("scalar", "V", vsrc)
    c.wait("scalar", "T")
    c.wait("scalar", "G")
    sx2 = c.emit("scalar", lambda e: e.activation(
        sb["f2s"].rearrange("p k t -> p (k t)"),
        sb[srcname].rearrange("p k t -> p (k t)"), AF.Square), "S")
    c.wait("tensor", "V", vsrc)
    c.wait("tensor", "S", sx2)
    for kc in range(8):
        c.emit("tensor", lambda e, kc=kc: e.matmul(
            ps[7][0:1, 0:n], sb["c_ones128bf"][:], sb[srcname][:, kc, :],
            start=(kc == 0), stop=(kc == 7)), None)
    t2 = None
    for kc in range(8):
        t2 = c.emit("tensor", lambda e, kc=kc: e.matmul(
            ps[6][0:1, 0:n], sb["c_ones128bf"][:], sb["f2s"][:, kc, :],
            start=(kc == 0), stop=(kc == 7)), "T" if kc == 7 else None)
    c.wait("vector", "T", t2)
    v0 = c.emit("vector", lambda e: e.tensor_scalar(sb["p2ra"][:], ps[7][0:1, 0:n], 1.0 / D, None, ALU.mult), "V")
    v1 = c.emit("vector", lambda e: e.tensor_scalar(sb["p2rb"][:], ps[6][0:1, 0:n], 1.0 / D, None, ALU.mult), "V")
    c.wait("vector", "V", v1)
    v2 = c.emit("vector", lambda e: e.tensor_tensor(sb["p2ra"][:], sb["p2ra"][:], sb["p2ra"][:], ALU.mult), "V")
    c.wait("vector", "V", v2)
    v3 = c.emit("vector", lambda e: e.tensor_tensor(sb["p2rb"][:], sb["p2rb"][:], sb["p2ra"][:], ALU.subtract), "V")
    c.wait("scalar", "V", v3)
    s1 = c.emit("scalar", lambda e: e.activation(sb["p2ra"][:], sb["p2rb"][:], AF.Ln, bias=sb["c_cbias"][0:1, 0:1]), "S")
    c.wait("scalar", "S", s1)
    s2 = c.emit("scalar", lambda e: e.activation(sb["p2rb"][:], sb["p2ra"][:], AF.Exp, scale=-0.5), "S")
    c.wait("vector", "S", s2)
    v4 = c.emit("vector", lambda e: e.tensor_scalar(sb["p2ra"][:], ps[7][0:1, 0:n], -1.0 / D, None, ALU.mult), "V")
    c.wait("vector", "V", v4)
    v5 = c.emit("vector", lambda e: e.tensor_tensor(sb["p2ra"][:], sb["p2ra"][:], sb["p2rb"][:], ALU.mult), "V")
    c.wait("tensor", "V", v5)
    c.emit("tensor", lambda e: e.matmul(ps[7][:, 0:n], sb["c_ones1x128"][:], sb["p2rb"][:],
                                        start=True, stop=True), None)
    t3 = c.emit("tensor", lambda e: e.matmul(ps[6][:, 0:n], sb["c_ones1x128"][:], sb["p2ra"][:],
                                             start=True, stop=True), "T")
    c.wait("vector", "T", t3)
    v6 = c.emit("vector", lambda e: e.tensor_copy(sb["bcr_bf"][:], ps[7][:, 0:n]), "V")
    v7 = c.emit("vector", lambda e: e.tensor_copy(sb["bcm_bf"][:], ps[6][:, 0:n]), "V")
    c.wait("vector", "V", v7)
    c.wait("vector", "G")
    for kc in range(8):
        vz = c.emit("vector", lambda e, kc=kc: e.tensor_tensor(
            sb["f2s"][:, kc, :], sb[srcname][:, kc, :], sb["bcr_bf"][:], ALU.mult), "V")
        c.wait("vector", "V", vz)
        vu = c.emit("vector", lambda e, kc=kc: e.tensor_tensor(
            sb["f2s"][:, kc, :], sb["f2s"][:, kc, :], sb["bcm_bf"][:], ALU.add), "V")
        if gelu:
            c.wait("scalar", "V", vu)
            c.emit("scalar", lambda e, kc=kc, blk=blk: e.activation(
                sb[dname][:, kc, dsl], sb["f2s"][:, kc, :], AF.Gelu,
                bias=sb["lnb_f"][:, kc, blk:blk + 1],
                scale=sb["lng_f"][:, kc, blk:blk + 1]), "S")
        else:
            c.wait("gpsimd", "V", vu)
            c.emit("gpsimd", lambda e, kc=kc, dsl=dsl: e.tensor_copy(
                sb[dname][:, kc, dsl], sb["f2s"][:, kc, :]), "G")


def _dequant_layer(c: Ctx, l, slot):
    sb, dr, ps = c.sb, c.dr, c.ps
    c.wait("sync", "G")
    c.wait("sync", "V")
    c.emit("sync", lambda e, l=l: e.dma_start(
        out=sb["q8s"][:], in_=dr["q8"][l].rearrange("(k p) o -> p k o", p=128)), "LF", 16)
    c.emit("sync", lambda e, l=l: e.dma_start(out=sb["la_f"][:], in_=dr["la"][l]), "LF", 16)
    lf1 = c.emit("sync", lambda e, l=l: e.dma_start(out=sb["lbT_f"][:], in_=dr["lbT"][l]), "LF", 16)
    c.wait("vector", "LF", lf1)
    c.wait("gpsimd", "LF", lf1)
    vla = c.emit("vector", lambda e: e.tensor_copy(sb["la_bf"][:], sb["la_f"][:]), "V")
    vlb = c.emit("vector", lambda e: e.tensor_copy(sb["lbT_bf"][:], sb["lbT_f"][:]), "V")
    prev_weff = {0: 0, 1: 0}
    prev_g2 = {0: 0, 1: 0}
    for kc in range(8):
        pp = kc % 2
        c.wait("sync", "G", prev_g2[pp])
        lsr = c.emit("sync", lambda e, l=l, kc=kc, pp=pp: e.dma_start(
            out=sb["sreps"][:, pp, :], in_=dr["srep"][l, 128 * kc:128 * (kc + 1), :]), f"LR{pp}", 16)
        c.wait("tensor", "V", max(vla, vlb, prev_weff[0], prev_weff[1]))
        tl = None
        for half in range(2):
            tl = c.emit("tensor", lambda e, kc=kc, half=half: e.matmul(
                ps[4 + half][:, 0:512], sb["la_bf"][:, kc * 128:(kc + 1) * 128],
                sb["lbT_bf"][:, half * 512:(half + 1) * 512], start=True, stop=True),
                "T" if half == 1 else None)
        c.wait("gpsimd", f"LR{pp}", lsr)
        c.wait("gpsimd", "V", prev_weff[pp])
        g1 = c.emit("gpsimd", lambda e, kc=kc, pp=pp: e.tensor_scalar(
            sb["w1t"][:, pp, :], sb["q8s"][:, kc, :], 1.0 / 7.5, -1.0, ALU.mult, ALU.add), "G")
        c.wait("gpsimd", "G", g1)
        g2 = c.emit("gpsimd", lambda e, pp=pp: e.tensor_tensor(
            sb["w1t"][:, pp, :], sb["w1t"][:, pp, :], sb["sreps"][:, pp, :], ALU.mult), "G")
        prev_g2[pp] = g2
        c.wait("vector", "G", g2)
        c.wait("vector", "T", tl)
        vv = None
        for half in range(2):
            vv = c.emit("vector", lambda e, kc=kc, slot=slot, half=half, pp=pp: e.tensor_tensor(
                sb["weff"][:, slot, kc, half * 512:(half + 1) * 512],
                sb["w1t"][:, pp, half * 512:(half + 1) * 512],
                ps[4 + half][:, 0:512], ALU.add), "V")
        prev_weff[pp] = vv
    return c.val("V")


def _qlora_mm(c: Ctx, slot, srcname, src_tt, l, dst="ybuf"):
    sb, ps = c.sb, c.ps
    ssl = slice(src_tt * ST, (src_tt + 1) * ST) if srcname == "hbuf" else slice(0, ST)
    vw = c.val("V")
    bank_ev = {0: 0, 1: 0}
    for mc in range(8):
        bank = ps[mc % 2]
        c.wait("tensor", "V", max(vw, bank_ev[mc % 2]))
        tm = None
        for kc in range(8):
            tm = c.emit("tensor", lambda e, mc=mc, kc=kc, slot=slot, ssl=ssl, bank=bank, srcname=srcname: e.matmul(
                bank[:, 0:ST], sb["weff"][:, slot, kc, mc * 128:(mc + 1) * 128],
                sb[srcname][:, kc, ssl], start=(kc == 0), stop=(kc == 7)),
                "T" if kc == 7 else None)
        c.wait("vector", "T", tm)
        bank_ev[mc % 2] = c.emit("vector", lambda e, mc=mc, l=l, bank=bank, dst=dst: e.tensor_scalar(
            sb[dst][:, mc, 0:ST], bank[:, 0:ST], sb["bias_f"][:, mc, l:l + 1], None, ALU.add), "V")
    return c.val("V")


def _ffn_block(c: Ctx, blk):
    sb = c.sb
    l0, l1 = 2 * blk, 2 * blk + 1
    if blk > 0:
        _dequant_layer(c, l0, 0)
        _dequant_layer(c, l1, 1)
    dsts = ["ybuf", "s_bf"]
    if blk == 0:
        c.wait("tensor", "G")
    vy = _qlora_mm(c, 0, "hbuf", 0, l0, dst=dsts[0])
    for tt in range(NQ):
        if blk == 0 and tt == 0:
            c.tap(1, c.sb["ybuf"][:])
        _p2_ln(c, dsts[tt % 2], ("ygel", tt), blk, vy, gelu=True)
        if blk == 0 and tt == 0:
            c.tap(2, c.sb["ygel"][:])
        if tt + 1 < NQ:
            vy = _qlora_mm(c, 0, "hbuf", tt + 1, l0, dst=dsts[(tt + 1) % 2])
        sgel = c.val("S")
        c.wait("vector", "S", sgel)
        c.wait("tensor", "S", sgel)
        vy2 = _qlora_mm(c, 1, "ygel", tt, l1, dst=dsts[tt % 2])
        if blk == 0 and tt == 0:
            c.tap(3, c.sb["ybuf"][:])
        c.wait("vector", "V", vy2)
        for kc in range(8):
            c.emit("vector", lambda e, kc=kc, tt=tt, d=dsts[tt % 2]: e.tensor_tensor(
                sb["hbuf"][:, kc, tt * ST:(tt + 1) * ST], sb[d][:, kc, 0:ST],
                sb["hbuf"][:, kc, tt * ST:(tt + 1) * ST], ALU.add), "V")
        if blk == 0 and tt == 0:
            c.tap(4, c.sb["hbuf"][:, :, 0:ST])



def kernel(**inputs):
    consts = _host_consts()
    nc = build_nc()
    in_maps = [_prep_core_inputs(core, inputs, consts) for core in range(8)]
    res = run_bass_kernel_spmd(nc, in_maps, core_ids=list(range(8)))
    x = np.asarray(inputs["L0_attn_hidden_states"], np.float32)
    out = np.array(x)
    for core in range(8):
        b, j = core // 4, core % 4
        o = np.asarray(res.results[core]["out_fm"], np.float32)
        for qq in range(NQ):
            out[b, qq * QT + j * ST: qq * QT + (j + 1) * ST] = o[:, qq].T
    return out
